# version 03
# baseline (speedup 1.0000x reference)
from contextlib import ExitStack

import numpy as np
import concourse.bass as bass
import concourse.mybir as mybir

F32 = mybir.dt.float32
BF16 = mybir.dt.bfloat16
AF = mybir.ActivationFunctionType
ALU = mybir.AluOpType
AX = mybir.AxisListType

COMPUTE = ("tensor", "vector", "scalar", "gpsimd")
DMAQ = ("sync_q", "gp_q", "sc_q")
NSLOT = 8
LN_EPS = 1e-5


class Sched:

    def __init__(self, nc, ctx):
        self.nc = nc
        names = list(COMPUTE)
        for q in DMAQ:
            names += [f"{q}{i}" for i in range(NSLOT)]
        self.sems = {n: ctx.enter_context(nc.semaphore("sem_" + n)) for n in names}
        self.count = {n: 0 for n in names}
        self.qidx = {q: 0 for q in DMAQ}
        self.qlast = {q: [None] * NSLOT for q in DMAQ}
        self.ops = []

    def _tok(self, stream, amount):
        self.count[stream] += amount
        return (stream, self.count[stream])

    def op(self, eng, fn, deps=()):
        tok = self._tok(eng, 1)
        self.ops.append((eng, eng, fn, tuple(d for d in deps if d), 1))
        return tok

    def dma(self, queue, out, in_, deps=(), **kw):
        q = {"sync": "sync_q", "gp": "gp_q", "sc": "sc_q"}[queue]
        body = {"sync": "sync", "gp": "gpsimd", "sc": "scalar"}[queue]
        slot = self.qidx[q] % NSLOT
        self.qidx[q] += 1
        stream = f"{q}{slot}"
        deps = list(d for d in deps if d)
        prev = self.qlast[q][slot]
        if prev is not None:
            deps.append(prev)
        tok = self._tok(stream, 16)
        self.qlast[q][slot] = tok

        def fn(e):
            return e.dma_start(out=out, in_=in_, **kw)

        self.ops.append((body, stream, fn, tuple(deps), 16))
        return tok

    def frontier(self):
        return [(s, v) for s, v in self.count.items() if v > 0]

    def qfrontier(self, q):
        return [t for t in self.qlast[q] if t is not None]

    def build(self, blk):
        bodies = {}
        for body, stream, fn, deps, inc in self.ops:
            bodies.setdefault(body, []).append((stream, fn, deps, inc))
        if "sync" not in bodies:
            bodies["sync"] = []
        final_counts = dict(self.count)

        def make(body_name, items):
            def emit(eng):
                hw = {}
                for stream, fn, deps, inc in items:
                    for (s, v) in deps:
                        if hw.get(s, 0) < v:
                            eng.wait_ge(self.sems[s], v)
                            hw[s] = v
                    fn(eng).then_inc(self.sems[stream], inc)
                if body_name == "sync":
                    for q in self.count:
                        if final_counts[q] > hw.get(q, 0):
                            eng.wait_ge(self.sems[q], final_counts[q])
            return emit

        for body_name, items in bodies.items():
            getattr(blk, body_name)(make(body_name, items))


class CFG:
    def __init__(self, n_exp=16, inter=1792, t_own=2048, t_prior=6144):
        self.D = 2048
        self.NKD = 16
        self.E = n_exp
        self.I = inter
        self.NKI = inter // 128
        self.T = t_own
        self.TP = t_prior
        self.NT = t_own // 128
        self.NTP = t_prior // 128
        self.TH = t_own // 2
        self.NTB = self.TH // 512
        assert t_own % 1024 == 0 and self.TH % 512 == 0 and inter % 128 == 0


def build_nc(cfg: CFG, debug=False):
    nc = bass.Bass("TRN2", target_bir_lowering=False)
    D, E, I, T, TP = cfg.D, cfg.E, cfg.I, cfg.T, cfg.TP
    NKD, NKI, NT, NTP, TH, NTB = cfg.NKD, cfg.NKI, cfg.NT, cfg.NTP, cfg.TH, cfg.NTB

    xin = nc.dram_tensor("xin", [TP + T, D], F32, kind="ExternalInput").ap()
    logA = nc.dram_tensor("logA", [128], F32, kind="ExternalInput").ap()
    Bm = nc.dram_tensor("Bm", [128, D], F32, kind="ExternalInput").ap()
    Cm = nc.dram_tensor("Cm", [D, 128], F32, kind="ExternalInput").ap()
    Dv = nc.dram_tensor("Dv", [D], F32, kind="ExternalInput").ap()
    lnw = nc.dram_tensor("lnw", [D], F32, kind="ExternalInput").ap()
    lnb = nc.dram_tensor("lnb", [D], F32, kind="ExternalInput").ap()
    rw = nc.dram_tensor("rw", [E, D], F32, kind="ExternalInput").ap()
    rb = nc.dram_tensor("rb", [E], F32, kind="ExternalInput").ap()
    gpt = nc.dram_tensor("gpt", [E, NKI, 128, NKD, 128], F32, kind="ExternalInput").ap()
    upt = nc.dram_tensor("upt", [E, NKI, 128, NKD, 128], F32, kind="ExternalInput").ap()
    dpt = nc.dram_tensor("dpt", [E, NKI, 128, D], F32, kind="ExternalInput").ap()
    ident_in = nc.dram_tensor("ident", [128, 128], F32, kind="ExternalInput").ap()
    sel_in = nc.dram_tensor("sel", [E, E * 128], F32, kind="ExternalInput").ap()
    y = nc.dram_tensor("y", [T, D], F32, kind="ExternalOutput").ap()
    wfmd = nc.dram_tensor("wfmd", [E, T], BF16, kind="Internal").ap()
    dbg = {}
    if debug:
        dbg["x1_t0"] = nc.dram_tensor("dbg_x1_t0", [128, D], F32, kind="ExternalOutput").ap()
        dbg["hs_t0"] = nc.dram_tensor("dbg_hs_t0", [128, 128], F32, kind="ExternalOutput").ap()
        dbg["z_t0"] = nc.dram_tensor("dbg_z_t0", [128, D], F32, kind="ExternalOutput").ap()
        dbg["x3_t0"] = nc.dram_tensor("dbg_x3_t0", [128, D], F32, kind="ExternalOutput").ap()
        dbg["lg_t0"] = nc.dram_tensor("dbg_lg_t0", [128, E], F32, kind="ExternalOutput").ap()
        dbg["wtm_t0"] = nc.dram_tensor("dbg_wtm_t0", [128, E], F32, kind="ExternalOutput").ap()
        dbg["acc_th0"] = nc.dram_tensor("dbg_acc_th0", [128, NKD * (T // 2)], F32, kind="ExternalOutput").ap()
        dbg["act_e0"] = nc.dram_tensor("dbg_act_e0", [128, NKI * (T // 2)], F32, kind="ExternalOutput").ap()
        dbg["wbc_e0"] = nc.dram_tensor("dbg_wbc_e0", [128, T // 2], F32, kind="ExternalOutput").ap()
        dbg["wfm"] = nc.dram_tensor("dbg_wfm", [E, T], F32, kind="ExternalOutput").ap()
        dbg["selb"] = nc.dram_tensor("dbg_selb", [E, 128], F32, kind="ExternalOutput").ap()
        dbg["pbc"] = nc.dram_tensor("dbg_pbc", [128, 512], F32, kind="ExternalOutput").ap()

    with ExitStack() as ctx:
        en = ctx.enter_context
        sch = Sched(nc, ctx)

        x3T = en(nc.sbuf_tensor("x3T", [128, NKD * T], BF16)).ap()
        gub = en(nc.sbuf_tensor("gub", [128, 4 * NKD * 128], BF16)).ap()
        idf = en(nc.sbuf_tensor("idf", [128, 128], F32)).ap()
        idb = en(nc.sbuf_tensor("idb", [128, 128], BF16)).ap()
        rlhs = en(nc.sbuf_tensor("rlhs", [128, NKD * E], BF16)).ap()
        rbias = en(nc.sbuf_tensor("rbias", [E, 1], F32)).ap()
        pt = [en(nc.psum_tensor("pt0", [128, 512], BF16)).ap(),
              en(nc.psum_tensor("pt1", [128, 512], BF16)).ap()]
        pt_war = {0: [], 1: []}

        CHW = NKD * 128

        def pe_transpose(src_ap, dst_ap, k, deps, fp32=False):
            P, F = src_ap.shape
            bank = k % 2
            idap = (idf if fp32 else idb)[0:P, 0:P]
            pb = pt[bank].bitcast(F32) if fp32 else pt[bank]
            pdst = pb[0:F, 0:P]
            t_t = sch.op("tensor", lambda e: e.transpose(out=pdst, in_=src_ap, identity=idap),
                         list(deps) + pt_war[bank])
            t_e = sch.op("vector", lambda e: e.tensor_copy(out=dst_ap, in_=pdst), [t_t])
            pt_war[bank] = [t_e]
            return t_e

        d_id = sch.dma("sync", idf, ident_in)
        t_idb = sch.op("vector", lambda e: e.tensor_copy(out=idb, in_=idf), [d_id])

        def gu_load(eidx, ic, buf, war):
            tg = sch.dma("gp", gub[:, buf * CHW:(buf + 1) * CHW],
                         gpt[eidx, ic].rearrange("p k i -> p (k i)"), deps=war)
            tu = sch.dma("gp", gub[:, (2 + buf) * CHW:(2 + buf + 1) * CHW],
                         upt[eidx, ic].rearrange("p k i -> p (k i)"), deps=war)
            return (tg, tu)

        with ExitStack() as actx:
            aen = actx.enter_context
            Blhs = aen(nc.sbuf_tensor("Blhs", [128, NKD * 128], BF16)).ap()
            CT = aen(nc.sbuf_tensor("CT", [128, D], BF16)).ap()
            dbc = aen(nc.sbuf_tensor("dbc", [128, D], BF16)).ap()
            wbcst = aen(nc.sbuf_tensor("wbcst", [128, D], BF16)).ap()
            bbcst = aen(nc.sbuf_tensor("bbcst", [128, D], BF16)).ap()
            abc = aen(nc.sbuf_tensor("abc", [128, 128], F32)).ap()
            adv = aen(nc.sbuf_tensor("adv", [128, 4], F32)).ap()
            epsv = aen(nc.sbuf_tensor("epsv", [128, 1], F32)).ap()
            ppu = aen(nc.psum_tensor("ppu", [128, 512], F32)).ap()
            ppy = aen(nc.psum_tensor("ppy", [128, 2048], F32)).ap()
            ptf = aen(nc.psum_tensor("ptf", [128, 128], F32)).ap()

            with ExitStack() as sctx:
                sen = sctx.enter_context
                rowb = sen(nc.sbuf_tensor("rowb", [1, D], F32)).ap()
                ones1 = sen(nc.sbuf_tensor("ones1", [1, 128], F32)).ap()
                stg = sen(nc.sbuf_tensor("stg", [128, 2048], F32)).ap()
                stgb = sen(nc.sbuf_tensor("stgb", [128, 2048], BF16)).ap()

                t_eps = sch.op("vector", lambda e: e.memset(epsv, LN_EPS))
                d_logA = sch.dma("sync", adv[:, 0:1], logA.rearrange("(p one) -> p one", one=1))
                t_rate = sch.op("scalar", lambda e: e.activation(out=adv[:, 1:2], in_=adv[:, 0:1], func=AF.Exp), [d_logA])
                t_nr = sch.op("vector", lambda e: e.tensor_scalar_mul(adv[:, 2:3], adv[:, 1:2], -1.0), [t_rate])
                t_ad = sch.op("scalar", lambda e: e.activation(out=adv[:, 3:4], in_=adv[:, 2:3], func=AF.Exp), [t_nr])
                t_ab0 = sch.op("vector", lambda e: e.memset(abc, 1.0))
                t_ab = sch.op("vector", lambda e: e.tensor_scalar_mul(abc, abc, adv[:, 3:4]), [t_ad, t_ab0])

                t_ones = sch.op("vector", lambda e: e.memset(ones1, 1.0))
                ppy_war = []
                rowb_war = []

                def row_bcast(dst_bf, src_row, add1):
                    nonlocal ppy_war, rowb_war
                    d_r = sch.dma("sync", rowb, src_row.rearrange("(one d) -> one d", one=1),
                                  deps=rowb_war)
                    tlast = None
                    last_mm = None
                    for c in range(D // 512):
                        sl = slice(c * 512, (c + 1) * 512)
                        t_mm = sch.op("tensor", lambda e, sl=sl: e.matmul(
                            out=ppy[:, 0:512], lhsT=ones1, rhs=rowb[0:1, sl],
                            start=True, stop=True), [d_r, t_ones] + ppy_war)
                        last_mm = t_mm
                        if add1:
                            t_ev = sch.op("vector", lambda e, sl=sl: e.tensor_scalar_add(
                                dst_bf[:, sl], ppy[:, 0:512], 1.0), [t_mm])
                        else:
                            t_ev = sch.op("vector", lambda e, sl=sl: e.tensor_copy(
                                out=dst_bf[:, sl], in_=ppy[:, 0:512]), [t_mm])
                        ppy_war = [t_ev]
                        tlast = t_ev
                    rowb_war = [last_mm]
                    return tlast

                t_dbc = row_bcast(dbc, Dv, add1=True)
                t_wb = row_bcast(wbcst, lnw, add1=False)
                t_bb = row_bcast(bbcst, lnb, add1=False)

                d_B = sch.dma("sync", stg, Bm)
                t_Bc = sch.op("vector", lambda e: e.tensor_copy(out=stgb, in_=stg), [d_B, t_idb])
                bl_toks = [pe_transpose(stgb[:, k * 128:(k + 1) * 128], Blhs[:, k * 128:(k + 1) * 128],
                                        k, [t_Bc]) for k in range(NKD)]
                t_Blhs = bl_toks[-1]
                d_C = sch.dma("sync", stg.rearrange("p (k n) -> p k n", k=NKD),
                              Cm.rearrange("(k p) n -> p k n", p=128), deps=[t_Bc])
                t_Cc = sch.op("vector", lambda e: e.tensor_copy(out=stgb, in_=stg), [d_C, t_Blhs])
                ct_toks = [pe_transpose(stgb[:, k * 128:(k + 1) * 128], CT[:, k * 128:(k + 1) * 128],
                                        k, [t_Cc]) for k in range(NKD)]
                t_CT = ct_toks[-1]
                d_rw = sch.dma("sync", stg[0:E, :], rw, deps=[t_Cc])
                t_rwc = sch.op("vector", lambda e: e.tensor_copy(out=stgb[0:E, :], in_=stg[0:E, :]),
                               [d_rw, t_CT])
                rl_toks = [pe_transpose(stgb[0:E, k * 128:(k + 1) * 128], rlhs[:, k * E:(k + 1) * E],
                                        k, [t_rwc]) for k in range(NKD)]
                t_rlhs = rl_toks[-1]
                d_rb = sch.dma("sync", rbias, rb.rearrange("(p one) -> p one", one=1))
                setup_scratch_end = [t_rlhs]

            xbuf = aen(nc.sbuf_tensor("xbuf", [128, 2 * D], F32)).ap()
            x1b = aen(nc.sbuf_tensor("x1b", [128, D], BF16)).ap()
            x1T = aen(nc.sbuf_tensor("x1T", [128, NKD * 128], BF16)).ap()
            r1b = aen(nc.sbuf_tensor("r1b", [128, D], F32)).ap()
            zb = aen(nc.sbuf_tensor("zb", [128, D], F32)).ap()
            hsb = aen(nc.sbuf_tensor("hsb", [128, 2 * 128], F32)).ap()
            hbfb = aen(nc.sbuf_tensor("hbfb", [128, 128], BF16)).ap()
            bnb = aen(nc.sbuf_tensor("bnb", [128, 24], F32)).ap()
            stat = aen(nc.sbuf_tensor("stat", [128, 4], F32)).ap()
            lgf = aen(nc.sbuf_tensor("lgf", [E, 128], F32)).ap()
            lgt = aen(nc.sbuf_tensor("lgt", [128, E], F32)).ap()
            m8 = aen(nc.sbuf_tensor("m8", [128, 8], F32)).ap()
            tp4 = aen(nc.sbuf_tensor("tp4", [128, 2 * E], F32)).ap()
            wtm = aen(nc.sbuf_tensor("wtm", [128, E], F32)).ap()
            wrow = aen(nc.sbuf_tensor("wrow", [E, 128], BF16)).ap()

            arena2_war = setup_scratch_end

            gu_war = {0: [], 1: []}
            gu_next = gu_load(0, 0, 0, [])
            gu_bufpar = 0

            xw = {0: list(arena2_war), 1: list(arena2_war)}
            r1war = list(arena2_war)
            x1b_war = list(arena2_war)
            x1T_war = list(arena2_war)
            stat_war = []
            zb_war = []
            hs_war = {0: [], 1: []}
            hbf_war = []
            ppu_war = []
            router_war = []
            carry = {"tok": None, "par": 0, "valid": False}
            last_r1_dma = None

            def ln_stats(src_ap, deps):
                nonlocal stat_war
                tks = [sch.op("vector", lambda e, c=c: e.bn_stats(
                    out=bnb[:, 6 * c:6 * (c + 1)], in_=src_ap[:, 512 * c:512 * (c + 1)]), deps)
                    for c in range(4)]
                t_ag = sch.op("vector", lambda e: e.bn_aggr(out=stat[:, 0:2], in_=bnb),
                              tks + stat_war)
                t_ln = sch.op("scalar", lambda e: e.activation(
                    out=stat[:, 2:3], in_=stat[:, 1:2], func=AF.Ln, bias=epsv), [t_ag, t_eps])
                t_rs = sch.op("scalar", lambda e: e.activation(
                    out=stat[:, 2:3], in_=stat[:, 2:3], func=AF.Exp, scale=-0.5), [t_ln])
                t_nb = sch.op("vector", lambda e: e.tensor_scalar(
                    out=stat[:, 3:4], in0=stat[:, 0:1], scalar1=stat[:, 2:3], scalar2=-1.0,
                    op0=ALU.mult, op1=ALU.mult), [t_rs])
                stat_war = []
                return t_rs, t_nb

            for gt in range(NTP + NT):
                own = gt >= NTP
                tt = gt - NTP
                par = gt % 2
                xs = xbuf[:, par * D:(par + 1) * D]
                d_x = sch.dma("sync", xs, xin[gt * 128:(gt + 1) * 128, :], deps=xw[par])
                xw[par] = []
                t_rs, t_nb = ln_stats(xs, [d_x])
                t_x1 = sch.op("scalar", lambda e, xs=xs: e.activation(
                    out=x1b, in_=xs, func=AF.Identity, bias=stat[:, 3:4], scale=stat[:, 2:3]),
                    [t_nb] + x1b_war)
                x1b_war = []
                stat_war.append(t_x1)
                xw[par].append(t_x1)
                if debug and own and tt == 0:
                    x1b_war.append(sch.dma("gp", dbg["x1_t0"], x1b, deps=[t_x1]))
                tr_toks = [pe_transpose(x1b[:, k * 128:(k + 1) * 128], x1T[:, k * 128:(k + 1) * 128],
                                        k, [t_x1] + x1T_war) for k in range(NKD)]
                x1T_war = []
                x1b_war.append(tr_toks[-1])

                def u_mm(e):
                    for k in range(NKD):
                        mm = e.matmul(out=ppu[:, 0:128], lhsT=Blhs[:, k * 128:(k + 1) * 128],
                                      rhs=x1T[:, k * 128:(k + 1) * 128],
                                      start=(k == 0), stop=(k == NKD - 1))
                    return mm
                t_umm = sch.op("tensor", u_mm,
                               [tr_toks[-1], t_Blhs] + ppu_war + ([carry["tok"]] if carry["valid"] else []))
                x1T_war.append(t_umm)
                spar = par
                hs = hsb[:, spar * 128:(spar + 1) * 128]
                init = hsb[:, (1 - spar) * 128 + 127:(1 - spar) * 128 + 128] if carry["valid"] else 0.0
                t_scan = sch.op("vector", lambda e, hs=hs, init=init: e.tensor_tensor_scan(
                    out=hs, data0=abc, data1=ppu[:, 0:128], initial=init,
                    op0=ALU.mult, op1=ALU.add),
                    [t_umm, t_ab] + hs_war[spar] + ([carry["tok"]] if carry["valid"] else []))
                hs_war[spar] = []
                ppu_war = [t_scan]
                carry.update(tok=t_scan, par=spar, valid=True)
                if not own:
                    continue

                t_hc = sch.op("vector", lambda e, hs=hs: e.tensor_copy(out=hbfb, in_=hs),
                              [t_scan] + hbf_war)
                hbf_war = []
                hs_war[spar].append(t_hc)
                t_m1 = sch.op("vector", lambda e: e.tensor_tensor(
                    out=zb, in0=x1b, in1=dbc, op=ALU.mult), [t_x1, t_dbc] + zb_war)
                zb_war = []
                x1b_war.append(t_m1)

                def y_mm(e):
                    for c in range(4):
                        mm = e.matmul(out=ppy[:, c * 512:(c + 1) * 512], lhsT=hbfb,
                                      rhs=CT[:, c * 512:(c + 1) * 512], start=True, stop=True)
                    return mm
                t_ymm = sch.op("tensor", y_mm, [t_hc, t_CT] + ppy_war)
                hbf_war.append(t_ymm)
                hs_war[spar].append(t_hc)
                t_z = sch.op("vector", lambda e: e.tensor_add(out=zb, in0=zb, in1=ppy),
                             [t_m1, t_ymm])
                ppy_war = [t_z]
                t_rs2, t_nb2 = ln_stats(zb, [t_z])
                t_nrm = sch.op("scalar", lambda e: e.activation(
                    out=x1b, in_=zb, func=AF.Identity, bias=stat[:, 3:4], scale=stat[:, 2:3]),
                    [t_nb2] + x1b_war)
                x1b_war = []
                stat_war.append(t_nrm)
                zb_war.append(t_nrm)
                r1 = r1b
                t_t2 = sch.op("vector", lambda e, r1=r1: e.tensor_tensor(
                    out=r1, in0=x1b, in1=wbcst, op=ALU.mult), [t_nrm, t_wb] + r1war)
                r1war = []
                x1b_war.append(t_t2)
                t_t3 = sch.op("vector", lambda e, r1=r1: e.tensor_add(out=r1, in0=r1, in1=bbcst),
                              [t_t2, t_bb])
                t_r1 = sch.op("vector", lambda e, r1=r1, xs=xs: e.tensor_add(out=r1, in0=r1, in1=xs),
                              [t_t3])
                xw[par].append(t_r1)
                d_r1 = sch.dma("sync", y[tt * 128:(tt + 1) * 128, :], r1, deps=[t_r1])
                last_r1_dma = d_r1
                r1war.append(d_r1)
                t_rs3, t_nb3 = ln_stats(r1, [t_r1])
                t_x3 = sch.op("scalar", lambda e, r1=r1: e.activation(
                    out=x1b, in_=r1, func=AF.Identity, bias=stat[:, 3:4], scale=stat[:, 2:3]),
                    [t_nb3] + x1b_war)
                x1b_war = []
                stat_war.append(t_x3)
                r1war.append(t_x3)
                tr3 = [pe_transpose(x1b[:, k * 128:(k + 1) * 128],
                                    x3T[:, k * T + tt * 128:k * T + tt * 128 + 128],
                                    k, [t_x3]) for k in range(NKD)]
                x1b_war.append(tr3[-1])

                def lg_mm(e, tt=tt):
                    for k in range(NKD):
                        mm = e.matmul(out=ppu[0:E, 384:384 + 128], lhsT=rlhs[:, k * E:(k + 1) * E],
                                      rhs=x3T[:, k * T + tt * 128:k * T + tt * 128 + 128],
                                      start=(k == 0), stop=(k == NKD - 1))
                    return mm
                t_lg = sch.op("tensor", lg_mm, [tr3[-1], t_rlhs] + ppu_war + router_war)
                t_lge = sch.op("vector", lambda e: e.tensor_scalar(
                    out=lgf, in0=ppu[0:E, 384:384 + 128], scalar1=rbias, scalar2=None,
                    op0=ALU.add), [t_lg, d_rb])
                ppu_war = [t_lge]
                t_lgt = sch.op("tensor", lambda e: e.transpose(
                    out=ptf[0:128, 0:E], in_=lgf, identity=idf[0:E, 0:E]), [t_lge] + router_war)
                t_lgte = sch.op("vector", lambda e: e.tensor_copy(out=lgt, in_=ptf[0:128, 0:E]),
                                [t_lgt])
                t_m8 = sch.op("vector", lambda e: e.max(out=m8, in_=lgt), [t_lgte])
                t_nmx = sch.op("vector", lambda e: e.tensor_scalar_mul(m8[:, 7:8], m8[:, 0:1], -1.0),
                               [t_m8])
                t_ex = sch.op("scalar", lambda e: e.activation(
                    out=tp4[:, 0:E], in_=lgt, func=AF.Exp, bias=m8[:, 7:8]), [t_nmx])
                t_mk = sch.op("vector", lambda e: e.tensor_scalar(
                    out=tp4[:, E:2 * E], in0=lgt, scalar1=m8[:, 3:4], scalar2=None,
                    op0=ALU.is_ge), [t_m8, t_ex])
                t_we = sch.op("vector", lambda e: e.tensor_mul(
                    out=tp4[:, 0:E], in0=tp4[:, 0:E], in1=tp4[:, E:2 * E]), [t_ex, t_mk])
                t_sm = sch.op("vector", lambda e: e.reduce_sum(out=m8[:, 6:7], in_=tp4[:, 0:E],
                                                               axis=AX.X), [t_we])
                t_rc = sch.op("vector", lambda e: e.reciprocal(out=m8[:, 6:7], in_=m8[:, 6:7]),
                              [t_sm])
                t_wt = sch.op("vector", lambda e: e.tensor_scalar_mul(wtm, tp4[:, 0:E], m8[:, 6:7]),
                              [t_rc])
                t_wT = sch.op("tensor", lambda e: e.transpose(
                    out=ptf[0:E, 0:128], in_=wtm, identity=idf), [t_wt, t_lgte])
                t_wfm = sch.op("vector", lambda e: e.tensor_copy(
                    out=wrow, in_=ptf[0:E, 0:128]), [t_wT] + router_war[3:4])
                d_wfm = sch.dma("sync", wfmd[:, tt * 128:(tt + 1) * 128], wrow, deps=[t_wfm])
                router_war = [t_wfm, t_ex, t_wT, d_wfm]
                if debug and tt == 0:
                    x1b_war.append(sch.dma("gp", dbg["x3_t0"], x1b, deps=[t_x3]))
                    hs_war[spar].append(sch.dma("sync", dbg["hs_t0"], hs, deps=[t_scan]))
                    zb_war.append(sch.dma("sync", dbg["z_t0"], zb, deps=[t_z]))
                    router_war.append(sch.dma("sync", dbg["lg_t0"], lgt, deps=[t_lgte]))
                    router_war.append(sch.dma("sync", dbg["wtm_t0"], wtm, deps=[t_wt]))

            if debug:
                sch.dma("gp", dbg["wfm"], wfm, deps=router_war)

            aend = sch.frontier()

        with ExitStack() as fctx:
            fen = fctx.enter_context
            dwn = fen(nc.sbuf_tensor("dwn", [128, NKI * D], BF16)).ap()
            ostg = fen(nc.sbuf_tensor("ostg", [128, D // 4], F32)).ap()
            wbcf = fen(nc.sbuf_tensor("wbcf", [128, TH], BF16)).ap()
            sgb = fen(nc.sbuf_tensor("sgb", [128, 512], BF16)).ap()
            sub = fen(nc.sbuf_tensor("sub", [128, 512], BF16)).ap()
            actb = fen(nc.sbuf_tensor("actb", [128, NKI * TH], BF16)).ap()
            accb0 = fen(nc.sbuf_tensor("accb0", [128, (NKD // 2) * TH], BF16)).ap()
            accb1 = fen(nc.sbuf_tensor("accb1", [128, (NKD // 2) * TH], BF16)).ap()

            def acc_ap(dt, lo, hi):
                b = accb0 if dt < NKD // 2 else accb1
                return b[:, (dt % (NKD // 2)) * TH + lo:(dt % (NKD // 2)) * TH + hi]
            pg = [fen(nc.psum_tensor(f"pg{i}", [128, 512], F32)).ap() for i in range(NTB)]
            pu2 = [fen(nc.psum_tensor(f"pu{i}", [128, 512], F32)).ap() for i in range(NTB)]
            pd = [fen(nc.psum_tensor(f"pd{i}", [128, 512], F32)).ap() for i in range(2)]

            def dwn_load_ki(eidx, ki, war):
                return sch.dma("gp", dwn[:, ki * D:(ki + 1) * D], dpt[eidx, ki], deps=war)

            pd_war = {0: aend, 1: aend}
            pg_war = {i: aend for i in range(NTB)}
            pu_war = {i: aend for i in range(NTB)}
            sg_war, su_war = list(aend), list(aend)
            act_war = list(aend)
            acc_war = list(aend)
            wbc_war = list(aend)
            ostg_war = []
            d_dwn = [dwn_load_ki(0, ki, aend) for ki in range(NKI)]
            dwn_war = []
            dwn_pend = None
            last_act = None
            last_down = None
            out_dma_toks = []

            for th in range(2):
                th0 = th * TH
                acc_tok = {}
                for e_i in range(E):
                    d_wbc = sch.dma("sync", wbcf,
                                    wfmd[e_i:e_i + 1, th0:th0 + TH].to_broadcast([128, TH]),
                                    deps=wbc_war + su_war)
                    wtoks = [d_wbc] * NTB
                    wbc_war = []
                    if debug and th == 0 and e_i == 0:
                        wbc_war.append(sch.dma("gp", dbg["wbc_e0"], wbcf, deps=wtoks))
                    for ic in range(NKI):
                        if dwn_pend is not None and e_i == dwn_pend:
                            d_dwn.append(dwn_load_ki(e_i, ic, dwn_war))
                            if ic == NKI - 1:
                                dwn_pend = None
                        cur = gu_next
                        curbuf = gu_bufpar
                        ne, nth, nic = e_i, th, ic + 1
                        if nic == NKI:
                            nic, ne = 0, e_i + 1
                            if ne == E:
                                ne, nth = 0, th + 1
                        if nth < 2:
                            gu_bufpar = 1 - gu_bufpar
                            gu_next = gu_load(ne, nic, gu_bufpar, gu_war[gu_bufpar])
                        g_ap = gub[:, curbuf * CHW:(curbuf + 1) * CHW]
                        u_ap = gub[:, (2 + curbuf) * CHW:(2 + curbuf + 1) * CHW]

                        def gumm(e, g_ap=g_ap, u_ap=u_ap, th0=th0):
                            for k in range(NKD):
                                gl = g_ap[:, k * 128:(k + 1) * 128]
                                ul = u_ap[:, k * 128:(k + 1) * 128]
                                for tb in range(NTB):
                                    rhs = x3T[:, k * T + th0 + tb * 512:k * T + th0 + tb * 512 + 512]
                                    e.matmul(out=pg[tb], lhsT=gl, rhs=rhs,
                                             start=(k == 0), stop=(k == NKD - 1))
                                for tb in range(NTB):
                                    rhs = x3T[:, k * T + th0 + tb * 512:k * T + th0 + tb * 512 + 512]
                                    mm = e.matmul(out=pu2[tb], lhsT=ul, rhs=rhs,
                                                  start=(k == 0), stop=(k == NKD - 1))
                            return mm

                        gu_deps = [cur[0], cur[1]]
                        for tb in range(NTB):
                            gu_deps += pg_war[tb] + pu_war[tb]
                        t_gu = sch.op("tensor", gumm, gu_deps)
                        for tb in range(NTB):
                            pgx, pux = pg[tb], pu2[tb]
                            t_sg = sch.op("scalar", lambda e, pgx=pgx: e.activation(
                                out=sgb, in_=pgx, func=AF.Silu), [t_gu] + sg_war)
                            t_su = sch.op("vector", lambda e, pux=pux, tb=tb: e.tensor_tensor(
                                out=sub, in0=pux, in1=wbcf[:, tb * 512:(tb + 1) * 512],
                                op=ALU.mult), [t_gu, wtoks[tb]] + su_war)
                            dst = actb[:, ic * TH + tb * 512:ic * TH + (tb + 1) * 512]
                            t_act = sch.op("vector", lambda e, dst=dst: e.tensor_mul(
                                out=dst, in0=sgb, in1=sub), [t_sg, t_su] + act_war)
                            act_war = []
                            sg_war, su_war = [t_act], [t_act]
                            pg_war[tb] = [t_sg]
                            pu_war[tb] = [t_su]
                            last_act = t_act
                        gu_war[curbuf] = [t_gu]
                    for tb in range(NTB):
                        for dt in range(NKD):
                            pdx = pd[dt % 2]

                            def dmm(e, tb=tb, dt=dt, pdx=pdx):
                                for ki in range(NKI):
                                    mm = e.matmul(
                                        out=pdx,
                                        lhsT=dwn[:, ki * D + dt * 128:ki * D + dt * 128 + 128],
                                        rhs=actb[:, ki * TH + tb * 512:ki * TH + (tb + 1) * 512],
                                        start=(ki == 0), stop=(ki == NKI - 1))
                                return mm

                            t_d = sch.op("tensor", dmm, [last_act] + d_dwn + pd_war[dt % 2])
                            last_down = t_d
                            asl = acc_ap(dt, tb * 512, (tb + 1) * 512)
                            if e_i == 0:
                                t_a = sch.op("vector", lambda e, asl=asl, pdx=pdx: e.tensor_copy(
                                    out=asl, in_=pdx), [t_d] + acc_war)
                            else:
                                t_a = sch.op("vector", lambda e, asl=asl, pdx=pdx: e.tensor_tensor(
                                    out=asl, in0=pdx, in1=asl, op=ALU.add),
                                    [t_d, acc_tok.get((dt, tb))] + acc_war)
                            acc_tok[(dt, tb)] = t_a
                            pd_war[dt % 2] = [t_a]
                    acc_war = []
                    if debug and th == 0 and e_i == 0:
                        act_war_dump = sch.dma("gp", dbg["act_e0"], actb, deps=[last_act])
                        act_war = [act_war_dump, last_down]
                    else:
                        act_war = [last_down]
                    dwn_war = [last_down]
                    dwn_pend = (e_i + 1) if e_i + 1 < E else (0 if th == 0 else None)
                    d_dwn = []
                if debug and th == 0:
                    acc_war.append(sch.dma("gp", dbg["acc_th0"][:, 0:(NKD // 2) * TH], accb0,
                                           deps=[acc_tok[k] for k in sorted(acc_tok)]))
                    acc_war.append(sch.dma("gp", dbg["acc_th0"][:, (NKD // 2) * TH:], accb1,
                                           deps=[acc_tok[k] for k in sorted(acc_tok)]))
                wo_tr_last = None
                for tt in range(TH // 128):
                    trow = th0 + tt * 128
                    for dh in range(4):
                        etoks = []
                        for j in range(NKD // 4):
                            dt = dh * (NKD // 4) + j
                            src = acc_ap(dt, tt * 128, tt * 128 + 128)
                            t_e = pe_transpose(src, ostg[:, j * 128:(j + 1) * 128], dt,
                                               [acc_tok[(dt, (tt * 128) // 512)]] + ostg_war)
                            etoks.append(t_e)
                            wo_tr_last = t_e
                        d_o = sch.dma("gp", y[trow:trow + 128, dh * (D // 4):(dh + 1) * (D // 4)],
                                      ostg, deps=[etoks[-1]] + sch.qfrontier("sync_q"),
                                      accum_op=ALU.add)
                        ostg_war = [d_o]
                        out_dma_toks.append(d_o)
                acc_war = [wo_tr_last]

        blk = en(nc.Block())
        sch.build(blk)
    return nc


from concourse.bass_utils import run_bass_kernel_spmd

_D = 2048


def _pretile_gu(w, NKI, NKD=16):
    E = w.shape[0]
    return np.ascontiguousarray(
        w.reshape(E, NKD, 128, NKI, 128).transpose(0, 3, 2, 1, 4))


def _make_sel(E):
    s = np.zeros((E, E * 128), dtype=np.float32)
    for e in range(E):
        s[e, e * 128:(e + 1) * 128] = 1.0
    return s


def _make_xin(x_batch, p, t_own, t_prior):
    own = x_batch[p * t_own:(p + 1) * t_own]
    prior = x_batch[:p * t_own]
    pad = np.zeros((t_prior - prior.shape[0], _D), dtype=np.float32)
    return np.ascontiguousarray(np.concatenate([pad, prior, own], axis=0))


def kernel(L0_attn_x, L0_attn_log_A_diag, L0_attn_B_mat, L0_attn_C_mat,
           L0_attn_D_vec, L0_attn_ln_w, L0_attn_ln_b, L0_ffn_router_w,
           L0_ffn_expert_bias, L0_ffn_gate_proj, L0_ffn_up_proj,
           L0_ffn_down_proj):
    cfg = CFG(16, 1792, 2048, 6144)
    x = np.ascontiguousarray(np.asarray(L0_attn_x, dtype=np.float32))
    shared = {
        "logA": np.ascontiguousarray(np.asarray(L0_attn_log_A_diag, np.float32)),
        "Bm": np.ascontiguousarray(np.asarray(L0_attn_B_mat, np.float32)),
        "Cm": np.ascontiguousarray(np.asarray(L0_attn_C_mat, np.float32)),
        "Dv": np.ascontiguousarray(np.asarray(L0_attn_D_vec, np.float32)),
        "lnw": np.ascontiguousarray(np.asarray(L0_attn_ln_w, np.float32)),
        "lnb": np.ascontiguousarray(np.asarray(L0_attn_ln_b, np.float32)),
        "rw": np.ascontiguousarray(np.asarray(L0_ffn_router_w, np.float32)),
        "rb": np.ascontiguousarray(np.asarray(L0_ffn_expert_bias, np.float32)),
        "gpt": _pretile_gu(np.asarray(L0_ffn_gate_proj, np.float32), cfg.NKI),
        "upt": _pretile_gu(np.asarray(L0_ffn_up_proj, np.float32), cfg.NKI),
        "dpt": np.ascontiguousarray(
            np.asarray(L0_ffn_down_proj, np.float32).reshape(cfg.E, cfg.NKI, 128, _D)),
        "ident": np.eye(128, dtype=np.float32),
        "sel": _make_sel(cfg.E),
    }
    nc = build_nc(cfg)
    in_maps = []
    for core in range(8):
        b, p = core // 4, core % 4
        m = dict(shared)
        m["xin"] = _make_xin(x[b], p, cfg.T, cfg.TP)
        in_maps.append(m)
    res = run_bass_kernel_spmd(nc, in_maps, core_ids=list(range(8)))
    y = np.empty((2, 8192, _D), dtype=np.float32)
    for core in range(8):
        b, p = core // 4, core % 4
        y[b, p * cfg.T:(p + 1) * cfg.T] = res.results[core]["y"]
    return y


# version 04
# speedup vs baseline: 1.0915x; 1.0915x over previous
from contextlib import ExitStack

import numpy as np
import concourse.bass as bass
import concourse.mybir as mybir

F32 = mybir.dt.float32
BF16 = mybir.dt.bfloat16
AF = mybir.ActivationFunctionType
ALU = mybir.AluOpType
AX = mybir.AxisListType

COMPUTE = ("tensor", "vector", "scalar", "gpsimd")
DMAQ = ("sync_q", "gp_q", "sc_q")
NSLOT = 8
LN_EPS = 1e-5


class Sched:

    def __init__(self, nc, ctx):
        self.nc = nc
        names = list(COMPUTE)
        for q in DMAQ:
            names += [f"{q}{i}" for i in range(NSLOT)]
        self.sems = {n: ctx.enter_context(nc.semaphore("sem_" + n)) for n in names}
        self.count = {n: 0 for n in names}
        self.qidx = {q: 0 for q in DMAQ}
        self.qlast = {q: [None] * NSLOT for q in DMAQ}
        self.ops = []

    def _tok(self, stream, amount):
        self.count[stream] += amount
        return (stream, self.count[stream])

    def op(self, eng, fn, deps=()):
        tok = self._tok(eng, 1)
        self.ops.append((eng, eng, fn, tuple(d for d in deps if d), 1))
        return tok

    def dma(self, queue, out, in_, deps=(), **kw):
        q = {"sync": "sync_q", "gp": "gp_q", "sc": "sc_q"}[queue]
        body = {"sync": "sync", "gp": "gpsimd", "sc": "scalar"}[queue]
        slot = self.qidx[q] % NSLOT
        self.qidx[q] += 1
        stream = f"{q}{slot}"
        deps = list(d for d in deps if d)
        prev = self.qlast[q][slot]
        if prev is not None:
            deps.append(prev)
        tok = self._tok(stream, 16)
        self.qlast[q][slot] = tok

        def fn(e):
            return e.dma_start(out=out, in_=in_, **kw)

        self.ops.append((body, stream, fn, tuple(deps), 16))
        return tok

    def frontier(self):
        return [(s, v) for s, v in self.count.items() if v > 0]

    def qfrontier(self, q):
        return [t for t in self.qlast[q] if t is not None]

    def build(self, blk):
        bodies = {}
        for body, stream, fn, deps, inc in self.ops:
            bodies.setdefault(body, []).append((stream, fn, deps, inc))
        if "sync" not in bodies:
            bodies["sync"] = []
        final_counts = dict(self.count)

        def make(body_name, items):
            def emit(eng):
                hw = {}
                for stream, fn, deps, inc in items:
                    for (s, v) in deps:
                        if hw.get(s, 0) < v:
                            eng.wait_ge(self.sems[s], v)
                            hw[s] = v
                    fn(eng).then_inc(self.sems[stream], inc)
                if body_name == "sync":
                    for q in self.count:
                        if final_counts[q] > hw.get(q, 0):
                            eng.wait_ge(self.sems[q], final_counts[q])
            return emit

        for body_name, items in bodies.items():
            getattr(blk, body_name)(make(body_name, items))


class CFG:
    def __init__(self, n_exp=16, inter=1792, t_own=2048, t_prior=6144):
        self.D = 2048
        self.NKD = 16
        self.E = n_exp
        self.I = inter
        self.NKI = inter // 128
        self.T = t_own
        self.TP = t_prior
        self.NT = t_own // 128
        self.NTP = t_prior // 128
        self.TH = t_own // 2
        self.NTB = self.TH // 512
        assert t_own % 1024 == 0 and self.TH % 512 == 0 and inter % 128 == 0


def build_nc(cfg: CFG, debug=False):
    nc = bass.Bass("TRN2", target_bir_lowering=False)
    D, E, I, T, TP = cfg.D, cfg.E, cfg.I, cfg.T, cfg.TP
    NKD, NKI, NT, NTP, TH, NTB = cfg.NKD, cfg.NKI, cfg.NT, cfg.NTP, cfg.TH, cfg.NTB

    xin = nc.dram_tensor("xin", [TP + T, D], F32, kind="ExternalInput").ap()
    logA = nc.dram_tensor("logA", [128], F32, kind="ExternalInput").ap()
    Bm = nc.dram_tensor("Bm", [128, D], F32, kind="ExternalInput").ap()
    Cm = nc.dram_tensor("Cm", [D, 128], F32, kind="ExternalInput").ap()
    Dv = nc.dram_tensor("Dv", [D], F32, kind="ExternalInput").ap()
    lnw = nc.dram_tensor("lnw", [D], F32, kind="ExternalInput").ap()
    lnb = nc.dram_tensor("lnb", [D], F32, kind="ExternalInput").ap()
    rw = nc.dram_tensor("rw", [E, D], F32, kind="ExternalInput").ap()
    rb = nc.dram_tensor("rb", [E], F32, kind="ExternalInput").ap()
    gpt = nc.dram_tensor("gpt", [E, NKI, 128, NKD, 128], F32, kind="ExternalInput").ap()
    upt = nc.dram_tensor("upt", [E, NKI, 128, NKD, 128], F32, kind="ExternalInput").ap()
    dpt = nc.dram_tensor("dpt", [E, NKI, 128, D], F32, kind="ExternalInput").ap()
    ident_in = nc.dram_tensor("ident", [128, 128], F32, kind="ExternalInput").ap()
    sel_in = nc.dram_tensor("sel", [E, E * 128], F32, kind="ExternalInput").ap()
    y = nc.dram_tensor("y", [T, D], F32, kind="ExternalOutput").ap()
    wfmd = nc.dram_tensor("wfmd", [E, T], BF16, kind="Internal").ap()
    dbg = {}
    if debug:
        dbg["x1_t0"] = nc.dram_tensor("dbg_x1_t0", [128, D], F32, kind="ExternalOutput").ap()
        dbg["hs_t0"] = nc.dram_tensor("dbg_hs_t0", [128, 128], F32, kind="ExternalOutput").ap()
        dbg["z_t0"] = nc.dram_tensor("dbg_z_t0", [128, D], F32, kind="ExternalOutput").ap()
        dbg["x3_t0"] = nc.dram_tensor("dbg_x3_t0", [128, D], F32, kind="ExternalOutput").ap()
        dbg["lg_t0"] = nc.dram_tensor("dbg_lg_t0", [128, E], F32, kind="ExternalOutput").ap()
        dbg["wtm_t0"] = nc.dram_tensor("dbg_wtm_t0", [128, E], F32, kind="ExternalOutput").ap()
        dbg["acc_th0"] = nc.dram_tensor("dbg_acc_th0", [128, NKD * (T // 2)], F32, kind="ExternalOutput").ap()
        dbg["act_e0"] = nc.dram_tensor("dbg_act_e0", [128, NKI * (T // 2)], F32, kind="ExternalOutput").ap()
        dbg["wbc_e0"] = nc.dram_tensor("dbg_wbc_e0", [128, T // 2], F32, kind="ExternalOutput").ap()
        dbg["wfm"] = nc.dram_tensor("dbg_wfm", [E, T], F32, kind="ExternalOutput").ap()
        dbg["selb"] = nc.dram_tensor("dbg_selb", [E, 128], F32, kind="ExternalOutput").ap()
        dbg["pbc"] = nc.dram_tensor("dbg_pbc", [128, 512], F32, kind="ExternalOutput").ap()

    with ExitStack() as ctx:
        en = ctx.enter_context
        sch = Sched(nc, ctx)

        x3T = en(nc.sbuf_tensor("x3T", [128, NKD * T], BF16)).ap()
        gub = en(nc.sbuf_tensor("gub", [128, 6 * NKD * 128], BF16)).ap()
        idf = en(nc.sbuf_tensor("idf", [128, 128], F32)).ap()
        idb = en(nc.sbuf_tensor("idb", [128, 128], BF16)).ap()
        rlhs = en(nc.sbuf_tensor("rlhs", [128, NKD * E], BF16)).ap()
        rbias = en(nc.sbuf_tensor("rbias", [E, 1], F32)).ap()
        pt = [en(nc.psum_tensor("pt0", [128, 512], BF16)).ap(),
              en(nc.psum_tensor("pt1", [128, 512], BF16)).ap()]
        pt_war = {0: [], 1: []}

        CHW = NKD * 128

        def pe_transpose(src_ap, dst_ap, k, deps, fp32=False):
            P, F = src_ap.shape
            bank = k % 2
            idap = (idf if fp32 else idb)[0:P, 0:P]
            pb = pt[bank].bitcast(F32) if fp32 else pt[bank]
            pdst = pb[0:F, 0:P]
            t_t = sch.op("tensor", lambda e: e.transpose(out=pdst, in_=src_ap, identity=idap),
                         list(deps) + pt_war[bank])
            t_e = sch.op("vector", lambda e: e.tensor_copy(out=dst_ap, in_=pdst), [t_t])
            pt_war[bank] = [t_e]
            return t_e

        d_id = sch.dma("sync", idf, ident_in)
        t_idb = sch.op("vector", lambda e: e.tensor_copy(out=idb, in_=idf), [d_id])

        def gu_load(eidx, ic, buf, war):
            tg = sch.dma("gp", gub[:, buf * CHW:(buf + 1) * CHW],
                         gpt[eidx, ic].rearrange("p k i -> p (k i)"), deps=war)
            tu = sch.dma("gp", gub[:, (3 + buf) * CHW:(3 + buf + 1) * CHW],
                         upt[eidx, ic].rearrange("p k i -> p (k i)"), deps=war)
            return (tg, tu)

        gu_seq = [(th, e_i, ic) for th in range(2) for e_i in range(E)
                  for ic in range(NKI)]

        with ExitStack() as actx:
            aen = actx.enter_context
            Blhs = aen(nc.sbuf_tensor("Blhs", [128, NKD * 128], BF16)).ap()
            CT = aen(nc.sbuf_tensor("CT", [128, D], BF16)).ap()
            dbc = aen(nc.sbuf_tensor("dbc", [128, D], BF16)).ap()
            wbcst = aen(nc.sbuf_tensor("wbcst", [128, D], BF16)).ap()
            bbcst = aen(nc.sbuf_tensor("bbcst", [128, D], BF16)).ap()
            abc = aen(nc.sbuf_tensor("abc", [128, 128], F32)).ap()
            adv = aen(nc.sbuf_tensor("adv", [128, 4], F32)).ap()
            epsv = aen(nc.sbuf_tensor("epsv", [128, 1], F32)).ap()
            ppu = aen(nc.psum_tensor("ppu", [128, 512], F32)).ap()
            ppy = aen(nc.psum_tensor("ppy", [128, 2048], F32)).ap()
            ptf = aen(nc.psum_tensor("ptf", [128, 128], F32)).ap()

            with ExitStack() as sctx:
                sen = sctx.enter_context
                rowb = sen(nc.sbuf_tensor("rowb", [1, D], F32)).ap()
                ones1 = sen(nc.sbuf_tensor("ones1", [1, 128], F32)).ap()
                stg = sen(nc.sbuf_tensor("stg", [128, 2048], F32)).ap()
                stgb = sen(nc.sbuf_tensor("stgb", [128, 2048], BF16)).ap()

                t_eps = sch.op("vector", lambda e: e.memset(epsv, LN_EPS))
                d_logA = sch.dma("sync", adv[:, 0:1], logA.rearrange("(p one) -> p one", one=1))
                t_rate = sch.op("scalar", lambda e: e.activation(out=adv[:, 1:2], in_=adv[:, 0:1], func=AF.Exp), [d_logA])
                t_nr = sch.op("vector", lambda e: e.tensor_scalar_mul(adv[:, 2:3], adv[:, 1:2], -1.0), [t_rate])
                t_ad = sch.op("scalar", lambda e: e.activation(out=adv[:, 3:4], in_=adv[:, 2:3], func=AF.Exp), [t_nr])
                t_ab0 = sch.op("vector", lambda e: e.memset(abc, 1.0))
                t_ab = sch.op("vector", lambda e: e.tensor_scalar_mul(abc, abc, adv[:, 3:4]), [t_ad, t_ab0])

                t_ones = sch.op("vector", lambda e: e.memset(ones1, 1.0))
                ppy_war = []
                rowb_war = []

                def row_bcast(dst_bf, src_row, add1):
                    nonlocal ppy_war, rowb_war
                    d_r = sch.dma("sync", rowb, src_row.rearrange("(one d) -> one d", one=1),
                                  deps=rowb_war)
                    tlast = None
                    last_mm = None
                    for c in range(D // 512):
                        sl = slice(c * 512, (c + 1) * 512)
                        t_mm = sch.op("tensor", lambda e, sl=sl: e.matmul(
                            out=ppy[:, 0:512], lhsT=ones1, rhs=rowb[0:1, sl],
                            start=True, stop=True), [d_r, t_ones] + ppy_war)
                        last_mm = t_mm
                        if add1:
                            t_ev = sch.op("vector", lambda e, sl=sl: e.tensor_scalar_add(
                                dst_bf[:, sl], ppy[:, 0:512], 1.0), [t_mm])
                        else:
                            t_ev = sch.op("vector", lambda e, sl=sl: e.tensor_copy(
                                out=dst_bf[:, sl], in_=ppy[:, 0:512]), [t_mm])
                        ppy_war = [t_ev]
                        tlast = t_ev
                    rowb_war = [last_mm]
                    return tlast

                t_dbc = row_bcast(dbc, Dv, add1=True)
                t_wb = row_bcast(wbcst, lnw, add1=False)
                t_bb = row_bcast(bbcst, lnb, add1=False)

                d_B = sch.dma("sync", stg, Bm)
                t_Bc = sch.op("vector", lambda e: e.tensor_copy(out=stgb, in_=stg), [d_B, t_idb])
                bl_toks = [pe_transpose(stgb[:, k * 128:(k + 1) * 128], Blhs[:, k * 128:(k + 1) * 128],
                                        k, [t_Bc]) for k in range(NKD)]
                t_Blhs = bl_toks[-1]
                d_C = sch.dma("sync", stg.rearrange("p (k n) -> p k n", k=NKD),
                              Cm.rearrange("(k p) n -> p k n", p=128), deps=[t_Bc])
                t_Cc = sch.op("vector", lambda e: e.tensor_copy(out=stgb, in_=stg), [d_C, t_Blhs])
                ct_toks = [pe_transpose(stgb[:, k * 128:(k + 1) * 128], CT[:, k * 128:(k + 1) * 128],
                                        k, [t_Cc]) for k in range(NKD)]
                t_CT = ct_toks[-1]
                d_rw = sch.dma("sync", stg[0:E, :], rw, deps=[t_Cc])
                t_rwc = sch.op("vector", lambda e: e.tensor_copy(out=stgb[0:E, :], in_=stg[0:E, :]),
                               [d_rw, t_CT])
                rl_toks = [pe_transpose(stgb[0:E, k * 128:(k + 1) * 128], rlhs[:, k * E:(k + 1) * E],
                                        k, [t_rwc]) for k in range(NKD)]
                t_rlhs = rl_toks[-1]
                d_rb = sch.dma("sync", rbias, rb.rearrange("(p one) -> p one", one=1))
                setup_scratch_end = [t_rlhs]

            xbuf = aen(nc.sbuf_tensor("xbuf", [128, 2 * D], F32)).ap()
            x1b = aen(nc.sbuf_tensor("x1b", [128, D], BF16)).ap()
            x1T = aen(nc.sbuf_tensor("x1T", [128, NKD * 128], BF16)).ap()
            r1b = aen(nc.sbuf_tensor("r1b", [128, D], F32)).ap()
            zb = aen(nc.sbuf_tensor("zb", [128, D], F32)).ap()
            hsb = aen(nc.sbuf_tensor("hsb", [128, 2 * 128], F32)).ap()
            hbfb = aen(nc.sbuf_tensor("hbfb", [128, 128], BF16)).ap()
            bnb = aen(nc.sbuf_tensor("bnb", [128, 24], F32)).ap()
            stat = aen(nc.sbuf_tensor("stat", [128, 4], F32)).ap()
            lgf = aen(nc.sbuf_tensor("lgf", [E, 128], F32)).ap()
            lgt = aen(nc.sbuf_tensor("lgt", [128, E], F32)).ap()
            m8 = aen(nc.sbuf_tensor("m8", [128, 8], F32)).ap()
            tp4 = aen(nc.sbuf_tensor("tp4", [128, 2 * E], F32)).ap()
            wtm = aen(nc.sbuf_tensor("wtm", [128, E], F32)).ap()
            wrow = aen(nc.sbuf_tensor("wrow", [E, 128], BF16)).ap()

            arena2_war = setup_scratch_end

            gu_toks = {}
            gu_done = {}
            for s in range(3):
                th_s, e_s, ic_s = gu_seq[s]
                gu_toks[s] = gu_load(e_s, ic_s, s % 3, [])

            xw = {0: list(arena2_war), 1: list(arena2_war)}
            r1war = list(arena2_war)
            x1b_war = list(arena2_war)
            x1T_war = list(arena2_war)
            stat_war = []
            zb_war = []
            hs_war = {0: [], 1: []}
            hbf_war = []
            ppu_war = []
            router_war = []
            carry = {"tok": None, "par": 0, "valid": False}
            last_r1_dma = None

            def ln_stats(src_ap, deps):
                nonlocal stat_war
                tks = [sch.op("vector", lambda e, c=c: e.bn_stats(
                    out=bnb[:, 6 * c:6 * (c + 1)], in_=src_ap[:, 512 * c:512 * (c + 1)]), deps)
                    for c in range(4)]
                t_ag = sch.op("vector", lambda e: e.bn_aggr(out=stat[:, 0:2], in_=bnb),
                              tks + stat_war)
                t_ln = sch.op("scalar", lambda e: e.activation(
                    out=stat[:, 2:3], in_=stat[:, 1:2], func=AF.Ln, bias=epsv), [t_ag, t_eps])
                t_rs = sch.op("scalar", lambda e: e.activation(
                    out=stat[:, 2:3], in_=stat[:, 2:3], func=AF.Exp, scale=-0.5), [t_ln])
                t_nb = sch.op("vector", lambda e: e.tensor_scalar(
                    out=stat[:, 3:4], in0=stat[:, 0:1], scalar1=stat[:, 2:3], scalar2=-1.0,
                    op0=ALU.mult, op1=ALU.mult), [t_rs])
                stat_war = []
                return t_rs, t_nb

            for gt in range(NTP + NT):
                own = gt >= NTP
                tt = gt - NTP
                par = gt % 2
                xs = xbuf[:, par * D:(par + 1) * D]
                d_x = sch.dma("sync", xs, xin[gt * 128:(gt + 1) * 128, :], deps=xw[par])
                xw[par] = []
                t_rs, t_nb = ln_stats(xs, [d_x])
                t_x1 = sch.op("scalar", lambda e, xs=xs: e.activation(
                    out=x1b, in_=xs, func=AF.Identity, bias=stat[:, 3:4], scale=stat[:, 2:3]),
                    [t_nb] + x1b_war)
                x1b_war = []
                stat_war.append(t_x1)
                xw[par].append(t_x1)
                if debug and own and tt == 0:
                    x1b_war.append(sch.dma("gp", dbg["x1_t0"], x1b, deps=[t_x1]))
                tr_toks = [pe_transpose(x1b[:, k * 128:(k + 1) * 128], x1T[:, k * 128:(k + 1) * 128],
                                        k, [t_x1] + x1T_war) for k in range(NKD)]
                x1T_war = []
                x1b_war.append(tr_toks[-1])

                def u_mm(e):
                    for k in range(NKD):
                        mm = e.matmul(out=ppu[:, 0:128], lhsT=Blhs[:, k * 128:(k + 1) * 128],
                                      rhs=x1T[:, k * 128:(k + 1) * 128],
                                      start=(k == 0), stop=(k == NKD - 1))
                    return mm
                t_umm = sch.op("tensor", u_mm,
                               [tr_toks[-1], t_Blhs] + ppu_war + ([carry["tok"]] if carry["valid"] else []))
                x1T_war.append(t_umm)
                spar = par
                hs = hsb[:, spar * 128:(spar + 1) * 128]
                init = hsb[:, (1 - spar) * 128 + 127:(1 - spar) * 128 + 128] if carry["valid"] else 0.0
                t_scan = sch.op("vector", lambda e, hs=hs, init=init: e.tensor_tensor_scan(
                    out=hs, data0=abc, data1=ppu[:, 0:128], initial=init,
                    op0=ALU.mult, op1=ALU.add),
                    [t_umm, t_ab] + hs_war[spar] + ([carry["tok"]] if carry["valid"] else []))
                hs_war[spar] = []
                ppu_war = [t_scan]
                carry.update(tok=t_scan, par=spar, valid=True)
                if not own:
                    continue

                t_hc = sch.op("vector", lambda e, hs=hs: e.tensor_copy(out=hbfb, in_=hs),
                              [t_scan] + hbf_war)
                hbf_war = []
                hs_war[spar].append(t_hc)
                t_m1 = sch.op("vector", lambda e: e.tensor_tensor(
                    out=zb, in0=x1b, in1=dbc, op=ALU.mult), [t_x1, t_dbc] + zb_war)
                zb_war = []
                x1b_war.append(t_m1)

                def y_mm(e):
                    for c in range(4):
                        mm = e.matmul(out=ppy[:, c * 512:(c + 1) * 512], lhsT=hbfb,
                                      rhs=CT[:, c * 512:(c + 1) * 512], start=True, stop=True)
                    return mm
                t_ymm = sch.op("tensor", y_mm, [t_hc, t_CT] + ppy_war)
                hbf_war.append(t_ymm)
                hs_war[spar].append(t_hc)
                t_z = sch.op("vector", lambda e: e.tensor_add(out=zb, in0=zb, in1=ppy),
                             [t_m1, t_ymm])
                ppy_war = [t_z]
                t_rs2, t_nb2 = ln_stats(zb, [t_z])
                t_nrm = sch.op("scalar", lambda e: e.activation(
                    out=x1b, in_=zb, func=AF.Identity, bias=stat[:, 3:4], scale=stat[:, 2:3]),
                    [t_nb2] + x1b_war)
                x1b_war = []
                stat_war.append(t_nrm)
                zb_war.append(t_nrm)
                r1 = r1b
                t_t2 = sch.op("vector", lambda e, r1=r1: e.tensor_tensor(
                    out=r1, in0=x1b, in1=wbcst, op=ALU.mult), [t_nrm, t_wb] + r1war)
                r1war = []
                x1b_war.append(t_t2)
                t_t3 = sch.op("vector", lambda e, r1=r1: e.tensor_add(out=r1, in0=r1, in1=bbcst),
                              [t_t2, t_bb])
                t_r1 = sch.op("vector", lambda e, r1=r1, xs=xs: e.tensor_add(out=r1, in0=r1, in1=xs),
                              [t_t3])
                xw[par].append(t_r1)
                d_r1 = sch.dma("sync", y[tt * 128:(tt + 1) * 128, :], r1, deps=[t_r1])
                last_r1_dma = d_r1
                r1war.append(d_r1)
                t_rs3, t_nb3 = ln_stats(r1, [t_r1])
                t_x3 = sch.op("scalar", lambda e, r1=r1: e.activation(
                    out=x1b, in_=r1, func=AF.Identity, bias=stat[:, 3:4], scale=stat[:, 2:3]),
                    [t_nb3] + x1b_war)
                x1b_war = []
                stat_war.append(t_x3)
                r1war.append(t_x3)
                tr3 = [pe_transpose(x1b[:, k * 128:(k + 1) * 128],
                                    x3T[:, k * T + tt * 128:k * T + tt * 128 + 128],
                                    k, [t_x3]) for k in range(NKD)]
                x1b_war.append(tr3[-1])

                def lg_mm(e, tt=tt):
                    for k in range(NKD):
                        mm = e.matmul(out=ppu[0:E, 384:384 + 128], lhsT=rlhs[:, k * E:(k + 1) * E],
                                      rhs=x3T[:, k * T + tt * 128:k * T + tt * 128 + 128],
                                      start=(k == 0), stop=(k == NKD - 1))
                    return mm
                t_lg = sch.op("tensor", lg_mm, [tr3[-1], t_rlhs] + ppu_war + router_war)
                t_lge = sch.op("vector", lambda e: e.tensor_scalar(
                    out=lgf, in0=ppu[0:E, 384:384 + 128], scalar1=rbias, scalar2=None,
                    op0=ALU.add), [t_lg, d_rb])
                ppu_war = [t_lge]
                t_lgt = sch.op("tensor", lambda e: e.transpose(
                    out=ptf[0:128, 0:E], in_=lgf, identity=idf[0:E, 0:E]), [t_lge] + router_war)
                t_lgte = sch.op("vector", lambda e: e.tensor_copy(out=lgt, in_=ptf[0:128, 0:E]),
                                [t_lgt])
                t_m8 = sch.op("vector", lambda e: e.max(out=m8, in_=lgt), [t_lgte])
                t_nmx = sch.op("vector", lambda e: e.tensor_scalar_mul(m8[:, 7:8], m8[:, 0:1], -1.0),
                               [t_m8])
                t_ex = sch.op("scalar", lambda e: e.activation(
                    out=tp4[:, 0:E], in_=lgt, func=AF.Exp, bias=m8[:, 7:8]), [t_nmx])
                t_mk = sch.op("vector", lambda e: e.tensor_scalar(
                    out=tp4[:, E:2 * E], in0=lgt, scalar1=m8[:, 3:4], scalar2=None,
                    op0=ALU.is_ge), [t_m8, t_ex])
                t_we = sch.op("vector", lambda e: e.tensor_mul(
                    out=tp4[:, 0:E], in0=tp4[:, 0:E], in1=tp4[:, E:2 * E]), [t_ex, t_mk])
                t_sm = sch.op("vector", lambda e: e.reduce_sum(out=m8[:, 6:7], in_=tp4[:, 0:E],
                                                               axis=AX.X), [t_we])
                t_rc = sch.op("vector", lambda e: e.reciprocal(out=m8[:, 6:7], in_=m8[:, 6:7]),
                              [t_sm])
                t_wt = sch.op("vector", lambda e: e.tensor_scalar_mul(wtm, tp4[:, 0:E], m8[:, 6:7]),
                              [t_rc])
                t_wT = sch.op("tensor", lambda e: e.transpose(
                    out=ptf[0:E, 0:128], in_=wtm, identity=idf), [t_wt, t_lgte])
                t_wfm = sch.op("vector", lambda e: e.tensor_copy(
                    out=wrow, in_=ptf[0:E, 0:128]), [t_wT] + router_war[3:4])
                d_wfm = sch.dma("sync", wfmd[:, tt * 128:(tt + 1) * 128], wrow, deps=[t_wfm])
                router_war = [t_wfm, t_ex, t_wT, d_wfm]
                if debug and tt == 0:
                    x1b_war.append(sch.dma("gp", dbg["x3_t0"], x1b, deps=[t_x3]))
                    hs_war[spar].append(sch.dma("sync", dbg["hs_t0"], hs, deps=[t_scan]))
                    zb_war.append(sch.dma("sync", dbg["z_t0"], zb, deps=[t_z]))
                    router_war.append(sch.dma("sync", dbg["lg_t0"], lgt, deps=[t_lgte]))
                    router_war.append(sch.dma("sync", dbg["wtm_t0"], wtm, deps=[t_wt]))

            if debug:
                sch.dma("gp", dbg["wfm"], wfm, deps=router_war)

            aend = sch.frontier()

        NKH = NKI // 2
        with ExitStack() as fctx:
            fen = fctx.enter_context
            dwn = fen(nc.sbuf_tensor("dwn", [128, NKI * D], BF16)).ap()
            ostg = fen(nc.sbuf_tensor("ostg", [128, 2 * (D // 4)], F32)).ap()
            wbcf = fen(nc.sbuf_tensor("wbcf", [128, 2 * TH], BF16)).ap()
            sgb = fen(nc.sbuf_tensor("sgb", [128, 512], BF16)).ap()
            sub = fen(nc.sbuf_tensor("sub", [128, 512], BF16)).ap()
            actb = fen(nc.sbuf_tensor("actb", [128, NKH * TH], BF16)).ap()
            accb0 = fen(nc.sbuf_tensor("accb0", [128, (NKD // 2) * TH], BF16)).ap()
            accb1 = fen(nc.sbuf_tensor("accb1", [128, (NKD // 2) * TH], BF16)).ap()

            def acc_ap(dt, lo, hi):
                b = accb0 if dt < NKD // 2 else accb1
                return b[:, (dt % (NKD // 2)) * TH + lo:(dt % (NKD // 2)) * TH + hi]
            pg = [fen(nc.psum_tensor(f"pg{i}", [128, 512], F32)).ap() for i in range(NTB)]
            pu2 = [fen(nc.psum_tensor(f"pu{i}", [128, 512], F32)).ap() for i in range(NTB)]
            pd = [fen(nc.psum_tensor(f"pd{i}", [128, 512], F32)).ap() for i in range(2)]

            def dwn_load_ki(eidx, ki, war):
                return sch.dma("gp", dwn[:, ki * D:(ki + 1) * D], dpt[eidx, ki], deps=war)

            pd_war = {0: aend, 1: aend}
            pg_war = {i: aend for i in range(NTB)}
            pu_war = {i: aend for i in range(NTB)}
            sg_war, su_war = list(aend), list(aend)
            act_war = list(aend)
            acc_war = list(aend)
            ostg_war = {0: [], 1: []}
            d_dwn = [dwn_load_ki(0, ki, aend) for ki in range(NKI)]
            dwn_war = []
            dwn_pend = None
            last_act = None
            last_down = None
            seq = 0
            NSEQ = len(gu_seq)
            wbc_rd = {0: list(aend), 1: list(aend)}
            d_wbc_tok = {(0, 0): sch.dma("sync", wbcf[:, 0:TH],
                                         wfmd[0:1, 0:TH].to_broadcast([128, TH]),
                                         deps=list(aend))}

            for th in range(2):
                th0 = th * TH
                acc_tok = {}
                for e_i in range(E):
                    ne, nth = e_i + 1, th
                    if ne == E:
                        ne, nth = 0, th + 1
                    if nth < 2:
                        buf = ne % 2
                        d_wbc_tok[(nth, ne)] = sch.dma(
                            "sync", wbcf[:, buf * TH:(buf + 1) * TH],
                            wfmd[ne:ne + 1, nth * TH:nth * TH + TH].to_broadcast([128, TH]),
                            deps=wbc_rd[buf])
                        wbc_rd[buf] = []
                    d_wbc = d_wbc_tok[(th, e_i)]
                    wslot = (e_i % 2) * TH
                    for ih in range(2):
                        ic0 = ih * NKH
                        for kk in range(NKH):
                            ic = ic0 + kk
                            if dwn_pend is not None and e_i == dwn_pend:
                                d_dwn.append(dwn_load_ki(e_i, ic, dwn_war))
                            cur = gu_toks[seq]
                            curbuf = seq % 3
                            g_ap = gub[:, curbuf * CHW:(curbuf + 1) * CHW]
                            u_ap = gub[:, (3 + curbuf) * CHW:(3 + curbuf + 1) * CHW]

                            def gumm(e, g_ap=g_ap, u_ap=u_ap, th0=th0):
                                for k in range(NKD):
                                    gl = g_ap[:, k * 128:(k + 1) * 128]
                                    ul = u_ap[:, k * 128:(k + 1) * 128]
                                    for tb in range(NTB):
                                        rhs = x3T[:, k * T + th0 + tb * 512:k * T + th0 + tb * 512 + 512]
                                        e.matmul(out=pg[tb], lhsT=gl, rhs=rhs,
                                                 start=(k == 0), stop=(k == NKD - 1))
                                    for tb in range(NTB):
                                        rhs = x3T[:, k * T + th0 + tb * 512:k * T + th0 + tb * 512 + 512]
                                        mm = e.matmul(out=pu2[tb], lhsT=ul, rhs=rhs,
                                                      start=(k == 0), stop=(k == NKD - 1))
                                return mm

                            gu_deps = [cur[0], cur[1]]
                            for tb in range(NTB):
                                gu_deps += pg_war[tb] + pu_war[tb]
                            t_gu = sch.op("tensor", gumm, gu_deps)
                            gu_done[seq] = t_gu
                            if seq + 3 < NSEQ:
                                th_n, e_n, ic_n = gu_seq[seq + 3]
                                gu_toks[seq + 3] = gu_load(e_n, ic_n, (seq + 3) % 3,
                                                           [gu_done[seq]])
                            for tb in range(NTB):
                                pgx, pux = pg[tb], pu2[tb]
                                t_sg = sch.op("scalar", lambda e, pgx=pgx: e.activation(
                                    out=sgb, in_=pgx, func=AF.Silu), [t_gu] + sg_war)
                                t_su = sch.op("vector", lambda e, pux=pux, tb=tb, wslot=wslot: e.tensor_tensor(
                                    out=sub, in0=pux,
                                    in1=wbcf[:, wslot + tb * 512:wslot + (tb + 1) * 512],
                                    op=ALU.mult), [t_gu, d_wbc] + su_war)
                                wbc_rd[e_i % 2].append(t_su)
                                dst = actb[:, kk * TH + tb * 512:kk * TH + (tb + 1) * 512]
                                t_act = sch.op("vector", lambda e, dst=dst: e.tensor_mul(
                                    out=dst, in0=sgb, in1=sub), [t_sg, t_su] + act_war)
                                act_war = []
                                sg_war, su_war = [t_act], [t_act]
                                pg_war[tb] = [t_sg]
                                pu_war[tb] = [t_su]
                                last_act = t_act
                            seq += 1
                        for tb in range(NTB):
                            for dt in range(NKD):
                                pdx = pd[dt % 2]

                                def dmm(e, tb=tb, dt=dt, pdx=pdx, ic0=ic0):
                                    for kk in range(NKH):
                                        ki = ic0 + kk
                                        mm = e.matmul(
                                            out=pdx,
                                            lhsT=dwn[:, ki * D + dt * 128:ki * D + dt * 128 + 128],
                                            rhs=actb[:, kk * TH + tb * 512:kk * TH + (tb + 1) * 512],
                                            start=(kk == 0), stop=(kk == NKH - 1))
                                    return mm

                                t_d = sch.op("tensor", dmm, [last_act] + d_dwn + pd_war[dt % 2])
                                last_down = t_d
                                asl = acc_ap(dt, tb * 512, (tb + 1) * 512)
                                if e_i == 0 and ih == 0:
                                    t_a = sch.op("vector", lambda e, asl=asl, pdx=pdx: e.tensor_copy(
                                        out=asl, in_=pdx), [t_d] + acc_war)
                                else:
                                    t_a = sch.op("vector", lambda e, asl=asl, pdx=pdx: e.tensor_tensor(
                                        out=asl, in0=pdx, in1=asl, op=ALU.add),
                                        [t_d, acc_tok.get((dt, tb))] + acc_war)
                                acc_tok[(dt, tb)] = t_a
                                pd_war[dt % 2] = [t_a]
                        acc_war = []
                        act_war = [last_down]
                    dwn_war = [last_down]
                    dwn_pend = (e_i + 1) if e_i + 1 < E else (0 if th == 0 else None)
                    d_dwn = []
                wo_tr_last = None
                for tt in range(TH // 128):
                    trow = th0 + tt * 128
                    for dh in range(4):
                        bank = dh % 2
                        half = (tt * 4 + dh) % 2
                        etoks = []
                        for j in range(NKD // 4):
                            dt = dh * (NKD // 4) + j
                            src = acc_ap(dt, tt * 128, tt * 128 + 128)
                            dep = [acc_tok[(dt, (tt * 128) // 512)]]
                            if j == 0:
                                dep = dep + pt_war[bank]
                            t_t = sch.op("tensor", lambda e, src=src, bank=bank, j=j: e.transpose(
                                out=pt[bank][0:128, j * 128:(j + 1) * 128], in_=src,
                                identity=idb), dep)
                            etoks.append(t_t)
                            wo_tr_last = t_t
                        osl = ostg[:, half * 512:(half + 1) * 512]
                        t_e = sch.op("vector", lambda e, bank=bank, osl=osl: e.tensor_copy(
                            out=osl, in_=pt[bank][0:128, 0:512]),
                            etoks[-1:] + ostg_war[half])
                        pt_war[bank] = [t_e]
                        d_o = sch.dma("gp", y[trow:trow + 128, dh * (D // 4):(dh + 1) * (D // 4)],
                                      osl, deps=[t_e] + sch.qfrontier("sync_q"),
                                      accum_op=ALU.add)
                        ostg_war[half] = [d_o]
                acc_war = [wo_tr_last]

        blk = en(nc.Block())
        sch.build(blk)
    return nc


from concourse.bass_utils import run_bass_kernel_spmd

_D = 2048


def _pretile_gu(w, NKI, NKD=16):
    E = w.shape[0]
    return np.ascontiguousarray(
        w.reshape(E, NKD, 128, NKI, 128).transpose(0, 3, 2, 1, 4))


def _make_sel(E):
    s = np.zeros((E, E * 128), dtype=np.float32)
    for e in range(E):
        s[e, e * 128:(e + 1) * 128] = 1.0
    return s


def _make_xin(x_batch, p, t_own, t_prior):
    own = x_batch[p * t_own:(p + 1) * t_own]
    prior = x_batch[:p * t_own]
    pad = np.zeros((t_prior - prior.shape[0], _D), dtype=np.float32)
    return np.ascontiguousarray(np.concatenate([pad, prior, own], axis=0))


def kernel(L0_attn_x, L0_attn_log_A_diag, L0_attn_B_mat, L0_attn_C_mat,
           L0_attn_D_vec, L0_attn_ln_w, L0_attn_ln_b, L0_ffn_router_w,
           L0_ffn_expert_bias, L0_ffn_gate_proj, L0_ffn_up_proj,
           L0_ffn_down_proj):
    cfg = CFG(16, 1792, 2048, 6144)
    x = np.ascontiguousarray(np.asarray(L0_attn_x, dtype=np.float32))
    shared = {
        "logA": np.ascontiguousarray(np.asarray(L0_attn_log_A_diag, np.float32)),
        "Bm": np.ascontiguousarray(np.asarray(L0_attn_B_mat, np.float32)),
        "Cm": np.ascontiguousarray(np.asarray(L0_attn_C_mat, np.float32)),
        "Dv": np.ascontiguousarray(np.asarray(L0_attn_D_vec, np.float32)),
        "lnw": np.ascontiguousarray(np.asarray(L0_attn_ln_w, np.float32)),
        "lnb": np.ascontiguousarray(np.asarray(L0_attn_ln_b, np.float32)),
        "rw": np.ascontiguousarray(np.asarray(L0_ffn_router_w, np.float32)),
        "rb": np.ascontiguousarray(np.asarray(L0_ffn_expert_bias, np.float32)),
        "gpt": _pretile_gu(np.asarray(L0_ffn_gate_proj, np.float32), cfg.NKI),
        "upt": _pretile_gu(np.asarray(L0_ffn_up_proj, np.float32), cfg.NKI),
        "dpt": np.ascontiguousarray(
            np.asarray(L0_ffn_down_proj, np.float32).reshape(cfg.E, cfg.NKI, 128, _D)),
        "ident": np.eye(128, dtype=np.float32),
        "sel": _make_sel(cfg.E),
    }
    nc = build_nc(cfg)
    in_maps = []
    for core in range(8):
        b, p = core // 4, core % 4
        m = dict(shared)
        m["xin"] = _make_xin(x[b], p, cfg.T, cfg.TP)
        in_maps.append(m)
    res = run_bass_kernel_spmd(nc, in_maps, core_ids=list(range(8)))
    y = np.empty((2, 8192, _D), dtype=np.float32)
    for core in range(8):
        b, p = core // 4, core % 4
        y[b, p * cfg.T:(p + 1) * cfg.T] = res.results[core]["y"]
    return y


# version 05
# speedup vs baseline: 1.1454x; 1.0494x over previous
from contextlib import ExitStack

import numpy as np
import concourse.bass as bass
import concourse.mybir as mybir

F32 = mybir.dt.float32
BF16 = mybir.dt.bfloat16
AF = mybir.ActivationFunctionType
ALU = mybir.AluOpType
AX = mybir.AxisListType

COMPUTE = ("tensor", "vector", "scalar", "gpsimd")
DMAQ = ("sync_q", "gp_q", "sc_q")
NSLOT = 8
LN_EPS = 1e-5


class Sched:

    def __init__(self, nc, ctx):
        self.nc = nc
        names = list(COMPUTE)
        for q in DMAQ:
            names += [f"{q}{i}" for i in range(NSLOT)]
        self.sems = {n: ctx.enter_context(nc.semaphore("sem_" + n)) for n in names}
        self.count = {n: 0 for n in names}
        self.qidx = {q: 0 for q in DMAQ}
        self.qlast = {q: [None] * NSLOT for q in DMAQ}
        self.ops = []

    def _tok(self, stream, amount):
        self.count[stream] += amount
        return (stream, self.count[stream])

    def op(self, eng, fn, deps=()):
        tok = self._tok(eng, 1)
        self.ops.append((eng, eng, fn, tuple(d for d in deps if d), 1))
        return tok

    def dma(self, queue, out, in_, deps=(), **kw):
        q = {"sync": "sync_q", "gp": "gp_q", "sc": "sc_q"}[queue]
        body = {"sync": "sync", "gp": "gpsimd", "sc": "scalar"}[queue]
        slot = self.qidx[q] % NSLOT
        self.qidx[q] += 1
        stream = f"{q}{slot}"
        deps = list(d for d in deps if d)
        prev = self.qlast[q][slot]
        if prev is not None:
            deps.append(prev)
        tok = self._tok(stream, 16)
        self.qlast[q][slot] = tok

        def fn(e):
            return e.dma_start(out=out, in_=in_, **kw)

        self.ops.append((body, stream, fn, tuple(deps), 16))
        return tok

    def frontier(self):
        return [(s, v) for s, v in self.count.items() if v > 0]

    def qfrontier(self, q):
        return [t for t in self.qlast[q] if t is not None]

    def build(self, blk):
        bodies = {}
        for body, stream, fn, deps, inc in self.ops:
            bodies.setdefault(body, []).append((stream, fn, deps, inc))
        if "sync" not in bodies:
            bodies["sync"] = []
        final_counts = dict(self.count)

        def make(body_name, items):
            def emit(eng):
                hw = {}
                for stream, fn, deps, inc in items:
                    for (s, v) in deps:
                        if hw.get(s, 0) < v:
                            eng.wait_ge(self.sems[s], v)
                            hw[s] = v
                    fn(eng).then_inc(self.sems[stream], inc)
                if body_name == "sync":
                    for q in self.count:
                        if final_counts[q] > hw.get(q, 0):
                            eng.wait_ge(self.sems[q], final_counts[q])
            return emit

        for body_name, items in bodies.items():
            getattr(blk, body_name)(make(body_name, items))


class CFG:
    def __init__(self, n_exp=16, inter=1792, t_own=2048, t_prior=6144):
        self.D = 2048
        self.NKD = 16
        self.E = n_exp
        self.I = inter
        self.NKI = inter // 128
        self.T = t_own
        self.TP = t_prior
        self.NT = t_own // 128
        self.NTP = t_prior // 128
        self.TH = t_own // 2
        self.NTB = self.TH // 512
        assert t_own % 1024 == 0 and self.TH % 512 == 0 and inter % 128 == 0


def build_nc(cfg: CFG, debug=False):
    nc = bass.Bass("TRN2", target_bir_lowering=False)
    D, E, I, T, TP = cfg.D, cfg.E, cfg.I, cfg.T, cfg.TP
    NKD, NKI, NT, NTP, TH, NTB = cfg.NKD, cfg.NKI, cfg.NT, cfg.NTP, cfg.TH, cfg.NTB

    xin = nc.dram_tensor("xin", [TP + T, D], F32, kind="ExternalInput").ap()
    logA = nc.dram_tensor("logA", [128], F32, kind="ExternalInput").ap()
    Bm = nc.dram_tensor("Bm", [128, D], F32, kind="ExternalInput").ap()
    Cm = nc.dram_tensor("Cm", [D, 128], F32, kind="ExternalInput").ap()
    Dv = nc.dram_tensor("Dv", [D], F32, kind="ExternalInput").ap()
    lnw = nc.dram_tensor("lnw", [D], F32, kind="ExternalInput").ap()
    lnb = nc.dram_tensor("lnb", [D], F32, kind="ExternalInput").ap()
    rw = nc.dram_tensor("rw", [E, D], F32, kind="ExternalInput").ap()
    rb = nc.dram_tensor("rb", [E], F32, kind="ExternalInput").ap()
    gpt = nc.dram_tensor("gpt", [E, NKI, 128, NKD, 128], F32, kind="ExternalInput").ap()
    upt = nc.dram_tensor("upt", [E, NKI, 128, NKD, 128], F32, kind="ExternalInput").ap()
    dpt = nc.dram_tensor("dpt", [E, NKI, 128, D], F32, kind="ExternalInput").ap()
    ident_in = nc.dram_tensor("ident", [128, 128], F32, kind="ExternalInput").ap()
    sel_in = nc.dram_tensor("sel", [E, E * 128], F32, kind="ExternalInput").ap()
    y = nc.dram_tensor("y", [T, D], F32, kind="ExternalOutput").ap()
    wfmd = nc.dram_tensor("wfmd", [E, T], BF16, kind="Internal").ap()
    dbg = {}
    if debug:
        dbg["x1_t0"] = nc.dram_tensor("dbg_x1_t0", [128, D], F32, kind="ExternalOutput").ap()
        dbg["hs_t0"] = nc.dram_tensor("dbg_hs_t0", [128, 128], F32, kind="ExternalOutput").ap()
        dbg["z_t0"] = nc.dram_tensor("dbg_z_t0", [128, D], F32, kind="ExternalOutput").ap()
        dbg["x3_t0"] = nc.dram_tensor("dbg_x3_t0", [128, D], F32, kind="ExternalOutput").ap()
        dbg["lg_t0"] = nc.dram_tensor("dbg_lg_t0", [128, E], F32, kind="ExternalOutput").ap()
        dbg["wtm_t0"] = nc.dram_tensor("dbg_wtm_t0", [128, E], F32, kind="ExternalOutput").ap()
        dbg["acc_th0"] = nc.dram_tensor("dbg_acc_th0", [128, NKD * (T // 2)], F32, kind="ExternalOutput").ap()
        dbg["act_e0"] = nc.dram_tensor("dbg_act_e0", [128, NKI * (T // 2)], F32, kind="ExternalOutput").ap()
        dbg["wbc_e0"] = nc.dram_tensor("dbg_wbc_e0", [128, T // 2], F32, kind="ExternalOutput").ap()
        dbg["wfm"] = nc.dram_tensor("dbg_wfm", [E, T], F32, kind="ExternalOutput").ap()
        dbg["selb"] = nc.dram_tensor("dbg_selb", [E, 128], F32, kind="ExternalOutput").ap()
        dbg["pbc"] = nc.dram_tensor("dbg_pbc", [128, 512], F32, kind="ExternalOutput").ap()

    with ExitStack() as ctx:
        en = ctx.enter_context
        sch = Sched(nc, ctx)

        x3T = en(nc.sbuf_tensor("x3T", [128, NKD * T], BF16)).ap()
        gub = en(nc.sbuf_tensor("gub", [128, 6 * NKD * 128], BF16)).ap()
        idf = en(nc.sbuf_tensor("idf", [128, 128], F32)).ap()
        idb = en(nc.sbuf_tensor("idb", [128, 128], BF16)).ap()
        rlhs = en(nc.sbuf_tensor("rlhs", [128, NKD * E], BF16)).ap()
        rbias = en(nc.sbuf_tensor("rbias", [E, 1], F32)).ap()
        pt = [en(nc.psum_tensor("pt0", [128, 512], BF16)).ap(),
              en(nc.psum_tensor("pt1", [128, 512], BF16)).ap()]
        pt_war = {0: [], 1: []}

        CHW = NKD * 128

        def pe_transpose(src_ap, dst_ap, k, deps, fp32=False):
            P, F = src_ap.shape
            bank = k % 2
            idap = (idf if fp32 else idb)[0:P, 0:P]
            pb = pt[bank].bitcast(F32) if fp32 else pt[bank]
            pdst = pb[0:F, 0:P]
            t_t = sch.op("tensor", lambda e: e.transpose(out=pdst, in_=src_ap, identity=idap),
                         list(deps) + pt_war[bank])
            t_e = sch.op("vector", lambda e: e.tensor_copy(out=dst_ap, in_=pdst), [t_t])
            pt_war[bank] = [t_e]
            return t_e

        d_id = sch.dma("sync", idf, ident_in)
        t_idb = sch.op("vector", lambda e: e.tensor_copy(out=idb, in_=idf), [d_id])

        def gu_load(eidx, ic, buf, war):
            tg = sch.dma("gp", gub[:, buf * CHW:(buf + 1) * CHW],
                         gpt[eidx, ic].rearrange("p k i -> p (k i)"), deps=war)
            tu = sch.dma("gp", gub[:, (3 + buf) * CHW:(3 + buf + 1) * CHW],
                         upt[eidx, ic].rearrange("p k i -> p (k i)"), deps=war)
            return (tg, tu)

        gu_seq = [(th, e_i, ic) for th in range(2) for e_i in range(E)
                  for ic in range(NKI)]

        with ExitStack() as actx:
            aen = actx.enter_context
            Blhs = aen(nc.sbuf_tensor("Blhs", [128, NKD * 128], BF16)).ap()
            CT = aen(nc.sbuf_tensor("CT", [128, D], BF16)).ap()
            dbc = aen(nc.sbuf_tensor("dbc", [128, D], BF16)).ap()
            wbcst = aen(nc.sbuf_tensor("wbcst", [128, D], BF16)).ap()
            bbcst = aen(nc.sbuf_tensor("bbcst", [128, D], F32)).ap()
            abc = aen(nc.sbuf_tensor("abc", [128, 128], F32)).ap()
            adv = aen(nc.sbuf_tensor("adv", [128, 4], F32)).ap()
            epsv = aen(nc.sbuf_tensor("epsv", [128, 1], F32)).ap()
            pua = aen(nc.psum_tensor("pua", [128, 512], F32)).ap()
            py = [aen(nc.psum_tensor("py0", [128, 512], F32)).ap(),
                  aen(nc.psum_tensor("py1", [128, 512], F32)).ap()]
            ppy = py[0]

            with ExitStack() as sctx:
                sen = sctx.enter_context
                rowb = sen(nc.sbuf_tensor("rowb", [1, D], F32)).ap()
                ones1 = sen(nc.sbuf_tensor("ones1", [1, 128], F32)).ap()
                stg = sen(nc.sbuf_tensor("stg", [128, 2048], F32)).ap()
                stgb = sen(nc.sbuf_tensor("stgb", [128, 2048], BF16)).ap()

                t_eps = sch.op("vector", lambda e: e.memset(epsv, LN_EPS))
                d_logA = sch.dma("sync", adv[:, 0:1], logA.rearrange("(p one) -> p one", one=1))
                t_rate = sch.op("scalar", lambda e: e.activation(out=adv[:, 1:2], in_=adv[:, 0:1], func=AF.Exp), [d_logA])
                t_nr = sch.op("vector", lambda e: e.tensor_scalar_mul(adv[:, 2:3], adv[:, 1:2], -1.0), [t_rate])
                t_ad = sch.op("scalar", lambda e: e.activation(out=adv[:, 3:4], in_=adv[:, 2:3], func=AF.Exp), [t_nr])
                t_ab0 = sch.op("vector", lambda e: e.memset(abc, 1.0))
                t_ab = sch.op("vector", lambda e: e.tensor_scalar_mul(abc, abc, adv[:, 3:4]), [t_ad, t_ab0])

                t_ones = sch.op("vector", lambda e: e.memset(ones1, 1.0))
                ppy_war = []
                rowb_war = []

                def row_bcast(dst_bf, src_row, add1):
                    nonlocal ppy_war, rowb_war
                    d_r = sch.dma("sync", rowb, src_row.rearrange("(one d) -> one d", one=1),
                                  deps=rowb_war)
                    tlast = None
                    last_mm = None
                    for c in range(D // 512):
                        sl = slice(c * 512, (c + 1) * 512)
                        t_mm = sch.op("tensor", lambda e, sl=sl: e.matmul(
                            out=ppy[:, 0:512], lhsT=ones1, rhs=rowb[0:1, sl],
                            start=True, stop=True), [d_r, t_ones] + ppy_war)
                        last_mm = t_mm
                        if add1:
                            t_ev = sch.op("vector", lambda e, sl=sl: e.tensor_scalar_add(
                                dst_bf[:, sl], ppy[:, 0:512], 1.0), [t_mm])
                        else:
                            t_ev = sch.op("vector", lambda e, sl=sl: e.tensor_copy(
                                out=dst_bf[:, sl], in_=ppy[:, 0:512]), [t_mm])
                        ppy_war = [t_ev]
                        tlast = t_ev
                    rowb_war = [last_mm]
                    return tlast

                t_dbc = row_bcast(dbc, Dv, add1=True)
                t_wb = row_bcast(wbcst, lnw, add1=False)
                t_bb = row_bcast(bbcst, lnb, add1=False)

                d_B = sch.dma("sync", stg, Bm)
                t_Bc = sch.op("vector", lambda e: e.tensor_copy(out=stgb, in_=stg), [d_B, t_idb])
                bl_toks = [pe_transpose(stgb[:, k * 128:(k + 1) * 128], Blhs[:, k * 128:(k + 1) * 128],
                                        k, [t_Bc]) for k in range(NKD)]
                t_Blhs = bl_toks[-1]
                d_C = sch.dma("sync", stg.rearrange("p (k n) -> p k n", k=NKD),
                              Cm.rearrange("(k p) n -> p k n", p=128), deps=[t_Bc])
                t_Cc = sch.op("vector", lambda e: e.tensor_copy(out=stgb, in_=stg), [d_C, t_Blhs])
                ct_toks = [pe_transpose(stgb[:, k * 128:(k + 1) * 128], CT[:, k * 128:(k + 1) * 128],
                                        k, [t_Cc]) for k in range(NKD)]
                t_CT = ct_toks[-1]
                d_rw = sch.dma("sync", stg[0:E, :], rw, deps=[t_Cc])
                t_rwc = sch.op("vector", lambda e: e.tensor_copy(out=stgb[0:E, :], in_=stg[0:E, :]),
                               [d_rw, t_CT])
                rl_toks = [pe_transpose(stgb[0:E, k * 128:(k + 1) * 128], rlhs[:, k * E:(k + 1) * E],
                                        k, [t_rwc]) for k in range(NKD)]
                t_rlhs = rl_toks[-1]
                d_rb = sch.dma("sync", rbias, rb.rearrange("(p one) -> p one", one=1))
                setup_scratch_end = [t_rlhs]

            NXB = 4
            xbuf = aen(nc.sbuf_tensor("xbuf", [128, NXB * D], F32)).ap()
            x1b = aen(nc.sbuf_tensor("x1b", [128, 2 * D], BF16)).ap()
            x1T = aen(nc.sbuf_tensor("x1T", [128, 2 * NKD * 128], BF16)).ap()
            r1b = aen(nc.sbuf_tensor("r1b", [128, 2 * D], F32)).ap()
            zb = aen(nc.sbuf_tensor("zb", [128, 2 * D], F32)).ap()
            hsb = aen(nc.sbuf_tensor("hsb", [128, 2 * 128], F32)).ap()
            hbfb = aen(nc.sbuf_tensor("hbfb", [128, 2 * 128], BF16)).ap()
            bnb = aen(nc.sbuf_tensor("bnb", [128, 2 * 24], F32)).ap()
            stat = aen(nc.sbuf_tensor("stat", [128, 2 * 4], F32)).ap()
            lgf = aen(nc.sbuf_tensor("lgf", [E, 128], F32)).ap()
            lgt = aen(nc.sbuf_tensor("lgt", [128, E], F32)).ap()
            m8 = aen(nc.sbuf_tensor("m8", [128, 8], F32)).ap()
            tp4 = aen(nc.sbuf_tensor("tp4", [128, 2 * E], F32)).ap()
            wtm = aen(nc.sbuf_tensor("wtm", [128, E], F32)).ap()
            wrow = aen(nc.sbuf_tensor("wrow", [E, 128], BF16)).ap()
            x3v = x3T.rearrange("p (k t) -> p k t", k=NKD)

            arena2_war = setup_scratch_end

            gu_toks = {}
            gu_done = {}
            for s in range(3):
                th_s, e_s, ic_s = gu_seq[s]
                gu_toks[s] = gu_load(e_s, ic_s, s % 3, [])

            NGT = NTP + NT
            x_rd = {i: list(arena2_war) for i in range(NXB)}
            x1_rd = {0: list(arena2_war), 1: list(arena2_war)}
            x1T_rd = {0: list(arena2_war), 1: list(arena2_war)}
            r1_rd = {0: list(arena2_war), 1: list(arena2_war)}
            zb_rd = {0: list(arena2_war), 1: list(arena2_war)}
            st_war = {0: [], 1: []}
            st_rd = {0: [], 1: []}
            hs_rd = {0: [], 1: []}
            hbf_rd = {0: [], 1: []}
            pua_war = {0: [], 1: []}
            py_war = {0: list(ppy_war), 1: []}
            lg_war = []
            router_war = []
            carry = {"tok": None, "valid": False}
            d_x_tok = {}

            def x_load(gt):
                sl = gt % NXB
                xs = xbuf[:, sl * D:(sl + 1) * D]
                d0 = sch.dma("sync", xs[:, 0:D // 2],
                             xin[gt * 128:(gt + 1) * 128, 0:D // 2], deps=x_rd[sl])
                d1 = sch.dma("gp", xs[:, D // 2:D],
                             xin[gt * 128:(gt + 1) * 128, D // 2:D], deps=x_rd[sl])
                x_rd[sl] = []
                d_x_tok[gt] = [d0, d1]

            for gt in range(min(NXB, NGT)):
                x_load(gt)

            def ln_stats(src_ap, par, deps=(), chunk_deps=None):
                bn = bnb[:, par * 24:(par + 1) * 24]
                st = stat[:, par * 4:(par + 1) * 4]
                tks = []
                for c in range(4):
                    dd = list(deps) + (chunk_deps[c] if chunk_deps else [])
                    tks.append(sch.op("vector", lambda e, c=c, bn=bn, src_ap=src_ap: e.bn_stats(
                        out=bn[:, 6 * c:6 * (c + 1)],
                        in_=src_ap[:, 512 * c:512 * (c + 1)]), dd))
                t_ag = sch.op("vector", lambda e, bn=bn, st=st: e.bn_aggr(
                    out=st[:, 0:2], in_=bn), tks + st_war[par] + st_rd[par])
                t_ln = sch.op("scalar", lambda e, st=st: e.activation(
                    out=st[:, 2:3], in_=st[:, 1:2], func=AF.Ln, bias=epsv), [t_ag, t_eps])
                t_rs = sch.op("scalar", lambda e, st=st: e.activation(
                    out=st[:, 2:3], in_=st[:, 2:3], func=AF.Exp, scale=-0.5), [t_ln])
                t_nb = sch.op("vector", lambda e, st=st: e.tensor_scalar(
                    out=st[:, 3:4], in0=st[:, 0:1], scalar1=st[:, 2:3], scalar2=-1.0,
                    op0=ALU.mult, op1=ALU.mult), [t_rs])
                st_war[par] = [t_ln]
                st_rd[par] = []
                return t_nb, st

            def tr16(src2048, dst_evac, deps):
                evs = []
                for b in range(4):
                    bank = b % 2
                    tks = []
                    for j in range(4):
                        k = 4 * b + j
                        dep = list(deps) + (pt_war[bank] if j == 0 else [])
                        tks.append(sch.op("tensor", lambda e, k=k, bank=bank, j=j: e.transpose(
                            out=pt[bank][0:128, j * 128:(j + 1) * 128],
                            in_=src2048[:, k * 128:(k + 1) * 128], identity=idb), dep))
                    t_ev = dst_evac(b, pt[bank][0:128, 0:512], [tks[-1]])
                    pt_war[bank] = [t_ev]
                    evs.append(t_ev)
                return evs

            for gt in range(NGT):
                own = gt >= NTP
                tt = gt - NTP
                par = gt % 2
                sl = gt % NXB
                xs = xbuf[:, sl * D:(sl + 1) * D]
                d_x = d_x_tok[gt]
                if gt >= 1 and gt - 1 + NXB < NGT:
                    x_load(gt - 1 + NXB)
                x1s = x1b[:, par * D:(par + 1) * D]
                x1Ts = x1T[:, par * NKD * 128:(par + 1) * NKD * 128]
                t_nb, st = ln_stats(xs, par, deps=d_x)
                t_x1 = sch.op("scalar", lambda e, xs=xs, x1s=x1s, st=st: e.activation(
                    out=x1s, in_=xs, func=AF.Identity, bias=st[:, 3:4], scale=st[:, 2:3]),
                    [t_nb] + x1_rd[par])
                x1_rd[par] = []
                st_rd[par].append(t_x1)
                x_rd[sl].append(t_x1)

                def x1t_evac(b, pt_ap, dep, par=par, x1Ts=x1Ts):
                    eng = "scalar" if b % 2 == 0 else "vector"
                    dd = dep + (x1T_rd[par] if b < 2 else [])
                    if eng == "scalar":
                        return sch.op(eng, lambda e, b=b, pt_ap=pt_ap: e.activation(
                            out=x1Ts[:, b * 512:(b + 1) * 512], in_=pt_ap,
                            func=AF.Identity), dd)
                    return sch.op(eng, lambda e, b=b, pt_ap=pt_ap: e.tensor_copy(
                        out=x1Ts[:, b * 512:(b + 1) * 512], in_=pt_ap), dd)

                tr_evs = tr16(x1s, x1t_evac, [t_x1])
                x1T_rd[par] = []
                x1_rd[par].append(tr_evs[-1])
                x1_rd[par].append(tr_evs[-2])

                uap = pua[:, par * 128:(par + 1) * 128]

                def u_mm(e, x1Ts=x1Ts, uap=uap):
                    for k in range(NKD):
                        mm = e.matmul(out=uap, lhsT=Blhs[:, k * 128:(k + 1) * 128],
                                      rhs=x1Ts[:, k * 128:(k + 1) * 128],
                                      start=(k == 0), stop=(k == NKD - 1))
                    return mm
                t_umm = sch.op("tensor", u_mm, tr_evs + [t_Blhs] + pua_war[par])
                x1T_rd[par].append(t_umm)
                hs = hsb[:, par * 128:(par + 1) * 128]
                init = hsb[:, (1 - par) * 128 + 127:(1 - par) * 128 + 128] if carry["valid"] else 0.0
                t_scan = sch.op("vector", lambda e, hs=hs, init=init, uap=uap: e.tensor_tensor_scan(
                    out=hs, data0=abc, data1=uap, initial=init,
                    op0=ALU.mult, op1=ALU.add),
                    [t_umm, t_ab] + hs_rd[par] + ([carry["tok"]] if carry["valid"] else []))
                hs_rd[par] = []
                pua_war[par] = [t_scan]
                carry.update(tok=t_scan, valid=True)
                if not own:
                    continue

                hbs = hbfb[:, par * 128:(par + 1) * 128]
                t_hc = sch.op("vector", lambda e, hs=hs, hbs=hbs: e.tensor_copy(
                    out=hbs, in_=hs), [t_scan] + hbf_rd[par])
                hbf_rd[par] = []
                hs_rd[par].append(t_hc)
                zs = zb[:, par * D:(par + 1) * D]
                t_m1 = sch.op("vector", lambda e, zs=zs, x1s=x1s: e.tensor_tensor(
                    out=zs, in0=x1s, in1=dbc, op=ALU.mult), [t_x1, t_dbc] + zb_rd[par])
                zb_rd[par] = []
                x1_rd[par].append(t_m1)

                zc_toks = []
                for c in range(4):
                    pyb = py[c % 2]
                    t_yc = sch.op("tensor", lambda e, pyb=pyb, c=c, hbs=hbs: e.matmul(
                        out=pyb, lhsT=hbs, rhs=CT[:, c * 512:(c + 1) * 512],
                        start=True, stop=True), [t_hc, t_CT] + py_war[c % 2])
                    t_zc = sch.op("vector", lambda e, pyb=pyb, c=c, zs=zs: e.tensor_add(
                        out=zs[:, c * 512:(c + 1) * 512], in0=zs[:, c * 512:(c + 1) * 512],
                        in1=pyb), [t_yc, t_m1])
                    py_war[c % 2] = [t_zc]
                    zc_toks.append(t_zc)
                hbf_rd[par].append(t_yc)
                t_nb2, st2 = ln_stats(zs, par, chunk_deps=[[t] for t in zc_toks])
                t_nrm = sch.op("scalar", lambda e, zs=zs, x1s=x1s, st2=st2: e.activation(
                    out=x1s, in_=zs, func=AF.Identity, bias=st2[:, 3:4], scale=st2[:, 2:3]),
                    [t_nb2] + x1_rd[par])
                x1_rd[par] = []
                st_rd[par].append(t_nrm)
                zb_rd[par].append(t_nrm)
                r1 = r1b[:, par * D:(par + 1) * D]
                t_t2 = sch.op("vector", lambda e, r1=r1, x1s=x1s: e.tensor_tensor(
                    out=r1, in0=x1s, in1=wbcst, op=ALU.mult), [t_nrm, t_wb] + r1_rd[par])
                r1_rd[par] = []
                x1_rd[par].append(t_t2)
                t_t3 = sch.op("gpsimd", lambda e, r1=r1: e.tensor_add(
                    out=r1, in0=r1, in1=bbcst), [t_t2, t_bb])
                t_r1 = sch.op("gpsimd", lambda e, r1=r1, xs=xs: e.tensor_add(
                    out=r1, in0=r1, in1=xs), [t_t3])
                x_rd[sl].append(t_r1)
                d_r1 = sch.dma("sync", y[tt * 128:(tt + 1) * 128, :], r1, deps=[t_r1])
                r1_rd[par].append(d_r1)
                t_nb3, st3 = ln_stats(r1, par, deps=[t_r1])
                t_x3 = sch.op("scalar", lambda e, r1=r1, x1s=x1s, st3=st3: e.activation(
                    out=x1s, in_=r1, func=AF.Identity, bias=st3[:, 3:4], scale=st3[:, 2:3]),
                    [t_nb3] + x1_rd[par])
                x1_rd[par] = []
                st_rd[par].append(t_x3)
                r1_rd[par].append(t_x3)

                def x3_evac(b, pt_ap, dep, tt=tt):
                    eng = "scalar" if b % 2 == 0 else "vector"
                    dst = x3v[:, b * 4:(b + 1) * 4, tt * 128:(tt + 1) * 128]
                    srcv = pt_ap.rearrange("p (j t) -> p j t", j=4)
                    if eng == "scalar":
                        return sch.op(eng, lambda e, dst=dst, srcv=srcv: e.activation(
                            out=dst, in_=srcv, func=AF.Identity), dep)
                    return sch.op(eng, lambda e, dst=dst, srcv=srcv: e.tensor_copy(
                        out=dst, in_=srcv), dep)

                tr3_evs = tr16(x1s, x3_evac, [t_x3])
                x1_rd[par] += [tr3_evs[-1], tr3_evs[-2]]

                def lg_mm(e, tt=tt):
                    for k in range(NKD):
                        mm = e.matmul(out=pua[0:E, 384:384 + 128], lhsT=rlhs[:, k * E:(k + 1) * E],
                                      rhs=x3T[:, k * T + tt * 128:k * T + tt * 128 + 128],
                                      start=(k == 0), stop=(k == NKD - 1))
                    return mm
                t_lg = sch.op("tensor", lg_mm, tr3_evs + [t_rlhs] + lg_war + router_war)
                t_lge = sch.op("vector", lambda e: e.tensor_scalar(
                    out=lgf, in0=pua[0:E, 384:384 + 128], scalar1=rbias, scalar2=None,
                    op0=ALU.add), [t_lg, d_rb])
                lg_war = [t_lge]
                t_lgte = pe_transpose(lgf, lgt, 0, [t_lge] + router_war, fp32=True)
                t_m8 = sch.op("vector", lambda e: e.max(out=m8, in_=lgt), [t_lgte])
                t_nmx = sch.op("vector", lambda e: e.tensor_scalar_mul(m8[:, 7:8], m8[:, 0:1], -1.0),
                               [t_m8])
                t_ex = sch.op("scalar", lambda e: e.activation(
                    out=tp4[:, 0:E], in_=lgt, func=AF.Exp, bias=m8[:, 7:8]), [t_nmx])
                t_mk = sch.op("vector", lambda e: e.tensor_scalar(
                    out=tp4[:, E:2 * E], in0=lgt, scalar1=m8[:, 3:4], scalar2=None,
                    op0=ALU.is_ge), [t_m8, t_ex])
                t_we = sch.op("vector", lambda e: e.tensor_mul(
                    out=tp4[:, 0:E], in0=tp4[:, 0:E], in1=tp4[:, E:2 * E]), [t_ex, t_mk])
                t_sm = sch.op("vector", lambda e: e.reduce_sum(out=m8[:, 6:7], in_=tp4[:, 0:E],
                                                               axis=AX.X), [t_we])
                t_rc = sch.op("vector", lambda e: e.reciprocal(out=m8[:, 6:7], in_=m8[:, 6:7]),
                              [t_sm])
                t_wt = sch.op("vector", lambda e: e.tensor_scalar_mul(wtm, tp4[:, 0:E], m8[:, 6:7]),
                              [t_rc])
                t_wT = pe_transpose(wtm, wrow, 1, [t_wt, t_lgte] + router_war[2:3],
                                    fp32=True)
                d_wfm = sch.dma("sync", wfmd[:, tt * 128:(tt + 1) * 128], wrow, deps=[t_wT])
                router_war = [t_ex, t_wT, d_wfm]

            aend = sch.frontier()

        NKH = NKI // 2
        with ExitStack() as fctx:
            fen = fctx.enter_context
            dwn = fen(nc.sbuf_tensor("dwn", [128, NKI * D], BF16)).ap()
            ostg = fen(nc.sbuf_tensor("ostg", [128, 2 * (D // 4)], F32)).ap()
            wbcf = fen(nc.sbuf_tensor("wbcf", [128, 2 * TH], BF16)).ap()
            sgb = fen(nc.sbuf_tensor("sgb", [128, 512], BF16)).ap()
            sub = fen(nc.sbuf_tensor("sub", [128, 512], BF16)).ap()
            actb = fen(nc.sbuf_tensor("actb", [128, NKH * TH], BF16)).ap()
            accb0 = fen(nc.sbuf_tensor("accb0", [128, (NKD // 2) * TH], BF16)).ap()
            accb1 = fen(nc.sbuf_tensor("accb1", [128, (NKD // 2) * TH], BF16)).ap()

            def acc_ap(dt, lo, hi):
                b = accb0 if dt < NKD // 2 else accb1
                return b[:, (dt % (NKD // 2)) * TH + lo:(dt % (NKD // 2)) * TH + hi]
            pg = [fen(nc.psum_tensor(f"pg{i}", [128, 512], F32)).ap() for i in range(NTB)]
            pu2 = [fen(nc.psum_tensor(f"pu{i}", [128, 512], F32)).ap() for i in range(NTB)]
            pd = [fen(nc.psum_tensor(f"pd{i}", [128, 512], F32)).ap() for i in range(2)]

            def dwn_load_ki(eidx, ki, war):
                return sch.dma("gp", dwn[:, ki * D:(ki + 1) * D], dpt[eidx, ki], deps=war)

            pd_war = {0: aend, 1: aend}
            pg_war = {i: aend for i in range(NTB)}
            pu_war = {i: aend for i in range(NTB)}
            sg_war, su_war = list(aend), list(aend)
            act_war = list(aend)
            acc_war = list(aend)
            ostg_war = {0: [], 1: []}
            d_dwn = [dwn_load_ki(0, ki, aend) for ki in range(NKI)]
            dwn_war = []
            dwn_pend = None
            last_act = None
            last_down = None
            seq = 0
            NSEQ = len(gu_seq)
            wbc_rd = {0: list(aend), 1: list(aend)}
            d_wbc_tok = {(0, 0): sch.dma("sync", wbcf[:, 0:TH],
                                         wfmd[0:1, 0:TH].to_broadcast([128, TH]),
                                         deps=list(aend))}

            for th in range(2):
                th0 = th * TH
                acc_tok = {}
                for e_i in range(E):
                    ne, nth = e_i + 1, th
                    if ne == E:
                        ne, nth = 0, th + 1
                    if nth < 2:
                        buf = ne % 2
                        d_wbc_tok[(nth, ne)] = sch.dma(
                            "sync", wbcf[:, buf * TH:(buf + 1) * TH],
                            wfmd[ne:ne + 1, nth * TH:nth * TH + TH].to_broadcast([128, TH]),
                            deps=wbc_rd[buf])
                        wbc_rd[buf] = []
                    d_wbc = d_wbc_tok[(th, e_i)]
                    wslot = (e_i % 2) * TH
                    for ih in range(2):
                        ic0 = ih * NKH
                        for kk in range(NKH):
                            ic = ic0 + kk
                            if dwn_pend is not None and e_i == dwn_pend:
                                d_dwn.append(dwn_load_ki(e_i, ic, dwn_war))
                            cur = gu_toks[seq]
                            curbuf = seq % 3
                            g_ap = gub[:, curbuf * CHW:(curbuf + 1) * CHW]
                            u_ap = gub[:, (3 + curbuf) * CHW:(3 + curbuf + 1) * CHW]

                            def gumm(e, g_ap=g_ap, u_ap=u_ap, th0=th0):
                                for k in range(NKD):
                                    gl = g_ap[:, k * 128:(k + 1) * 128]
                                    ul = u_ap[:, k * 128:(k + 1) * 128]
                                    for tb in range(NTB):
                                        rhs = x3T[:, k * T + th0 + tb * 512:k * T + th0 + tb * 512 + 512]
                                        e.matmul(out=pg[tb], lhsT=gl, rhs=rhs,
                                                 start=(k == 0), stop=(k == NKD - 1))
                                    for tb in range(NTB):
                                        rhs = x3T[:, k * T + th0 + tb * 512:k * T + th0 + tb * 512 + 512]
                                        mm = e.matmul(out=pu2[tb], lhsT=ul, rhs=rhs,
                                                      start=(k == 0), stop=(k == NKD - 1))
                                return mm

                            gu_deps = [cur[0], cur[1]]
                            for tb in range(NTB):
                                gu_deps += pg_war[tb] + pu_war[tb]
                            t_gu = sch.op("tensor", gumm, gu_deps)
                            gu_done[seq] = t_gu
                            if seq + 3 < NSEQ:
                                th_n, e_n, ic_n = gu_seq[seq + 3]
                                gu_toks[seq + 3] = gu_load(e_n, ic_n, (seq + 3) % 3,
                                                           [gu_done[seq]])
                            for tb in range(NTB):
                                pgx, pux = pg[tb], pu2[tb]
                                t_sg = sch.op("scalar", lambda e, pgx=pgx: e.activation(
                                    out=sgb, in_=pgx, func=AF.Silu), [t_gu] + sg_war)
                                t_su = sch.op("vector", lambda e, pux=pux, tb=tb, wslot=wslot: e.tensor_tensor(
                                    out=sub, in0=pux,
                                    in1=wbcf[:, wslot + tb * 512:wslot + (tb + 1) * 512],
                                    op=ALU.mult), [t_gu, d_wbc] + su_war)
                                wbc_rd[e_i % 2].append(t_su)
                                dst = actb[:, kk * TH + tb * 512:kk * TH + (tb + 1) * 512]
                                t_act = sch.op("vector", lambda e, dst=dst: e.tensor_mul(
                                    out=dst, in0=sgb, in1=sub), [t_sg, t_su] + act_war)
                                act_war = []
                                sg_war, su_war = [t_act], [t_act]
                                pg_war[tb] = [t_sg]
                                pu_war[tb] = [t_su]
                                last_act = t_act
                            seq += 1
                        for tb in range(NTB):
                            for dt in range(NKD):
                                pdx = pd[dt % 2]

                                def dmm(e, tb=tb, dt=dt, pdx=pdx, ic0=ic0):
                                    for kk in range(NKH):
                                        ki = ic0 + kk
                                        mm = e.matmul(
                                            out=pdx,
                                            lhsT=dwn[:, ki * D + dt * 128:ki * D + dt * 128 + 128],
                                            rhs=actb[:, kk * TH + tb * 512:kk * TH + (tb + 1) * 512],
                                            start=(kk == 0), stop=(kk == NKH - 1))
                                    return mm

                                t_d = sch.op("tensor", dmm, [last_act] + d_dwn + pd_war[dt % 2])
                                last_down = t_d
                                asl = acc_ap(dt, tb * 512, (tb + 1) * 512)
                                if e_i == 0 and ih == 0:
                                    t_a = sch.op("vector", lambda e, asl=asl, pdx=pdx: e.tensor_copy(
                                        out=asl, in_=pdx), [t_d] + acc_war)
                                else:
                                    t_a = sch.op("vector", lambda e, asl=asl, pdx=pdx: e.tensor_tensor(
                                        out=asl, in0=pdx, in1=asl, op=ALU.add),
                                        [t_d, acc_tok.get((dt, tb))] + acc_war)
                                acc_tok[(dt, tb)] = t_a
                                pd_war[dt % 2] = [t_a]
                        acc_war = []
                        act_war = [last_down]
                    dwn_war = [last_down]
                    dwn_pend = (e_i + 1) if e_i + 1 < E else (0 if th == 0 else None)
                    d_dwn = []
                wo_tr_last = None
                for tt in range(TH // 128):
                    trow = th0 + tt * 128
                    for dh in range(4):
                        bank = dh % 2
                        half = (tt * 4 + dh) % 2
                        etoks = []
                        for j in range(NKD // 4):
                            dt = dh * (NKD // 4) + j
                            src = acc_ap(dt, tt * 128, tt * 128 + 128)
                            dep = [acc_tok[(dt, (tt * 128) // 512)]]
                            if j == 0:
                                dep = dep + pt_war[bank]
                            t_t = sch.op("tensor", lambda e, src=src, bank=bank, j=j: e.transpose(
                                out=pt[bank][0:128, j * 128:(j + 1) * 128], in_=src,
                                identity=idb), dep)
                            etoks.append(t_t)
                            wo_tr_last = t_t
                        osl = ostg[:, half * 512:(half + 1) * 512]
                        t_e = sch.op("vector", lambda e, bank=bank, osl=osl: e.tensor_copy(
                            out=osl, in_=pt[bank][0:128, 0:512]),
                            etoks[-1:] + ostg_war[half])
                        pt_war[bank] = [t_e]
                        d_o = sch.dma("gp", y[trow:trow + 128, dh * (D // 4):(dh + 1) * (D // 4)],
                                      osl, deps=[t_e] + sch.qfrontier("sync_q"),
                                      accum_op=ALU.add)
                        ostg_war[half] = [d_o]
                acc_war = [wo_tr_last]

        blk = en(nc.Block())
        sch.build(blk)
    return nc


from concourse.bass_utils import run_bass_kernel_spmd

_D = 2048


def _pretile_gu(w, NKI, NKD=16):
    E = w.shape[0]
    return np.ascontiguousarray(
        w.reshape(E, NKD, 128, NKI, 128).transpose(0, 3, 2, 1, 4))


def _make_sel(E):
    s = np.zeros((E, E * 128), dtype=np.float32)
    for e in range(E):
        s[e, e * 128:(e + 1) * 128] = 1.0
    return s


def _make_xin(x_batch, p, t_own, t_prior):
    own = x_batch[p * t_own:(p + 1) * t_own]
    prior = x_batch[:p * t_own]
    pad = np.zeros((t_prior - prior.shape[0], _D), dtype=np.float32)
    return np.ascontiguousarray(np.concatenate([pad, prior, own], axis=0))


def kernel(L0_attn_x, L0_attn_log_A_diag, L0_attn_B_mat, L0_attn_C_mat,
           L0_attn_D_vec, L0_attn_ln_w, L0_attn_ln_b, L0_ffn_router_w,
           L0_ffn_expert_bias, L0_ffn_gate_proj, L0_ffn_up_proj,
           L0_ffn_down_proj):
    cfg = CFG(16, 1792, 2048, 6144)
    x = np.ascontiguousarray(np.asarray(L0_attn_x, dtype=np.float32))
    shared = {
        "logA": np.ascontiguousarray(np.asarray(L0_attn_log_A_diag, np.float32)),
        "Bm": np.ascontiguousarray(np.asarray(L0_attn_B_mat, np.float32)),
        "Cm": np.ascontiguousarray(np.asarray(L0_attn_C_mat, np.float32)),
        "Dv": np.ascontiguousarray(np.asarray(L0_attn_D_vec, np.float32)),
        "lnw": np.ascontiguousarray(np.asarray(L0_attn_ln_w, np.float32)),
        "lnb": np.ascontiguousarray(np.asarray(L0_attn_ln_b, np.float32)),
        "rw": np.ascontiguousarray(np.asarray(L0_ffn_router_w, np.float32)),
        "rb": np.ascontiguousarray(np.asarray(L0_ffn_expert_bias, np.float32)),
        "gpt": _pretile_gu(np.asarray(L0_ffn_gate_proj, np.float32), cfg.NKI),
        "upt": _pretile_gu(np.asarray(L0_ffn_up_proj, np.float32), cfg.NKI),
        "dpt": np.ascontiguousarray(
            np.asarray(L0_ffn_down_proj, np.float32).reshape(cfg.E, cfg.NKI, 128, _D)),
        "ident": np.eye(128, dtype=np.float32),
        "sel": _make_sel(cfg.E),
    }
    nc = build_nc(cfg)
    in_maps = []
    for core in range(8):
        b, p = core // 4, core % 4
        m = dict(shared)
        m["xin"] = _make_xin(x[b], p, cfg.T, cfg.TP)
        in_maps.append(m)
    res = run_bass_kernel_spmd(nc, in_maps, core_ids=list(range(8)))
    y = np.empty((2, 8192, _D), dtype=np.float32)
    for core in range(8):
        b, p = core // 4, core % 4
        y[b, p * cfg.T:(p + 1) * cfg.T] = res.results[core]["y"]
    return y


# version 07
# speedup vs baseline: 1.1472x; 1.0016x over previous
from contextlib import ExitStack

import numpy as np
import concourse.bass as bass
import concourse.mybir as mybir

F32 = mybir.dt.float32
BF16 = mybir.dt.bfloat16
AF = mybir.ActivationFunctionType
ALU = mybir.AluOpType
AX = mybir.AxisListType

COMPUTE = ("tensor", "vector", "scalar", "gpsimd")
DMAQ = ("sync_q", "gp_q", "sc_q")
NSLOT = 8
LN_EPS = 1e-5


class Sched:

    def __init__(self, nc, ctx):
        self.nc = nc
        names = list(COMPUTE)
        for q in DMAQ:
            names += [f"{q}{i}" for i in range(NSLOT)]
        self.sems = {n: ctx.enter_context(nc.semaphore("sem_" + n)) for n in names}
        self.count = {n: 0 for n in names}
        self.qidx = {q: 0 for q in DMAQ}
        self.qlast = {q: [None] * NSLOT for q in DMAQ}
        self.ops = []

    def _tok(self, stream, amount):
        self.count[stream] += amount
        return (stream, self.count[stream])

    def op(self, eng, fn, deps=()):
        tok = self._tok(eng, 1)
        self.ops.append((eng, eng, fn, tuple(d for d in deps if d), 1))
        return tok

    def dma(self, queue, out, in_, deps=(), **kw):
        q = {"sync": "sync_q", "gp": "gp_q", "sc": "sc_q"}[queue]
        body = {"sync": "sync", "gp": "gpsimd", "sc": "scalar"}[queue]
        slot = self.qidx[q] % NSLOT
        self.qidx[q] += 1
        stream = f"{q}{slot}"
        deps = list(d for d in deps if d)
        prev = self.qlast[q][slot]
        if prev is not None:
            deps.append(prev)
        tok = self._tok(stream, 16)
        self.qlast[q][slot] = tok

        def fn(e):
            return e.dma_start(out=out, in_=in_, **kw)

        self.ops.append((body, stream, fn, tuple(deps), 16))
        return tok

    def frontier(self):
        return [(s, v) for s, v in self.count.items() if v > 0]

    def qfrontier(self, q):
        return [t for t in self.qlast[q] if t is not None]

    def build(self, blk):
        bodies = {}
        for body, stream, fn, deps, inc in self.ops:
            bodies.setdefault(body, []).append((stream, fn, deps, inc))
        if "sync" not in bodies:
            bodies["sync"] = []
        final_counts = dict(self.count)

        def make(body_name, items):
            def emit(eng):
                hw = {}
                for stream, fn, deps, inc in items:
                    for (s, v) in deps:
                        if hw.get(s, 0) < v:
                            eng.wait_ge(self.sems[s], v)
                            hw[s] = v
                    fn(eng).then_inc(self.sems[stream], inc)
                if body_name == "sync":
                    for q in self.count:
                        if final_counts[q] > hw.get(q, 0):
                            eng.wait_ge(self.sems[q], final_counts[q])
            return emit

        for body_name, items in bodies.items():
            getattr(blk, body_name)(make(body_name, items))


class CFG:
    def __init__(self, n_exp=16, inter=1792, t_own=2048, t_prior=6144):
        self.D = 2048
        self.NKD = 16
        self.E = n_exp
        self.I = inter
        self.NKI = inter // 128
        self.T = t_own
        self.TP = t_prior
        self.NT = t_own // 128
        self.NTP = t_prior // 128
        self.TH = t_own // 2
        self.NTB = self.TH // 512
        assert t_own % 1024 == 0 and self.TH % 512 == 0 and inter % 128 == 0


def build_nc(cfg: CFG, debug=False):
    nc = bass.Bass("TRN2", target_bir_lowering=False)
    D, E, I, T, TP = cfg.D, cfg.E, cfg.I, cfg.T, cfg.TP
    NKD, NKI, NT, NTP, TH, NTB = cfg.NKD, cfg.NKI, cfg.NT, cfg.NTP, cfg.TH, cfg.NTB

    xin = nc.dram_tensor("xin", [TP + T, D], F32, kind="ExternalInput").ap()
    logA = nc.dram_tensor("logA", [128], F32, kind="ExternalInput").ap()
    Bm = nc.dram_tensor("Bm", [128, D], F32, kind="ExternalInput").ap()
    Cm = nc.dram_tensor("Cm", [D, 128], F32, kind="ExternalInput").ap()
    Dv = nc.dram_tensor("Dv", [D], F32, kind="ExternalInput").ap()
    lnw = nc.dram_tensor("lnw", [D], F32, kind="ExternalInput").ap()
    lnb = nc.dram_tensor("lnb", [D], F32, kind="ExternalInput").ap()
    rw = nc.dram_tensor("rw", [E, D], F32, kind="ExternalInput").ap()
    rb = nc.dram_tensor("rb", [E], F32, kind="ExternalInput").ap()
    gpt = nc.dram_tensor("gpt", [E, NKI, 128, NKD, 128], F32, kind="ExternalInput").ap()
    upt = nc.dram_tensor("upt", [E, NKI, 128, NKD, 128], F32, kind="ExternalInput").ap()
    dpt = nc.dram_tensor("dpt", [E, NKI, 128, D], F32, kind="ExternalInput").ap()
    ident_in = nc.dram_tensor("ident", [128, 128], F32, kind="ExternalInput").ap()
    sel_in = nc.dram_tensor("sel", [E, E * 128], F32, kind="ExternalInput").ap()
    y = nc.dram_tensor("y", [T, D], F32, kind="ExternalOutput").ap()
    wfmd = nc.dram_tensor("wfmd", [E, T], BF16, kind="Internal").ap()
    dbg = {}
    if debug:
        dbg["x1_t0"] = nc.dram_tensor("dbg_x1_t0", [128, D], F32, kind="ExternalOutput").ap()
        dbg["hs_t0"] = nc.dram_tensor("dbg_hs_t0", [128, 128], F32, kind="ExternalOutput").ap()
        dbg["z_t0"] = nc.dram_tensor("dbg_z_t0", [128, D], F32, kind="ExternalOutput").ap()
        dbg["x3_t0"] = nc.dram_tensor("dbg_x3_t0", [128, D], F32, kind="ExternalOutput").ap()
        dbg["lg_t0"] = nc.dram_tensor("dbg_lg_t0", [128, E], F32, kind="ExternalOutput").ap()
        dbg["wtm_t0"] = nc.dram_tensor("dbg_wtm_t0", [128, E], F32, kind="ExternalOutput").ap()
        dbg["acc_th0"] = nc.dram_tensor("dbg_acc_th0", [128, NKD * (T // 2)], F32, kind="ExternalOutput").ap()
        dbg["act_e0"] = nc.dram_tensor("dbg_act_e0", [128, NKI * (T // 2)], F32, kind="ExternalOutput").ap()
        dbg["wbc_e0"] = nc.dram_tensor("dbg_wbc_e0", [128, T // 2], F32, kind="ExternalOutput").ap()
        dbg["wfm"] = nc.dram_tensor("dbg_wfm", [E, T], F32, kind="ExternalOutput").ap()
        dbg["selb"] = nc.dram_tensor("dbg_selb", [E, 128], F32, kind="ExternalOutput").ap()
        dbg["pbc"] = nc.dram_tensor("dbg_pbc", [128, 512], F32, kind="ExternalOutput").ap()

    with ExitStack() as ctx:
        en = ctx.enter_context
        sch = Sched(nc, ctx)

        x3T = en(nc.sbuf_tensor("x3T", [128, NKD * T], BF16)).ap()
        gub = en(nc.sbuf_tensor("gub", [128, 6 * NKD * 128], BF16)).ap()
        idf = en(nc.sbuf_tensor("idf", [128, 128], F32)).ap()
        idb = en(nc.sbuf_tensor("idb", [128, 128], BF16)).ap()
        rlhs = en(nc.sbuf_tensor("rlhs", [128, NKD * E], BF16)).ap()
        rbias = en(nc.sbuf_tensor("rbias", [E, 1], F32)).ap()
        pt = [en(nc.psum_tensor("pt0", [128, 512], BF16)).ap(),
              en(nc.psum_tensor("pt1", [128, 512], BF16)).ap()]
        pt_war = {0: [], 1: []}

        CHW = NKD * 128

        def pe_transpose(src_ap, dst_ap, k, deps, fp32=False):
            P, F = src_ap.shape
            bank = k % 2
            idap = (idf if fp32 else idb)[0:P, 0:P]
            pb = pt[bank].bitcast(F32) if fp32 else pt[bank]
            pdst = pb[0:F, 0:P]
            t_t = sch.op("tensor", lambda e: e.transpose(out=pdst, in_=src_ap, identity=idap),
                         list(deps) + pt_war[bank])
            t_e = sch.op("vector", lambda e: e.tensor_copy(out=dst_ap, in_=pdst), [t_t])
            pt_war[bank] = [t_e]
            return t_e

        d_id = sch.dma("sync", idf, ident_in)
        t_idb = sch.op("vector", lambda e: e.tensor_copy(out=idb, in_=idf), [d_id])

        def gu_load(eidx, ic, buf, war):
            tg = sch.dma("gp", gub[:, buf * CHW:(buf + 1) * CHW],
                         gpt[eidx, ic].rearrange("p k i -> p (k i)"), deps=war)
            tu = sch.dma("gp", gub[:, (3 + buf) * CHW:(3 + buf + 1) * CHW],
                         upt[eidx, ic].rearrange("p k i -> p (k i)"), deps=war)
            return (tg, tu)

        gu_seq = [(th, e_i, ic) for th in range(2) for e_i in range(E)
                  for ic in range(NKI)]

        with ExitStack() as actx:
            aen = actx.enter_context
            Blhs = aen(nc.sbuf_tensor("Blhs", [128, NKD * 128], BF16)).ap()
            CT = aen(nc.sbuf_tensor("CT", [128, D], BF16)).ap()
            dbc = aen(nc.sbuf_tensor("dbc", [128, D], BF16)).ap()
            wbcst = aen(nc.sbuf_tensor("wbcst", [128, D], BF16)).ap()
            bbcst = aen(nc.sbuf_tensor("bbcst", [128, D], F32)).ap()
            abc = aen(nc.sbuf_tensor("abc", [128, 128], F32)).ap()
            adv = aen(nc.sbuf_tensor("adv", [128, 4], F32)).ap()
            epsv = aen(nc.sbuf_tensor("epsv", [128, 1], F32)).ap()
            pua = aen(nc.psum_tensor("pua", [128, 512], F32)).ap()
            py = [aen(nc.psum_tensor("py0", [128, 512], F32)).ap(),
                  aen(nc.psum_tensor("py1", [128, 512], F32)).ap()]
            ppy = py[0]

            with ExitStack() as sctx:
                sen = sctx.enter_context
                rowb = sen(nc.sbuf_tensor("rowb", [1, D], F32)).ap()
                ones1 = sen(nc.sbuf_tensor("ones1", [1, 128], F32)).ap()
                stg = sen(nc.sbuf_tensor("stg", [128, 2048], F32)).ap()
                stgb = sen(nc.sbuf_tensor("stgb", [128, 2048], BF16)).ap()

                t_eps = sch.op("vector", lambda e: e.memset(epsv, LN_EPS))
                d_logA = sch.dma("sync", adv[:, 0:1], logA.rearrange("(p one) -> p one", one=1))
                t_rate = sch.op("scalar", lambda e: e.activation(out=adv[:, 1:2], in_=adv[:, 0:1], func=AF.Exp), [d_logA])
                t_nr = sch.op("vector", lambda e: e.tensor_scalar_mul(adv[:, 2:3], adv[:, 1:2], -1.0), [t_rate])
                t_ad = sch.op("scalar", lambda e: e.activation(out=adv[:, 3:4], in_=adv[:, 2:3], func=AF.Exp), [t_nr])
                t_ab0 = sch.op("vector", lambda e: e.memset(abc, 1.0))
                t_ab = sch.op("vector", lambda e: e.tensor_scalar_mul(abc, abc, adv[:, 3:4]), [t_ad, t_ab0])

                t_ones = sch.op("vector", lambda e: e.memset(ones1, 1.0))
                ppy_war = []
                rowb_war = []

                def row_bcast(dst_bf, src_row, add1):
                    nonlocal ppy_war, rowb_war
                    d_r = sch.dma("sync", rowb, src_row.rearrange("(one d) -> one d", one=1),
                                  deps=rowb_war)
                    tlast = None
                    last_mm = None
                    for c in range(D // 512):
                        sl = slice(c * 512, (c + 1) * 512)
                        t_mm = sch.op("tensor", lambda e, sl=sl: e.matmul(
                            out=ppy[:, 0:512], lhsT=ones1, rhs=rowb[0:1, sl],
                            start=True, stop=True), [d_r, t_ones] + ppy_war)
                        last_mm = t_mm
                        if add1:
                            t_ev = sch.op("vector", lambda e, sl=sl: e.tensor_scalar_add(
                                dst_bf[:, sl], ppy[:, 0:512], 1.0), [t_mm])
                        else:
                            t_ev = sch.op("vector", lambda e, sl=sl: e.tensor_copy(
                                out=dst_bf[:, sl], in_=ppy[:, 0:512]), [t_mm])
                        ppy_war = [t_ev]
                        tlast = t_ev
                    rowb_war = [last_mm]
                    return tlast

                t_dbc = row_bcast(dbc, Dv, add1=True)
                t_wb = row_bcast(wbcst, lnw, add1=False)
                t_bb = row_bcast(bbcst, lnb, add1=False)

                d_B = sch.dma("sync", stg, Bm)
                t_Bc = sch.op("vector", lambda e: e.tensor_copy(out=stgb, in_=stg), [d_B, t_idb])
                bl_toks = [pe_transpose(stgb[:, k * 128:(k + 1) * 128], Blhs[:, k * 128:(k + 1) * 128],
                                        k, [t_Bc]) for k in range(NKD)]
                t_Blhs = bl_toks[-1]
                d_C = sch.dma("sync", stg.rearrange("p (k n) -> p k n", k=NKD),
                              Cm.rearrange("(k p) n -> p k n", p=128), deps=[t_Bc])
                t_Cc = sch.op("vector", lambda e: e.tensor_copy(out=stgb, in_=stg), [d_C, t_Blhs])
                ct_toks = [pe_transpose(stgb[:, k * 128:(k + 1) * 128], CT[:, k * 128:(k + 1) * 128],
                                        k, [t_Cc]) for k in range(NKD)]
                t_CT = ct_toks[-1]
                d_rw = sch.dma("sync", stg[0:E, :], rw, deps=[t_Cc])
                t_rwc = sch.op("vector", lambda e: e.tensor_copy(out=stgb[0:E, :], in_=stg[0:E, :]),
                               [d_rw, t_CT])
                rl_toks = [pe_transpose(stgb[0:E, k * 128:(k + 1) * 128], rlhs[:, k * E:(k + 1) * E],
                                        k, [t_rwc]) for k in range(NKD)]
                t_rlhs = rl_toks[-1]
                d_rb = sch.dma("sync", rbias, rb.rearrange("(p one) -> p one", one=1))
                setup_scratch_end = [t_rlhs]

            NXB = 4
            xbuf = aen(nc.sbuf_tensor("xbuf", [128, NXB * D], F32)).ap()
            x1b = aen(nc.sbuf_tensor("x1b", [128, 2 * D], BF16)).ap()
            x1T = aen(nc.sbuf_tensor("x1T", [128, 2 * NKD * 128], BF16)).ap()
            r1b = aen(nc.sbuf_tensor("r1b", [128, 2 * D], F32)).ap()
            zb = aen(nc.sbuf_tensor("zb", [128, 2 * D], F32)).ap()
            hsb = aen(nc.sbuf_tensor("hsb", [128, 2 * 128], F32)).ap()
            hbfb = aen(nc.sbuf_tensor("hbfb", [128, 2 * 128], BF16)).ap()
            bnb = aen(nc.sbuf_tensor("bnb", [128, 2 * 24], F32)).ap()
            stat = aen(nc.sbuf_tensor("stat", [128, 2 * 4], F32)).ap()
            lgf = aen(nc.sbuf_tensor("lgf", [E, 128], F32)).ap()
            lgt = aen(nc.sbuf_tensor("lgt", [128, E], F32)).ap()
            m8 = aen(nc.sbuf_tensor("m8", [128, 8], F32)).ap()
            tp4 = aen(nc.sbuf_tensor("tp4", [128, 2 * E], F32)).ap()
            wtm = aen(nc.sbuf_tensor("wtm", [128, E], F32)).ap()
            wrow = aen(nc.sbuf_tensor("wrow", [E, 128], BF16)).ap()
            x3v = x3T.rearrange("p (k t) -> p k t", k=NKD)

            arena2_war = setup_scratch_end

            gu_toks = {}
            gu_done = {}
            for s in range(3):
                th_s, e_s, ic_s = gu_seq[s]
                gu_toks[s] = gu_load(e_s, ic_s, s % 3, [])

            NGT = NTP + NT
            x_rd = {i: list(arena2_war) for i in range(NXB)}
            x1_rd = {0: list(arena2_war), 1: list(arena2_war)}
            x1T_rd = {0: list(arena2_war), 1: list(arena2_war)}
            r1_rd = {0: list(arena2_war), 1: list(arena2_war)}
            zb_rd = {0: list(arena2_war), 1: list(arena2_war)}
            st_war = {0: [], 1: []}
            st_rd = {0: [], 1: []}
            hs_rd = {0: [], 1: []}
            hbf_rd = {0: [], 1: []}
            pua_war = {0: [], 1: []}
            py_war = {0: list(ppy_war), 1: []}
            lg_war = []
            router_war = []
            carry = {"tok": None, "valid": False}
            d_x_tok = {}

            def x_load(gt):
                sl = gt % NXB
                xs = xbuf[:, sl * D:(sl + 1) * D]
                d0 = sch.dma("sync", xs[:, 0:D // 2],
                             xin[gt * 128:(gt + 1) * 128, 0:D // 2], deps=x_rd[sl])
                d1 = sch.dma("gp", xs[:, D // 2:D],
                             xin[gt * 128:(gt + 1) * 128, D // 2:D], deps=x_rd[sl])
                x_rd[sl] = []
                d_x_tok[gt] = [d0, d1]

            for gt in range(min(NXB, NGT)):
                x_load(gt)

            def ln_stats(src_ap, par, deps=(), chunk_deps=None):
                bn = bnb[:, par * 24:(par + 1) * 24]
                st = stat[:, par * 4:(par + 1) * 4]
                tks = []
                for c in range(4):
                    dd = list(deps) + (chunk_deps[c] if chunk_deps else [])
                    tks.append(sch.op("vector", lambda e, c=c, bn=bn, src_ap=src_ap: e.bn_stats(
                        out=bn[:, 6 * c:6 * (c + 1)],
                        in_=src_ap[:, 512 * c:512 * (c + 1)]), dd))
                t_ag = sch.op("vector", lambda e, bn=bn, st=st: e.bn_aggr(
                    out=st[:, 0:2], in_=bn), tks + st_war[par] + st_rd[par])
                t_ln = sch.op("scalar", lambda e, st=st: e.activation(
                    out=st[:, 2:3], in_=st[:, 1:2], func=AF.Ln, bias=epsv), [t_ag, t_eps])
                t_rs = sch.op("scalar", lambda e, st=st: e.activation(
                    out=st[:, 2:3], in_=st[:, 2:3], func=AF.Exp, scale=-0.5), [t_ln])
                t_nb = sch.op("vector", lambda e, st=st: e.tensor_scalar(
                    out=st[:, 3:4], in0=st[:, 0:1], scalar1=st[:, 2:3], scalar2=-1.0,
                    op0=ALU.mult, op1=ALU.mult), [t_rs])
                st_war[par] = [t_ln]
                st_rd[par] = []
                return t_nb, st

            def tr16(src2048, dst_evac, deps):
                evs = []
                for b in range(4):
                    bank = b % 2
                    tks = []
                    for j in range(4):
                        k = 4 * b + j
                        dep = list(deps) + (pt_war[bank] if j == 0 else [])
                        tks.append(sch.op("tensor", lambda e, k=k, bank=bank, j=j: e.transpose(
                            out=pt[bank][0:128, j * 128:(j + 1) * 128],
                            in_=src2048[:, k * 128:(k + 1) * 128], identity=idb), dep))
                    t_ev = dst_evac(b, pt[bank][0:128, 0:512], [tks[-1]])
                    pt_war[bank] = [t_ev]
                    evs.append(t_ev)
                return evs

            s1 = {}

            def stage1(gt):
                par = gt % 2
                sl = gt % NXB
                xs = xbuf[:, sl * D:(sl + 1) * D]
                d_x = d_x_tok[gt]
                x1s = x1b[:, par * D:(par + 1) * D]
                x1Ts = x1T[:, par * NKD * 128:(par + 1) * NKD * 128]
                t_nb, st = ln_stats(xs, par, deps=d_x)
                if gt >= NTP:
                    t_x1 = sch.op("scalar", lambda e, xs=xs, x1s=x1s, st=st: e.activation(
                        out=x1s, in_=xs, func=AF.Identity, bias=st[:, 3:4],
                        scale=st[:, 2:3]), [t_nb] + x1_rd[par])
                else:
                    t_x1 = sch.op("gpsimd", lambda e, xs=xs, x1s=x1s, st=st: e.tensor_scalar(
                        out=x1s, in0=xs, scalar1=st[:, 2:3], scalar2=st[:, 3:4],
                        op0=ALU.mult, op1=ALU.add), [t_nb] + x1_rd[par])
                x1_rd[par] = []
                st_rd[par].append(t_x1)
                x_rd[sl].append(t_x1)
                s1[gt] = (xs, sl, x1s, x1Ts, t_x1)

            def stage2(gt):
                nonlocal lg_war, router_war
                own = gt >= NTP
                tt = gt - NTP
                par = gt % 2
                xs, sl, x1s, x1Ts, t_x1 = s1.pop(gt)

                def x1t_evac(b, pt_ap, dep, par=par, x1Ts=x1Ts):
                    eng = "scalar" if b % 2 == 0 else "vector"
                    dd = dep + (x1T_rd[par] if b < 2 else [])
                    if eng == "scalar":
                        return sch.op(eng, lambda e, b=b, pt_ap=pt_ap: e.activation(
                            out=x1Ts[:, b * 512:(b + 1) * 512], in_=pt_ap,
                            func=AF.Identity), dd)
                    return sch.op(eng, lambda e, b=b, pt_ap=pt_ap: e.tensor_copy(
                        out=x1Ts[:, b * 512:(b + 1) * 512], in_=pt_ap), dd)

                tr_evs = tr16(x1s, x1t_evac, [t_x1])
                x1T_rd[par] = []
                x1_rd[par].append(tr_evs[-1])
                x1_rd[par].append(tr_evs[-2])

                uap = pua[:, par * 128:(par + 1) * 128]

                def u_mm(e, x1Ts=x1Ts, uap=uap):
                    for k in range(NKD):
                        mm = e.matmul(out=uap, lhsT=Blhs[:, k * 128:(k + 1) * 128],
                                      rhs=x1Ts[:, k * 128:(k + 1) * 128],
                                      start=(k == 0), stop=(k == NKD - 1))
                    return mm
                t_umm = sch.op("tensor", u_mm, tr_evs + [t_Blhs] + pua_war[par])
                x1T_rd[par].append(t_umm)
                hs = hsb[:, par * 128:(par + 1) * 128]
                init = hsb[:, (1 - par) * 128 + 127:(1 - par) * 128 + 128] if carry["valid"] else 0.0
                t_scan = sch.op("vector", lambda e, hs=hs, init=init, uap=uap: e.tensor_tensor_scan(
                    out=hs, data0=abc, data1=uap, initial=init,
                    op0=ALU.mult, op1=ALU.add),
                    [t_umm, t_ab] + hs_rd[par] + ([carry["tok"]] if carry["valid"] else []))
                hs_rd[par] = []
                pua_war[par] = [t_scan]
                carry.update(tok=t_scan, valid=True)
                if not own:
                    if gt + NXB < NGT:
                        x_load(gt + NXB)
                    return

                hbs = hbfb[:, par * 128:(par + 1) * 128]
                t_hc = sch.op("vector", lambda e, hs=hs, hbs=hbs: e.tensor_copy(
                    out=hbs, in_=hs), [t_scan] + hbf_rd[par])
                hbf_rd[par] = []
                hs_rd[par].append(t_hc)
                zs = zb[:, par * D:(par + 1) * D]
                t_m1 = sch.op("gpsimd", lambda e, zs=zs, x1s=x1s: e.tensor_tensor(
                    out=zs, in0=x1s, in1=dbc, op=ALU.mult), [t_x1, t_dbc] + zb_rd[par])
                zb_rd[par] = []
                x1_rd[par].append(t_m1)

                zc_toks = []
                for c in range(4):
                    pyb = py[c % 2]
                    t_yc = sch.op("tensor", lambda e, pyb=pyb, c=c, hbs=hbs: e.matmul(
                        out=pyb, lhsT=hbs, rhs=CT[:, c * 512:(c + 1) * 512],
                        start=True, stop=True), [t_hc, t_CT] + py_war[c % 2])
                    t_zc = sch.op("vector", lambda e, pyb=pyb, c=c, zs=zs: e.tensor_add(
                        out=zs[:, c * 512:(c + 1) * 512], in0=zs[:, c * 512:(c + 1) * 512],
                        in1=pyb), [t_yc, t_m1])
                    py_war[c % 2] = [t_zc]
                    zc_toks.append(t_zc)
                hbf_rd[par].append(t_yc)
                t_nb2, st2 = ln_stats(zs, par, chunk_deps=[[t] for t in zc_toks])
                t_nrm = sch.op("scalar", lambda e, zs=zs, x1s=x1s, st2=st2: e.activation(
                    out=x1s, in_=zs, func=AF.Identity, bias=st2[:, 3:4], scale=st2[:, 2:3]),
                    [t_nb2] + x1_rd[par])
                x1_rd[par] = []
                st_rd[par].append(t_nrm)
                zb_rd[par].append(t_nrm)
                r1 = r1b[:, par * D:(par + 1) * D]
                t_t2 = sch.op("gpsimd", lambda e, r1=r1, x1s=x1s: e.tensor_tensor(
                    out=r1, in0=x1s, in1=wbcst, op=ALU.mult), [t_nrm, t_wb] + r1_rd[par])
                r1_rd[par] = []
                x1_rd[par].append(t_t2)
                t_t3 = sch.op("gpsimd", lambda e, r1=r1: e.tensor_add(
                    out=r1, in0=r1, in1=bbcst), [t_t2, t_bb])
                t_r1 = sch.op("gpsimd", lambda e, r1=r1, xs=xs: e.tensor_add(
                    out=r1, in0=r1, in1=xs), [t_t3])
                x_rd[sl].append(t_r1)
                if gt + NXB < NGT:
                    x_load(gt + NXB)
                d_r1 = sch.dma("sync", y[tt * 128:(tt + 1) * 128, :], r1, deps=[t_r1])
                r1_rd[par].append(d_r1)
                t_nb3, st3 = ln_stats(r1, par, deps=[t_r1])
                t_x3 = sch.op("gpsimd", lambda e, r1=r1, x1s=x1s, st3=st3: e.tensor_scalar(
                    out=x1s, in0=r1, scalar1=st3[:, 2:3], scalar2=st3[:, 3:4],
                    op0=ALU.mult, op1=ALU.add), [t_nb3] + x1_rd[par])
                x1_rd[par] = []
                st_rd[par].append(t_x3)
                r1_rd[par].append(t_x3)

                def x3_evac(b, pt_ap, dep, tt=tt):
                    eng = "scalar" if b % 2 == 0 else "vector"
                    dst = x3v[:, b * 4:(b + 1) * 4, tt * 128:(tt + 1) * 128]
                    srcv = pt_ap.rearrange("p (j t) -> p j t", j=4)
                    if eng == "scalar":
                        return sch.op(eng, lambda e, dst=dst, srcv=srcv: e.activation(
                            out=dst, in_=srcv, func=AF.Identity), dep)
                    return sch.op(eng, lambda e, dst=dst, srcv=srcv: e.tensor_copy(
                        out=dst, in_=srcv), dep)

                tr3_evs = tr16(x1s, x3_evac, [t_x3])
                x1_rd[par] += [tr3_evs[-1], tr3_evs[-2]]

                def lg_mm(e, tt=tt):
                    for k in range(NKD):
                        mm = e.matmul(out=pua[0:E, 384:384 + 128], lhsT=rlhs[:, k * E:(k + 1) * E],
                                      rhs=x3T[:, k * T + tt * 128:k * T + tt * 128 + 128],
                                      start=(k == 0), stop=(k == NKD - 1))
                    return mm
                t_lg = sch.op("tensor", lg_mm, tr3_evs + [t_rlhs] + lg_war + router_war)
                t_lge = sch.op("vector", lambda e: e.tensor_scalar(
                    out=lgf, in0=pua[0:E, 384:384 + 128], scalar1=rbias, scalar2=None,
                    op0=ALU.add), [t_lg, d_rb])
                lg_war = [t_lge]
                t_lgte = pe_transpose(lgf, lgt, 0, [t_lge] + router_war, fp32=True)
                t_m8 = sch.op("vector", lambda e: e.max(out=m8, in_=lgt), [t_lgte])
                t_nmx = sch.op("gpsimd", lambda e: e.tensor_scalar_mul(m8[:, 7:8], m8[:, 0:1], -1.0),
                               [t_m8])
                t_ex = sch.op("scalar", lambda e: e.activation(
                    out=tp4[:, 0:E], in_=lgt, func=AF.Exp, bias=m8[:, 7:8]), [t_nmx])
                t_mk = sch.op("gpsimd", lambda e: e.tensor_scalar(
                    out=tp4[:, E:2 * E], in0=lgt, scalar1=m8[:, 3:4], scalar2=None,
                    op0=ALU.is_ge), [t_m8, t_ex])
                t_we = sch.op("gpsimd", lambda e: e.tensor_mul(
                    out=tp4[:, 0:E], in0=tp4[:, 0:E], in1=tp4[:, E:2 * E]), [t_ex, t_mk])
                t_sm = sch.op("vector", lambda e: e.reduce_sum(out=m8[:, 6:7], in_=tp4[:, 0:E],
                                                               axis=AX.X), [t_we])
                t_rc = sch.op("vector", lambda e: e.reciprocal(out=m8[:, 6:7], in_=m8[:, 6:7]),
                              [t_sm])
                t_wt = sch.op("gpsimd", lambda e: e.tensor_scalar_mul(wtm, tp4[:, 0:E], m8[:, 6:7]),
                              [t_rc])
                t_wT = pe_transpose(wtm, wrow, 1, [t_wt, t_lgte] + router_war[2:3],
                                    fp32=True)
                d_wfm = sch.dma("sync", wfmd[:, tt * 128:(tt + 1) * 128], wrow, deps=[t_wT])
                router_war = [t_ex, t_wT, d_wfm]

            for gt in range(NGT):
                stage1(gt)
                if gt > 0:
                    stage2(gt - 1)
            stage2(NGT - 1)

            aend = sch.frontier()

        NKH = NKI // 2
        with ExitStack() as fctx:
            fen = fctx.enter_context
            dwn = fen(nc.sbuf_tensor("dwn", [128, NKI * D], BF16)).ap()
            ostg = fen(nc.sbuf_tensor("ostg", [128, 2 * (D // 4)], F32)).ap()
            wbcf = fen(nc.sbuf_tensor("wbcf", [128, 2 * TH], BF16)).ap()
            sgb = fen(nc.sbuf_tensor("sgb", [128, 2 * 512], BF16)).ap()
            sub = fen(nc.sbuf_tensor("sub", [128, 2 * 512], BF16)).ap()
            actb = fen(nc.sbuf_tensor("actb", [128, NKH * TH], BF16)).ap()
            accb0 = fen(nc.sbuf_tensor("accb0", [128, (NKD // 2) * TH], BF16)).ap()
            accb1 = fen(nc.sbuf_tensor("accb1", [128, (NKD // 2) * TH], BF16)).ap()

            def acc_ap(dt, lo, hi):
                b = accb0 if dt < NKD // 2 else accb1
                return b[:, (dt % (NKD // 2)) * TH + lo:(dt % (NKD // 2)) * TH + hi]
            pg = [fen(nc.psum_tensor(f"pg{i}", [128, 512], F32)).ap() for i in range(NTB)]
            pu2 = [fen(nc.psum_tensor(f"pu{i}", [128, 512], F32)).ap() for i in range(NTB)]
            pd = [fen(nc.psum_tensor(f"pd{i}", [128, 512], F32)).ap() for i in range(2)]

            def dwn_load_ki(eidx, ki, war):
                return sch.dma("gp", dwn[:, ki * D:(ki + 1) * D], dpt[eidx, ki], deps=war)

            pd_war = {0: aend, 1: aend}
            pg_war = {i: aend for i in range(NTB)}
            pu_war = {i: aend for i in range(NTB)}
            sg_war = {i: list(aend) for i in range(NTB)}
            su_war = {i: list(aend) for i in range(NTB)}
            act_war = list(aend)
            acc_war = list(aend)
            ostg_war = {0: [], 1: []}
            d_dwn = [dwn_load_ki(0, ki, aend) for ki in range(NKI)]
            dwn_war = []
            dwn_pend = None
            last_act = None
            last_down = None
            seq = 0
            NSEQ = len(gu_seq)
            wbc_rd = {0: list(aend), 1: list(aend)}
            d_wbc_tok = {(0, 0): sch.dma("sync", wbcf[:, 0:TH],
                                         wfmd[0:1, 0:TH].to_broadcast([128, TH]),
                                         deps=list(aend))}

            for th in range(2):
                th0 = th * TH
                acc_tok = {}
                for e_i in range(E):
                    ne, nth = e_i + 1, th
                    if ne == E:
                        ne, nth = 0, th + 1
                    if nth < 2:
                        buf = ne % 2
                        d_wbc_tok[(nth, ne)] = sch.dma(
                            "sync", wbcf[:, buf * TH:(buf + 1) * TH],
                            wfmd[ne:ne + 1, nth * TH:nth * TH + TH].to_broadcast([128, TH]),
                            deps=wbc_rd[buf])
                        wbc_rd[buf] = []
                    d_wbc = d_wbc_tok[(th, e_i)]
                    wslot = (e_i % 2) * TH
                    for ih in range(2):
                        ic0 = ih * NKH
                        for kk in range(NKH):
                            ic = ic0 + kk
                            if dwn_pend is not None and e_i == dwn_pend:
                                d_dwn.append(dwn_load_ki(e_i, ic, dwn_war))
                            cur = gu_toks[seq]
                            curbuf = seq % 3
                            g_ap = gub[:, curbuf * CHW:(curbuf + 1) * CHW]
                            u_ap = gub[:, (3 + curbuf) * CHW:(3 + curbuf + 1) * CHW]

                            def bmm(e, w_ap, pbank, tb, th0=th0):
                                for k in range(NKD):
                                    mm = e.matmul(
                                        out=pbank, lhsT=w_ap[:, k * 128:(k + 1) * 128],
                                        rhs=x3T[:, k * T + th0 + tb * 512:k * T + th0 + tb * 512 + 512],
                                        start=(k == 0), stop=(k == NKD - 1))
                                return mm

                            t_bank = []
                            for bi in range(4):
                                tb = bi % 2
                                w_ap = g_ap if bi < 2 else u_ap
                                pbank = pg[tb] if bi < 2 else pu2[tb]
                                war = pg_war[tb] if bi < 2 else pu_war[tb]
                                dtk = cur[0] if bi < 2 else cur[1]
                                t_bank.append(sch.op(
                                    "tensor",
                                    lambda e, w_ap=w_ap, pbank=pbank, tb=tb, bmm=bmm: bmm(e, w_ap, pbank, tb),
                                    [dtk] + war))
                            gu_done[seq] = t_bank[3]
                            if seq + 3 < NSEQ:
                                th_n, e_n, ic_n = gu_seq[seq + 3]
                                gu_toks[seq + 3] = gu_load(e_n, ic_n, (seq + 3) % 3,
                                                           [gu_done[seq]])
                            for tb in range(NTB):
                                pgx, pux = pg[tb], pu2[tb]
                                sgs = sgb[:, tb * 512:(tb + 1) * 512]
                                sus = sub[:, tb * 512:(tb + 1) * 512]
                                t_sg = sch.op("scalar", lambda e, pgx=pgx, sgs=sgs: e.activation(
                                    out=sgs, in_=pgx, func=AF.Silu), [t_bank[tb]] + sg_war[tb])
                                t_su = sch.op("vector", lambda e, pux=pux, tb=tb, wslot=wslot, sus=sus: e.tensor_tensor(
                                    out=sus, in0=pux,
                                    in1=wbcf[:, wslot + tb * 512:wslot + (tb + 1) * 512],
                                    op=ALU.mult), [t_bank[2 + tb], d_wbc] + su_war[tb])
                                wbc_rd[e_i % 2].append(t_su)
                                dst = actb[:, kk * TH + tb * 512:kk * TH + (tb + 1) * 512]
                                t_act = sch.op("vector", lambda e, dst=dst, sgs=sgs, sus=sus: e.tensor_mul(
                                    out=dst, in0=sgs, in1=sus), [t_sg, t_su] + act_war)
                                act_war = []
                                sg_war[tb], su_war[tb] = [t_act], [t_act]
                                pg_war[tb] = [t_sg]
                                pu_war[tb] = [t_su]
                                last_act = t_act
                            seq += 1
                        for tb in range(NTB):
                            for dt in range(NKD):
                                pdx = pd[dt % 2]

                                def dmm(e, tb=tb, dt=dt, pdx=pdx, ic0=ic0):
                                    for kk in range(NKH):
                                        ki = ic0 + kk
                                        mm = e.matmul(
                                            out=pdx,
                                            lhsT=dwn[:, ki * D + dt * 128:ki * D + dt * 128 + 128],
                                            rhs=actb[:, kk * TH + tb * 512:kk * TH + (tb + 1) * 512],
                                            start=(kk == 0), stop=(kk == NKH - 1))
                                    return mm

                                t_d = sch.op("tensor", dmm, [last_act] + d_dwn + pd_war[dt % 2])
                                last_down = t_d
                                asl = acc_ap(dt, tb * 512, (tb + 1) * 512)
                                if e_i == 0 and ih == 0:
                                    t_a = sch.op("vector", lambda e, asl=asl, pdx=pdx: e.tensor_copy(
                                        out=asl, in_=pdx), [t_d] + acc_war)
                                else:
                                    t_a = sch.op("vector", lambda e, asl=asl, pdx=pdx: e.tensor_tensor(
                                        out=asl, in0=pdx, in1=asl, op=ALU.add),
                                        [t_d, acc_tok.get((dt, tb))] + acc_war)
                                acc_tok[(dt, tb)] = t_a
                                pd_war[dt % 2] = [t_a]
                        acc_war = []
                        act_war = [last_down]
                    dwn_war = [last_down]
                    dwn_pend = (e_i + 1) if e_i + 1 < E else (0 if th == 0 else None)
                    d_dwn = []
                wo_tr_last = None
                for tt in range(TH // 128):
                    trow = th0 + tt * 128
                    for dh in range(4):
                        bank = dh % 2
                        half = (tt * 4 + dh) % 2
                        etoks = []
                        for j in range(NKD // 4):
                            dt = dh * (NKD // 4) + j
                            src = acc_ap(dt, tt * 128, tt * 128 + 128)
                            dep = [acc_tok[(dt, (tt * 128) // 512)]]
                            if j == 0:
                                dep = dep + pt_war[bank]
                            t_t = sch.op("tensor", lambda e, src=src, bank=bank, j=j: e.transpose(
                                out=pt[bank][0:128, j * 128:(j + 1) * 128], in_=src,
                                identity=idb), dep)
                            etoks.append(t_t)
                            wo_tr_last = t_t
                        osl = ostg[:, half * 512:(half + 1) * 512]
                        t_e = sch.op("vector", lambda e, bank=bank, osl=osl: e.tensor_copy(
                            out=osl, in_=pt[bank][0:128, 0:512]),
                            etoks[-1:] + ostg_war[half])
                        pt_war[bank] = [t_e]
                        d_o = sch.dma("gp", y[trow:trow + 128, dh * (D // 4):(dh + 1) * (D // 4)],
                                      osl, deps=[t_e] + sch.qfrontier("sync_q"),
                                      accum_op=ALU.add)
                        ostg_war[half] = [d_o]
                acc_war = [wo_tr_last]

        blk = en(nc.Block())
        sch.build(blk)
    return nc


from concourse.bass_utils import run_bass_kernel_spmd

_D = 2048


def _pretile_gu(w, NKI, NKD=16):
    E = w.shape[0]
    return np.ascontiguousarray(
        w.reshape(E, NKD, 128, NKI, 128).transpose(0, 3, 2, 1, 4))


def _make_sel(E):
    s = np.zeros((E, E * 128), dtype=np.float32)
    for e in range(E):
        s[e, e * 128:(e + 1) * 128] = 1.0
    return s


def _make_xin(x_batch, p, t_own, t_prior):
    own = x_batch[p * t_own:(p + 1) * t_own]
    prior = x_batch[:p * t_own]
    pad = np.zeros((t_prior - prior.shape[0], _D), dtype=np.float32)
    return np.ascontiguousarray(np.concatenate([pad, prior, own], axis=0))


def kernel(L0_attn_x, L0_attn_log_A_diag, L0_attn_B_mat, L0_attn_C_mat,
           L0_attn_D_vec, L0_attn_ln_w, L0_attn_ln_b, L0_ffn_router_w,
           L0_ffn_expert_bias, L0_ffn_gate_proj, L0_ffn_up_proj,
           L0_ffn_down_proj):
    cfg = CFG(16, 1792, 2048, 6144)
    x = np.ascontiguousarray(np.asarray(L0_attn_x, dtype=np.float32))
    shared = {
        "logA": np.ascontiguousarray(np.asarray(L0_attn_log_A_diag, np.float32)),
        "Bm": np.ascontiguousarray(np.asarray(L0_attn_B_mat, np.float32)),
        "Cm": np.ascontiguousarray(np.asarray(L0_attn_C_mat, np.float32)),
        "Dv": np.ascontiguousarray(np.asarray(L0_attn_D_vec, np.float32)),
        "lnw": np.ascontiguousarray(np.asarray(L0_attn_ln_w, np.float32)),
        "lnb": np.ascontiguousarray(np.asarray(L0_attn_ln_b, np.float32)),
        "rw": np.ascontiguousarray(np.asarray(L0_ffn_router_w, np.float32)),
        "rb": np.ascontiguousarray(np.asarray(L0_ffn_expert_bias, np.float32)),
        "gpt": _pretile_gu(np.asarray(L0_ffn_gate_proj, np.float32), cfg.NKI),
        "upt": _pretile_gu(np.asarray(L0_ffn_up_proj, np.float32), cfg.NKI),
        "dpt": np.ascontiguousarray(
            np.asarray(L0_ffn_down_proj, np.float32).reshape(cfg.E, cfg.NKI, 128, _D)),
        "ident": np.eye(128, dtype=np.float32),
        "sel": _make_sel(cfg.E),
    }
    nc = build_nc(cfg)
    in_maps = []
    for core in range(8):
        b, p = core // 4, core % 4
        m = dict(shared)
        m["xin"] = _make_xin(x[b], p, cfg.T, cfg.TP)
        in_maps.append(m)
    res = run_bass_kernel_spmd(nc, in_maps, core_ids=list(range(8)))
    y = np.empty((2, 8192, _D), dtype=np.float32)
    for core in range(8):
        b, p = core // 4, core % 4
        y[b, p * cfg.T:(p + 1) * cfg.T] = res.results[core]["y"]
    return y


# version 08
# speedup vs baseline: 1.2307x; 1.0729x over previous
from contextlib import ExitStack

import numpy as np
import concourse.bass as bass
import concourse.mybir as mybir

F32 = mybir.dt.float32
BF16 = mybir.dt.bfloat16
AF = mybir.ActivationFunctionType
ALU = mybir.AluOpType
AX = mybir.AxisListType

COMPUTE = ("tensor", "vector", "scalar", "gpsimd")
DMAQ = ("sync_q", "gp_q", "sc_q")
NSLOT = 8
LN_EPS = 1e-5


class Sched:

    def __init__(self, nc, ctx):
        self.nc = nc
        names = list(COMPUTE)
        for q in DMAQ:
            names += [f"{q}{i}" for i in range(NSLOT)]
        self.sems = {n: ctx.enter_context(nc.semaphore("sem_" + n)) for n in names}
        self.count = {n: 0 for n in names}
        self.qidx = {q: 0 for q in DMAQ}
        self.qlast = {q: [None] * NSLOT for q in DMAQ}
        self.ops = []

    def _tok(self, stream, amount):
        self.count[stream] += amount
        return (stream, self.count[stream])

    def op(self, eng, fn, deps=()):
        tok = self._tok(eng, 1)
        self.ops.append((eng, eng, fn, tuple(d for d in deps if d), 1))
        return tok

    def dma(self, queue, out, in_, deps=(), **kw):
        q = {"sync": "sync_q", "gp": "gp_q", "sc": "sc_q"}[queue]
        body = {"sync": "sync", "gp": "gpsimd", "sc": "scalar"}[queue]
        slot = self.qidx[q] % NSLOT
        self.qidx[q] += 1
        stream = f"{q}{slot}"
        deps = list(d for d in deps if d)
        prev = self.qlast[q][slot]
        if prev is not None:
            deps.append(prev)
        tok = self._tok(stream, 16)
        self.qlast[q][slot] = tok

        def fn(e):
            return e.dma_start(out=out, in_=in_, **kw)

        self.ops.append((body, stream, fn, tuple(deps), 16))
        return tok

    def frontier(self):
        return [(s, v) for s, v in self.count.items() if v > 0]

    def qfrontier(self, q):
        return [t for t in self.qlast[q] if t is not None]

    def build(self, blk):
        bodies = {}
        for body, stream, fn, deps, inc in self.ops:
            bodies.setdefault(body, []).append((stream, fn, deps, inc))
        if "sync" not in bodies:
            bodies["sync"] = []
        final_counts = dict(self.count)

        def make(body_name, items):
            def emit(eng):
                hw = {}
                for stream, fn, deps, inc in items:
                    for (s, v) in deps:
                        if hw.get(s, 0) < v:
                            eng.wait_ge(self.sems[s], v)
                            hw[s] = v
                    fn(eng).then_inc(self.sems[stream], inc)
                if body_name == "sync":
                    for q in self.count:
                        if final_counts[q] > hw.get(q, 0):
                            eng.wait_ge(self.sems[q], final_counts[q])
            return emit

        for body_name, items in bodies.items():
            getattr(blk, body_name)(make(body_name, items))


class CFG:
    def __init__(self, n_exp=16, inter=1792, t_own=2048, t_prior=6144):
        self.D = 2048
        self.NKD = 16
        self.E = n_exp
        self.I = inter
        self.NKI = inter // 128
        self.T = t_own
        self.TP = t_prior
        self.NT = t_own // 128
        self.NTP = t_prior // 128
        self.TH = t_own // 2
        self.NTB = self.TH // 512
        assert t_own % 1024 == 0 and self.TH % 512 == 0 and inter % 128 == 0


def build_nc(cfg: CFG, debug=False):
    nc = bass.Bass("TRN2", target_bir_lowering=False)
    D, E, I, T, TP = cfg.D, cfg.E, cfg.I, cfg.T, cfg.TP
    NKD, NKI, NT, NTP, TH, NTB = cfg.NKD, cfg.NKI, cfg.NT, cfg.NTP, cfg.TH, cfg.NTB

    xin = nc.dram_tensor("xin", [TP + T, D], F32, kind="ExternalInput").ap()
    logA = nc.dram_tensor("logA", [128], F32, kind="ExternalInput").ap()
    Bm = nc.dram_tensor("Bm", [128, D], F32, kind="ExternalInput").ap()
    Cm = nc.dram_tensor("Cm", [D, 128], F32, kind="ExternalInput").ap()
    Dv = nc.dram_tensor("Dv", [D], F32, kind="ExternalInput").ap()
    lnw = nc.dram_tensor("lnw", [D], F32, kind="ExternalInput").ap()
    lnb = nc.dram_tensor("lnb", [D], F32, kind="ExternalInput").ap()
    rw = nc.dram_tensor("rw", [E, D], F32, kind="ExternalInput").ap()
    rb = nc.dram_tensor("rb", [E], F32, kind="ExternalInput").ap()
    gpt = nc.dram_tensor("gpt", [E, NKI, 128, NKD, 128], F32, kind="ExternalInput").ap()
    upt = nc.dram_tensor("upt", [E, NKI, 128, NKD, 128], F32, kind="ExternalInput").ap()
    dpt = nc.dram_tensor("dpt", [E, NKI, 128, D], F32, kind="ExternalInput").ap()
    ident_in = nc.dram_tensor("ident", [128, 128], F32, kind="ExternalInput").ap()
    sel_in = nc.dram_tensor("sel", [E, E * 128], F32, kind="ExternalInput").ap()
    y = nc.dram_tensor("y", [T, D], F32, kind="ExternalOutput").ap()
    wfmd = nc.dram_tensor("wfmd", [E, T], BF16, kind="Internal").ap()
    dbg = {}
    if debug:
        dbg["x1_t0"] = nc.dram_tensor("dbg_x1_t0", [128, D], F32, kind="ExternalOutput").ap()
        dbg["hs_t0"] = nc.dram_tensor("dbg_hs_t0", [128, 128], F32, kind="ExternalOutput").ap()
        dbg["z_t0"] = nc.dram_tensor("dbg_z_t0", [128, D], F32, kind="ExternalOutput").ap()
        dbg["x3_t0"] = nc.dram_tensor("dbg_x3_t0", [128, D], F32, kind="ExternalOutput").ap()
        dbg["lg_t0"] = nc.dram_tensor("dbg_lg_t0", [128, E], F32, kind="ExternalOutput").ap()
        dbg["wtm_t0"] = nc.dram_tensor("dbg_wtm_t0", [128, E], F32, kind="ExternalOutput").ap()
        dbg["acc_th0"] = nc.dram_tensor("dbg_acc_th0", [128, NKD * (T // 2)], F32, kind="ExternalOutput").ap()
        dbg["act_e0"] = nc.dram_tensor("dbg_act_e0", [128, NKI * (T // 2)], F32, kind="ExternalOutput").ap()
        dbg["wbc_e0"] = nc.dram_tensor("dbg_wbc_e0", [128, T // 2], F32, kind="ExternalOutput").ap()
        dbg["wfm"] = nc.dram_tensor("dbg_wfm", [E, T], F32, kind="ExternalOutput").ap()
        dbg["selb"] = nc.dram_tensor("dbg_selb", [E, 128], F32, kind="ExternalOutput").ap()
        dbg["pbc"] = nc.dram_tensor("dbg_pbc", [128, 512], F32, kind="ExternalOutput").ap()

    with ExitStack() as ctx:
        en = ctx.enter_context
        sch = Sched(nc, ctx)

        x3T = en(nc.sbuf_tensor("x3T", [128, NKD * T], BF16)).ap()
        gub = en(nc.sbuf_tensor("gub", [128, 6 * NKD * 128], BF16)).ap()
        idf = en(nc.sbuf_tensor("idf", [128, 128], F32)).ap()
        idb = en(nc.sbuf_tensor("idb", [128, 128], BF16)).ap()
        rlhs = en(nc.sbuf_tensor("rlhs", [128, NKD * E], BF16)).ap()
        rbias = en(nc.sbuf_tensor("rbias", [E, 1], F32)).ap()
        pt = [en(nc.psum_tensor("pt0", [128, 512], BF16)).ap(),
              en(nc.psum_tensor("pt1", [128, 512], BF16)).ap()]
        pt_war = {0: [], 1: []}

        CHW = NKD * 128

        def pe_transpose(src_ap, dst_ap, k, deps, fp32=False):
            P, F = src_ap.shape
            bank = k % 2
            idap = (idf if fp32 else idb)[0:P, 0:P]
            pb = pt[bank].bitcast(F32) if fp32 else pt[bank]
            pdst = pb[0:F, 0:P]
            t_t = sch.op("tensor", lambda e: e.transpose(out=pdst, in_=src_ap, identity=idap),
                         list(deps) + pt_war[bank])
            t_e = sch.op("vector", lambda e: e.tensor_copy(out=dst_ap, in_=pdst), [t_t])
            pt_war[bank] = [t_e]
            return t_e

        d_id = sch.dma("sync", idf, ident_in)
        t_idb = sch.op("vector", lambda e: e.tensor_copy(out=idb, in_=idf), [d_id])

        def gu_load(eidx, ic, buf, war):
            tg = sch.dma("gp", gub[:, buf * CHW:(buf + 1) * CHW],
                         gpt[eidx, ic].rearrange("p k i -> p (k i)"), deps=war)
            tu = sch.dma("gp", gub[:, (3 + buf) * CHW:(3 + buf + 1) * CHW],
                         upt[eidx, ic].rearrange("p k i -> p (k i)"), deps=war)
            return (tg, tu)

        gu_seq = [(th, e_i, ic) for th in range(2) for e_i in range(E)
                  for ic in range(NKI)]

        with ExitStack() as actx:
            aen = actx.enter_context
            Blhs = aen(nc.sbuf_tensor("Blhs", [128, NKD * 128], BF16)).ap()
            CT = aen(nc.sbuf_tensor("CT", [128, D], BF16)).ap()
            dbc = aen(nc.sbuf_tensor("dbc", [128, D], BF16)).ap()
            wbcst = aen(nc.sbuf_tensor("wbcst", [128, D], BF16)).ap()
            bbcst = aen(nc.sbuf_tensor("bbcst", [128, D], F32)).ap()
            abc = aen(nc.sbuf_tensor("abc", [128, 128], F32)).ap()
            adv = aen(nc.sbuf_tensor("adv", [128, 4], F32)).ap()
            epsv = aen(nc.sbuf_tensor("epsv", [128, 1], F32)).ap()
            pua = aen(nc.psum_tensor("pua", [128, 512], F32)).ap()
            py = [aen(nc.psum_tensor("py0", [128, 512], F32)).ap(),
                  aen(nc.psum_tensor("py1", [128, 512], F32)).ap()]
            ppy = py[0]

            with ExitStack() as sctx:
                sen = sctx.enter_context
                rowb = sen(nc.sbuf_tensor("rowb", [1, D], F32)).ap()
                ones1 = sen(nc.sbuf_tensor("ones1", [1, 128], F32)).ap()
                stg = sen(nc.sbuf_tensor("stg", [128, 2048], F32)).ap()
                stgb = sen(nc.sbuf_tensor("stgb", [128, 2048], BF16)).ap()

                t_eps = sch.op("vector", lambda e: e.memset(epsv, LN_EPS))
                d_logA = sch.dma("sync", adv[:, 0:1], logA.rearrange("(p one) -> p one", one=1))
                t_rate = sch.op("scalar", lambda e: e.activation(out=adv[:, 1:2], in_=adv[:, 0:1], func=AF.Exp), [d_logA])
                t_nr = sch.op("vector", lambda e: e.tensor_scalar_mul(adv[:, 2:3], adv[:, 1:2], -1.0), [t_rate])
                t_ad = sch.op("scalar", lambda e: e.activation(out=adv[:, 3:4], in_=adv[:, 2:3], func=AF.Exp), [t_nr])
                t_ab0 = sch.op("vector", lambda e: e.memset(abc, 1.0))
                t_ab = sch.op("vector", lambda e: e.tensor_scalar_mul(abc, abc, adv[:, 3:4]), [t_ad, t_ab0])

                t_ones = sch.op("vector", lambda e: e.memset(ones1, 1.0))
                ppy_war = []
                rowb_war = []

                def row_bcast(dst_bf, src_row, add1):
                    nonlocal ppy_war, rowb_war
                    d_r = sch.dma("sync", rowb, src_row.rearrange("(one d) -> one d", one=1),
                                  deps=rowb_war)
                    tlast = None
                    last_mm = None
                    for c in range(D // 512):
                        sl = slice(c * 512, (c + 1) * 512)
                        t_mm = sch.op("tensor", lambda e, sl=sl: e.matmul(
                            out=ppy[:, 0:512], lhsT=ones1, rhs=rowb[0:1, sl],
                            start=True, stop=True), [d_r, t_ones] + ppy_war)
                        last_mm = t_mm
                        if add1:
                            t_ev = sch.op("vector", lambda e, sl=sl: e.tensor_scalar_add(
                                dst_bf[:, sl], ppy[:, 0:512], 1.0), [t_mm])
                        else:
                            t_ev = sch.op("vector", lambda e, sl=sl: e.tensor_copy(
                                out=dst_bf[:, sl], in_=ppy[:, 0:512]), [t_mm])
                        ppy_war = [t_ev]
                        tlast = t_ev
                    rowb_war = [last_mm]
                    return tlast

                t_dbc = row_bcast(dbc, Dv, add1=True)
                t_wb = row_bcast(wbcst, lnw, add1=False)
                t_bb = row_bcast(bbcst, lnb, add1=False)

                d_B = sch.dma("sync", stg, Bm)
                t_Bc = sch.op("vector", lambda e: e.tensor_copy(out=stgb, in_=stg), [d_B, t_idb])
                bl_toks = [pe_transpose(stgb[:, k * 128:(k + 1) * 128], Blhs[:, k * 128:(k + 1) * 128],
                                        k, [t_Bc]) for k in range(NKD)]
                t_Blhs = bl_toks[-1]
                d_C = sch.dma("sync", stg.rearrange("p (k n) -> p k n", k=NKD),
                              Cm.rearrange("(k p) n -> p k n", p=128), deps=[t_Bc])
                t_Cc = sch.op("vector", lambda e: e.tensor_copy(out=stgb, in_=stg), [d_C, t_Blhs])
                ct_toks = [pe_transpose(stgb[:, k * 128:(k + 1) * 128], CT[:, k * 128:(k + 1) * 128],
                                        k, [t_Cc]) for k in range(NKD)]
                t_CT = ct_toks[-1]
                d_rw = sch.dma("sync", stg[0:E, :], rw, deps=[t_Cc])
                t_rwc = sch.op("vector", lambda e: e.tensor_copy(out=stgb[0:E, :], in_=stg[0:E, :]),
                               [d_rw, t_CT])
                rl_toks = [pe_transpose(stgb[0:E, k * 128:(k + 1) * 128], rlhs[:, k * E:(k + 1) * E],
                                        k, [t_rwc]) for k in range(NKD)]
                t_rlhs = rl_toks[-1]
                d_rb = sch.dma("sync", rbias, rb.rearrange("(p one) -> p one", one=1))
                setup_scratch_end = [t_rlhs]

            NXB = 4
            xbuf = aen(nc.sbuf_tensor("xbuf", [128, NXB * D], F32)).ap()
            x1b = aen(nc.sbuf_tensor("x1b", [128, 2 * D], BF16)).ap()
            x1T = aen(nc.sbuf_tensor("x1T", [128, 2 * NKD * 128], BF16)).ap()
            r1b = aen(nc.sbuf_tensor("r1b", [128, 2 * D], F32)).ap()
            zb = aen(nc.sbuf_tensor("zb", [128, 2 * D], F32)).ap()
            hsb = aen(nc.sbuf_tensor("hsb", [128, 2 * 128], F32)).ap()
            hbfb = aen(nc.sbuf_tensor("hbfb", [128, 2 * 128], BF16)).ap()
            bnb = aen(nc.sbuf_tensor("bnb", [128, 6 * 24], F32)).ap()
            stat = aen(nc.sbuf_tensor("stat", [128, 6 * 4], F32)).ap()
            x2b = aen(nc.sbuf_tensor("x2b", [128, 2 * D], BF16)).ap()
            lgf = aen(nc.sbuf_tensor("lgf", [E, 128], F32)).ap()
            lgt = aen(nc.sbuf_tensor("lgt", [128, E], F32)).ap()
            m8 = aen(nc.sbuf_tensor("m8", [128, 8], F32)).ap()
            tp4 = aen(nc.sbuf_tensor("tp4", [128, 2 * E], F32)).ap()
            wtm = aen(nc.sbuf_tensor("wtm", [128, E], F32)).ap()
            wrow = aen(nc.sbuf_tensor("wrow", [E, 128], BF16)).ap()
            x3v = x3T.rearrange("p (k t) -> p k t", k=NKD)

            arena2_war = setup_scratch_end

            gu_toks = {}
            gu_done = {}
            for s in range(3):
                th_s, e_s, ic_s = gu_seq[s]
                gu_toks[s] = gu_load(e_s, ic_s, s % 3, [])

            NGT = NTP + NT
            x_rd = {i: list(arena2_war) for i in range(NXB)}
            x1_rd = {0: list(arena2_war), 1: list(arena2_war)}
            x1T_rd = {0: list(arena2_war), 1: list(arena2_war)}
            r1_rd = {0: list(arena2_war), 1: list(arena2_war)}
            zb_rd = {0: list(arena2_war), 1: list(arena2_war)}
            st_war = {i: [] for i in range(6)}
            st_rd = {i: [] for i in range(6)}
            x2_rd = {0: list(arena2_war), 1: list(arena2_war)}
            hs_rd = {0: [], 1: []}
            hbf_rd = {0: [], 1: []}
            pua_war = {0: [], 1: []}
            py_war = {0: list(ppy_war), 1: []}
            lg_war = []
            router_war = []
            carry = {"tok": None, "valid": False}
            d_x_tok = {}

            def x_load(gt):
                sl = gt % NXB
                xs = xbuf[:, sl * D:(sl + 1) * D]
                d0 = sch.dma("sync", xs[:, 0:D // 2],
                             xin[gt * 128:(gt + 1) * 128, 0:D // 2], deps=x_rd[sl])
                d1 = sch.dma("gp", xs[:, D // 2:D],
                             xin[gt * 128:(gt + 1) * 128, D // 2:D], deps=x_rd[sl])
                x_rd[sl] = []
                d_x_tok[gt] = [d0, d1]

            for gt in range(min(NXB, NGT)):
                x_load(gt)

            def ln_stats(src_ap, par, deps=(), chunk_deps=None):
                bn = bnb[:, par * 24:(par + 1) * 24]
                st = stat[:, par * 4:(par + 1) * 4]
                tks = []
                for c in range(4):
                    dd = list(deps) + (chunk_deps[c] if chunk_deps else [])
                    tks.append(sch.op("vector", lambda e, c=c, bn=bn, src_ap=src_ap: e.bn_stats(
                        out=bn[:, 6 * c:6 * (c + 1)],
                        in_=src_ap[:, 512 * c:512 * (c + 1)]), dd))
                t_ag = sch.op("vector", lambda e, bn=bn, st=st: e.bn_aggr(
                    out=st[:, 0:2], in_=bn), tks + st_war[par] + st_rd[par])
                t_ln = sch.op("scalar", lambda e, st=st: e.activation(
                    out=st[:, 2:3], in_=st[:, 1:2], func=AF.Ln, bias=epsv), [t_ag, t_eps])
                t_rs = sch.op("scalar", lambda e, st=st: e.activation(
                    out=st[:, 2:3], in_=st[:, 2:3], func=AF.Exp, scale=-0.5), [t_ln])
                t_nb = sch.op("vector", lambda e, st=st: e.tensor_scalar(
                    out=st[:, 3:4], in0=st[:, 0:1], scalar1=st[:, 2:3], scalar2=-1.0,
                    op0=ALU.mult, op1=ALU.mult), [t_rs])
                st_war[par] = [t_ln]
                st_rd[par] = []
                return t_nb, st

            def tr16(src2048, dst_evac, deps):
                evs = []
                for b in range(4):
                    bank = b % 2
                    tks = []
                    for j in range(4):
                        k = 4 * b + j
                        dep = list(deps) + (pt_war[bank] if j == 0 else [])
                        tks.append(sch.op("tensor", lambda e, k=k, bank=bank, j=j: e.transpose(
                            out=pt[bank][0:128, j * 128:(j + 1) * 128],
                            in_=src2048[:, k * 128:(k + 1) * 128], identity=idb), dep))
                    t_ev = dst_evac(b, pt[bank][0:128, 0:512], [tks[-1]])
                    pt_war[bank] = [t_ev]
                    evs.append(t_ev)
                return evs

            s1 = {}

            def stage1(gt):
                par = gt % 2
                sl = gt % NXB
                xs = xbuf[:, sl * D:(sl + 1) * D]
                d_x = d_x_tok[gt]
                x1s = x1b[:, par * D:(par + 1) * D]
                x1Ts = x1T[:, par * NKD * 128:(par + 1) * NKD * 128]
                t_nb, st = ln_stats(xs, par * 3, deps=d_x)
                if gt >= NTP:
                    t_x1 = sch.op("scalar", lambda e, xs=xs, x1s=x1s, st=st: e.activation(
                        out=x1s, in_=xs, func=AF.Identity, bias=st[:, 3:4],
                        scale=st[:, 2:3]), [t_nb] + x1_rd[par])
                else:
                    t_x1 = sch.op("gpsimd", lambda e, xs=xs, x1s=x1s, st=st: e.tensor_scalar(
                        out=x1s, in0=xs, scalar1=st[:, 2:3], scalar2=st[:, 3:4],
                        op0=ALU.mult, op1=ALU.add), [t_nb] + x1_rd[par])
                x1_rd[par] = []
                st_rd[par * 3].append(t_x1)
                x_rd[sl].append(t_x1)
                s1[gt] = (xs, sl, x1s, x1Ts, t_x1)

            s2 = {}

            def stage2a(gt):
                own = gt >= NTP
                tt = gt - NTP
                par = gt % 2
                xs, sl, x1s, x1Ts, t_x1 = s1.pop(gt)

                def x1t_evac(b, pt_ap, dep, par=par, x1Ts=x1Ts):
                    eng = "scalar" if b % 2 == 0 else "vector"
                    dd = dep + (x1T_rd[par] if b < 2 else [])
                    if eng == "scalar":
                        return sch.op(eng, lambda e, b=b, pt_ap=pt_ap: e.activation(
                            out=x1Ts[:, b * 512:(b + 1) * 512], in_=pt_ap,
                            func=AF.Identity), dd)
                    return sch.op(eng, lambda e, b=b, pt_ap=pt_ap: e.tensor_copy(
                        out=x1Ts[:, b * 512:(b + 1) * 512], in_=pt_ap), dd)

                tr_evs = tr16(x1s, x1t_evac, [t_x1])
                x1T_rd[par] = []
                x1_rd[par].append(tr_evs[-1])
                x1_rd[par].append(tr_evs[-2])

                uap = pua[:, par * 128:(par + 1) * 128]

                def u_mm(e, x1Ts=x1Ts, uap=uap):
                    for k in range(NKD):
                        mm = e.matmul(out=uap, lhsT=Blhs[:, k * 128:(k + 1) * 128],
                                      rhs=x1Ts[:, k * 128:(k + 1) * 128],
                                      start=(k == 0), stop=(k == NKD - 1))
                    return mm
                t_umm = sch.op("tensor", u_mm, tr_evs + [t_Blhs] + pua_war[par])
                x1T_rd[par].append(t_umm)
                hs = hsb[:, par * 128:(par + 1) * 128]
                init = hsb[:, (1 - par) * 128 + 127:(1 - par) * 128 + 128] if carry["valid"] else 0.0
                t_scan = sch.op("vector", lambda e, hs=hs, init=init, uap=uap: e.tensor_tensor_scan(
                    out=hs, data0=abc, data1=uap, initial=init,
                    op0=ALU.mult, op1=ALU.add),
                    [t_umm, t_ab] + hs_rd[par] + ([carry["tok"]] if carry["valid"] else []))
                hs_rd[par] = []
                pua_war[par] = [t_scan]
                carry.update(tok=t_scan, valid=True)
                if not own:
                    if gt + NXB < NGT:
                        x_load(gt + NXB)
                    return

                hbs = hbfb[:, par * 128:(par + 1) * 128]
                t_hc = sch.op("vector", lambda e, hs=hs, hbs=hbs: e.tensor_copy(
                    out=hbs, in_=hs), [t_scan] + hbf_rd[par])
                hbf_rd[par] = []
                hs_rd[par].append(t_hc)
                zs = zb[:, par * D:(par + 1) * D]
                t_m1 = sch.op("gpsimd", lambda e, zs=zs, x1s=x1s: e.tensor_tensor(
                    out=zs, in0=x1s, in1=dbc, op=ALU.mult), [t_x1, t_dbc] + zb_rd[par])
                zb_rd[par] = []
                x1_rd[par].append(t_m1)

                zc_toks = []
                for c in range(4):
                    pyb = py[c % 2]
                    t_yc = sch.op("tensor", lambda e, pyb=pyb, c=c, hbs=hbs: e.matmul(
                        out=pyb, lhsT=hbs, rhs=CT[:, c * 512:(c + 1) * 512],
                        start=True, stop=True), [t_hc, t_CT] + py_war[c % 2])
                    t_zc = sch.op("vector", lambda e, pyb=pyb, c=c, zs=zs: e.tensor_add(
                        out=zs[:, c * 512:(c + 1) * 512], in0=zs[:, c * 512:(c + 1) * 512],
                        in1=pyb), [t_yc, t_m1])
                    py_war[c % 2] = [t_zc]
                    zc_toks.append(t_zc)
                hbf_rd[par].append(t_yc)
                t_nb2, st2 = ln_stats(zs, par * 3 + 1, chunk_deps=[[t] for t in zc_toks])
                x2s = x2b[:, par * D:(par + 1) * D]
                t_nrm = sch.op("scalar", lambda e, zs=zs, x2s=x2s, st2=st2: e.activation(
                    out=x2s, in_=zs, func=AF.Identity, bias=st2[:, 3:4], scale=st2[:, 2:3]),
                    [t_nb2] + x2_rd[par])
                x2_rd[par] = []
                st_rd[par * 3 + 1].append(t_nrm)
                zb_rd[par].append(t_nrm)
                s2[gt] = (xs, sl, x2s, t_nrm)

            def stage2b(gt):
                nonlocal lg_war, router_war
                tt = gt - NTP
                par = gt % 2
                xs, sl, x2s, t_nrm = s2.pop(gt)
                r1 = r1b[:, par * D:(par + 1) * D]
                t_t2 = sch.op("gpsimd", lambda e, r1=r1, x2s=x2s: e.tensor_tensor(
                    out=r1, in0=x2s, in1=wbcst, op=ALU.mult), [t_nrm, t_wb] + r1_rd[par])
                r1_rd[par] = []
                x2_rd[par].append(t_t2)
                t_t3 = sch.op("gpsimd", lambda e, r1=r1: e.tensor_add(
                    out=r1, in0=r1, in1=bbcst), [t_t2, t_bb])
                t_r1 = sch.op("gpsimd", lambda e, r1=r1, xs=xs: e.tensor_add(
                    out=r1, in0=r1, in1=xs), [t_t3])
                x_rd[sl].append(t_r1)
                if gt + NXB < NGT:
                    x_load(gt + NXB)
                d_r1 = sch.dma("sync", y[tt * 128:(tt + 1) * 128, :], r1, deps=[t_r1])
                r1_rd[par].append(d_r1)
                t_nb3, st3 = ln_stats(r1, par * 3 + 2, deps=[t_r1])
                t_x3 = sch.op("gpsimd", lambda e, r1=r1, x2s=x2s, st3=st3: e.tensor_scalar(
                    out=x2s, in0=r1, scalar1=st3[:, 2:3], scalar2=st3[:, 3:4],
                    op0=ALU.mult, op1=ALU.add), [t_nb3] + x2_rd[par])
                x2_rd[par] = []
                st_rd[par * 3 + 2].append(t_x3)
                r1_rd[par].append(t_x3)

                def x3_evac(b, pt_ap, dep, tt=tt):
                    eng = "scalar" if b % 2 == 0 else "vector"
                    dst = x3v[:, b * 4:(b + 1) * 4, tt * 128:(tt + 1) * 128]
                    srcv = pt_ap.rearrange("p (j t) -> p j t", j=4)
                    if eng == "scalar":
                        return sch.op(eng, lambda e, dst=dst, srcv=srcv: e.activation(
                            out=dst, in_=srcv, func=AF.Identity), dep)
                    return sch.op(eng, lambda e, dst=dst, srcv=srcv: e.tensor_copy(
                        out=dst, in_=srcv), dep)

                tr3_evs = tr16(x2s, x3_evac, [t_x3])
                x2_rd[par] += [tr3_evs[-1], tr3_evs[-2]]

                def lg_mm(e, tt=tt):
                    for k in range(NKD):
                        mm = e.matmul(out=pua[0:E, 384:384 + 128], lhsT=rlhs[:, k * E:(k + 1) * E],
                                      rhs=x3T[:, k * T + tt * 128:k * T + tt * 128 + 128],
                                      start=(k == 0), stop=(k == NKD - 1))
                    return mm
                t_lg = sch.op("tensor", lg_mm, tr3_evs + [t_rlhs] + lg_war + router_war)
                t_lge = sch.op("vector", lambda e: e.tensor_scalar(
                    out=lgf, in0=pua[0:E, 384:384 + 128], scalar1=rbias, scalar2=None,
                    op0=ALU.add), [t_lg, d_rb])
                lg_war = [t_lge]
                t_lgte = pe_transpose(lgf, lgt, 0, [t_lge] + router_war, fp32=True)
                t_m8 = sch.op("vector", lambda e: e.max(out=m8, in_=lgt), [t_lgte])
                t_nmx = sch.op("gpsimd", lambda e: e.tensor_scalar_mul(m8[:, 7:8], m8[:, 0:1], -1.0),
                               [t_m8])
                t_ex = sch.op("scalar", lambda e: e.activation(
                    out=tp4[:, 0:E], in_=lgt, func=AF.Exp, bias=m8[:, 7:8]), [t_nmx])
                t_mk = sch.op("gpsimd", lambda e: e.tensor_scalar(
                    out=tp4[:, E:2 * E], in0=lgt, scalar1=m8[:, 3:4], scalar2=None,
                    op0=ALU.is_ge), [t_m8, t_ex])
                t_we = sch.op("gpsimd", lambda e: e.tensor_mul(
                    out=tp4[:, 0:E], in0=tp4[:, 0:E], in1=tp4[:, E:2 * E]), [t_ex, t_mk])
                t_sm = sch.op("vector", lambda e: e.reduce_sum(out=m8[:, 6:7], in_=tp4[:, 0:E],
                                                               axis=AX.X), [t_we])
                t_rc = sch.op("vector", lambda e: e.reciprocal(out=m8[:, 6:7], in_=m8[:, 6:7]),
                              [t_sm])
                t_wt = sch.op("gpsimd", lambda e: e.tensor_scalar_mul(wtm, tp4[:, 0:E], m8[:, 6:7]),
                              [t_rc])
                t_wT = pe_transpose(wtm, wrow, 1, [t_wt, t_lgte] + router_war[2:3],
                                    fp32=True)
                d_wfm = sch.dma("sync", wfmd[:, tt * 128:(tt + 1) * 128], wrow, deps=[t_wT])
                router_war = [t_ex, t_wT, d_wfm]

            for gt in range(NGT):
                stage1(gt)
                if gt >= 1:
                    stage2a(gt - 1)
                if gt >= 2 and gt - 2 >= NTP:
                    stage2b(gt - 2)
            stage2a(NGT - 1)
            stage2b(NGT - 2)
            stage2b(NGT - 1)

            aend = sch.frontier()

        NKH = NKI // 2
        with ExitStack() as fctx:
            fen = fctx.enter_context
            dwn = fen(nc.sbuf_tensor("dwn", [128, NKI * D], BF16)).ap()
            ostg = fen(nc.sbuf_tensor("ostg", [128, 2 * (D // 4)], F32)).ap()
            wbcf = fen(nc.sbuf_tensor("wbcf", [128, 2 * TH], BF16)).ap()
            sgb = fen(nc.sbuf_tensor("sgb", [128, 2 * 512], BF16)).ap()
            sub = fen(nc.sbuf_tensor("sub", [128, 2 * 512], BF16)).ap()
            actb = fen(nc.sbuf_tensor("actb", [128, NKH * TH], BF16)).ap()
            accb0 = fen(nc.sbuf_tensor("accb0", [128, (NKD // 2) * TH], BF16)).ap()
            accb1 = fen(nc.sbuf_tensor("accb1", [128, (NKD // 2) * TH], BF16)).ap()

            def acc_ap(dt, lo, hi):
                b = accb0 if dt < NKD // 2 else accb1
                return b[:, (dt % (NKD // 2)) * TH + lo:(dt % (NKD // 2)) * TH + hi]
            pg = [fen(nc.psum_tensor(f"pg{i}", [128, 512], F32)).ap() for i in range(NTB)]
            pu2 = [fen(nc.psum_tensor(f"pu{i}", [128, 512], F32)).ap() for i in range(NTB)]
            pd = [fen(nc.psum_tensor(f"pd{i}", [128, 512], F32)).ap() for i in range(2)]

            def dwn_load_ki(eidx, ki, war):
                return sch.dma("gp", dwn[:, ki * D:(ki + 1) * D], dpt[eidx, ki], deps=war)

            pd_war = {0: aend, 1: aend}
            pg_war = {i: aend for i in range(NTB)}
            pu_war = {i: aend for i in range(NTB)}
            sg_war = {i: list(aend) for i in range(NTB)}
            su_war = {i: list(aend) for i in range(NTB)}
            act_war = list(aend)
            acc_war = list(aend)
            ostg_war = {0: [], 1: []}
            d_dwn = [dwn_load_ki(0, ki, aend) for ki in range(NKI)]
            dwn_war = []
            dwn_pend = None
            last_act = None
            last_down = None
            seq = 0
            NSEQ = len(gu_seq)
            wbc_rd = {0: list(aend), 1: list(aend)}
            d_wbc_tok = {(0, 0): sch.dma("sync", wbcf[:, 0:TH],
                                         wfmd[0:1, 0:TH].to_broadcast([128, TH]),
                                         deps=list(aend))}

            for th in range(2):
                th0 = th * TH
                acc_tok = {}
                for e_i in range(E):
                    ne, nth = e_i + 1, th
                    if ne == E:
                        ne, nth = 0, th + 1
                    if nth < 2:
                        buf = ne % 2
                        d_wbc_tok[(nth, ne)] = sch.dma(
                            "sync", wbcf[:, buf * TH:(buf + 1) * TH],
                            wfmd[ne:ne + 1, nth * TH:nth * TH + TH].to_broadcast([128, TH]),
                            deps=wbc_rd[buf])
                        wbc_rd[buf] = []
                    d_wbc = d_wbc_tok[(th, e_i)]
                    wslot = (e_i % 2) * TH
                    for ih in range(2):
                        ic0 = ih * NKH
                        for kk in range(NKH):
                            ic = ic0 + kk
                            if dwn_pend is not None and e_i == dwn_pend:
                                d_dwn.append(dwn_load_ki(e_i, ic, dwn_war))
                            cur = gu_toks[seq]
                            curbuf = seq % 3
                            g_ap = gub[:, curbuf * CHW:(curbuf + 1) * CHW]
                            u_ap = gub[:, (3 + curbuf) * CHW:(3 + curbuf + 1) * CHW]

                            def bmm(e, w_ap, pbank, tb, th0=th0):
                                for k in range(NKD):
                                    mm = e.matmul(
                                        out=pbank, lhsT=w_ap[:, k * 128:(k + 1) * 128],
                                        rhs=x3T[:, k * T + th0 + tb * 512:k * T + th0 + tb * 512 + 512],
                                        start=(k == 0), stop=(k == NKD - 1))
                                return mm

                            t_bank = []
                            for bi in range(4):
                                tb = bi % 2
                                w_ap = g_ap if bi < 2 else u_ap
                                pbank = pg[tb] if bi < 2 else pu2[tb]
                                war = pg_war[tb] if bi < 2 else pu_war[tb]
                                dtk = cur[0] if bi < 2 else cur[1]
                                t_bank.append(sch.op(
                                    "tensor",
                                    lambda e, w_ap=w_ap, pbank=pbank, tb=tb, bmm=bmm: bmm(e, w_ap, pbank, tb),
                                    [dtk] + war))
                            gu_done[seq] = t_bank[3]
                            if seq + 3 < NSEQ:
                                th_n, e_n, ic_n = gu_seq[seq + 3]
                                gu_toks[seq + 3] = gu_load(e_n, ic_n, (seq + 3) % 3,
                                                           [gu_done[seq]])
                            for tb in range(NTB):
                                pgx, pux = pg[tb], pu2[tb]
                                sgs = sgb[:, tb * 512:(tb + 1) * 512]
                                sus = sub[:, tb * 512:(tb + 1) * 512]
                                t_sg = sch.op("scalar", lambda e, pgx=pgx, sgs=sgs: e.activation(
                                    out=sgs, in_=pgx, func=AF.Silu), [t_bank[tb]] + sg_war[tb])
                                t_su = sch.op("vector", lambda e, pux=pux, tb=tb, wslot=wslot, sus=sus: e.tensor_tensor(
                                    out=sus, in0=pux,
                                    in1=wbcf[:, wslot + tb * 512:wslot + (tb + 1) * 512],
                                    op=ALU.mult), [t_bank[2 + tb], d_wbc] + su_war[tb])
                                wbc_rd[e_i % 2].append(t_su)
                                dst = actb[:, kk * TH + tb * 512:kk * TH + (tb + 1) * 512]
                                t_act = sch.op("vector", lambda e, dst=dst, sgs=sgs, sus=sus: e.tensor_mul(
                                    out=dst, in0=sgs, in1=sus), [t_sg, t_su] + act_war)
                                act_war = []
                                sg_war[tb], su_war[tb] = [t_act], [t_act]
                                pg_war[tb] = [t_sg]
                                pu_war[tb] = [t_su]
                                last_act = t_act
                            seq += 1
                        for tb in range(NTB):
                            for dt in range(NKD):
                                pdx = pd[dt % 2]

                                def dmm(e, tb=tb, dt=dt, pdx=pdx, ic0=ic0):
                                    for kk in range(NKH):
                                        ki = ic0 + kk
                                        mm = e.matmul(
                                            out=pdx,
                                            lhsT=dwn[:, ki * D + dt * 128:ki * D + dt * 128 + 128],
                                            rhs=actb[:, kk * TH + tb * 512:kk * TH + (tb + 1) * 512],
                                            start=(kk == 0), stop=(kk == NKH - 1))
                                    return mm

                                t_d = sch.op("tensor", dmm, [last_act] + d_dwn + pd_war[dt % 2])
                                last_down = t_d
                                asl = acc_ap(dt, tb * 512, (tb + 1) * 512)
                                if e_i == 0 and ih == 0:
                                    t_a = sch.op("vector", lambda e, asl=asl, pdx=pdx: e.tensor_copy(
                                        out=asl, in_=pdx), [t_d] + acc_war)
                                else:
                                    t_a = sch.op("vector", lambda e, asl=asl, pdx=pdx: e.tensor_tensor(
                                        out=asl, in0=pdx, in1=asl, op=ALU.add),
                                        [t_d, acc_tok.get((dt, tb))] + acc_war)
                                acc_tok[(dt, tb)] = t_a
                                pd_war[dt % 2] = [t_a]
                        acc_war = []
                        act_war = [last_down]
                    dwn_war = [last_down]
                    dwn_pend = (e_i + 1) if e_i + 1 < E else (0 if th == 0 else None)
                    d_dwn = []
                wo_tr_last = None
                for tt in range(TH // 128):
                    trow = th0 + tt * 128
                    for dh in range(4):
                        bank = dh % 2
                        half = (tt * 4 + dh) % 2
                        etoks = []
                        for j in range(NKD // 4):
                            dt = dh * (NKD // 4) + j
                            src = acc_ap(dt, tt * 128, tt * 128 + 128)
                            dep = [acc_tok[(dt, (tt * 128) // 512)]]
                            if j == 0:
                                dep = dep + pt_war[bank]
                            t_t = sch.op("tensor", lambda e, src=src, bank=bank, j=j: e.transpose(
                                out=pt[bank][0:128, j * 128:(j + 1) * 128], in_=src,
                                identity=idb), dep)
                            etoks.append(t_t)
                            wo_tr_last = t_t
                        osl = ostg[:, half * 512:(half + 1) * 512]
                        t_e = sch.op("vector", lambda e, bank=bank, osl=osl: e.tensor_copy(
                            out=osl, in_=pt[bank][0:128, 0:512]),
                            etoks[-1:] + ostg_war[half])
                        pt_war[bank] = [t_e]
                        d_o = sch.dma("gp", y[trow:trow + 128, dh * (D // 4):(dh + 1) * (D // 4)],
                                      osl, deps=[t_e] + sch.qfrontier("sync_q"),
                                      accum_op=ALU.add)
                        ostg_war[half] = [d_o]
                acc_war = [wo_tr_last]

        blk = en(nc.Block())
        sch.build(blk)
    return nc


from concourse.bass_utils import run_bass_kernel_spmd

_D = 2048


def _pretile_gu(w, NKI, NKD=16):
    E = w.shape[0]
    return np.ascontiguousarray(
        w.reshape(E, NKD, 128, NKI, 128).transpose(0, 3, 2, 1, 4))


def _make_sel(E):
    s = np.zeros((E, E * 128), dtype=np.float32)
    for e in range(E):
        s[e, e * 128:(e + 1) * 128] = 1.0
    return s


def _make_xin(x_batch, p, t_own, t_prior):
    own = x_batch[p * t_own:(p + 1) * t_own]
    prior = x_batch[:p * t_own]
    pad = np.zeros((t_prior - prior.shape[0], _D), dtype=np.float32)
    return np.ascontiguousarray(np.concatenate([pad, prior, own], axis=0))


def kernel(L0_attn_x, L0_attn_log_A_diag, L0_attn_B_mat, L0_attn_C_mat,
           L0_attn_D_vec, L0_attn_ln_w, L0_attn_ln_b, L0_ffn_router_w,
           L0_ffn_expert_bias, L0_ffn_gate_proj, L0_ffn_up_proj,
           L0_ffn_down_proj):
    cfg = CFG(16, 1792, 2048, 6144)
    x = np.ascontiguousarray(np.asarray(L0_attn_x, dtype=np.float32))
    shared = {
        "logA": np.ascontiguousarray(np.asarray(L0_attn_log_A_diag, np.float32)),
        "Bm": np.ascontiguousarray(np.asarray(L0_attn_B_mat, np.float32)),
        "Cm": np.ascontiguousarray(np.asarray(L0_attn_C_mat, np.float32)),
        "Dv": np.ascontiguousarray(np.asarray(L0_attn_D_vec, np.float32)),
        "lnw": np.ascontiguousarray(np.asarray(L0_attn_ln_w, np.float32)),
        "lnb": np.ascontiguousarray(np.asarray(L0_attn_ln_b, np.float32)),
        "rw": np.ascontiguousarray(np.asarray(L0_ffn_router_w, np.float32)),
        "rb": np.ascontiguousarray(np.asarray(L0_ffn_expert_bias, np.float32)),
        "gpt": _pretile_gu(np.asarray(L0_ffn_gate_proj, np.float32), cfg.NKI),
        "upt": _pretile_gu(np.asarray(L0_ffn_up_proj, np.float32), cfg.NKI),
        "dpt": np.ascontiguousarray(
            np.asarray(L0_ffn_down_proj, np.float32).reshape(cfg.E, cfg.NKI, 128, _D)),
        "ident": np.eye(128, dtype=np.float32),
        "sel": _make_sel(cfg.E),
    }
    nc = build_nc(cfg)
    in_maps = []
    for core in range(8):
        b, p = core // 4, core % 4
        m = dict(shared)
        m["xin"] = _make_xin(x[b], p, cfg.T, cfg.TP)
        in_maps.append(m)
    res = run_bass_kernel_spmd(nc, in_maps, core_ids=list(range(8)))
    y = np.empty((2, 8192, _D), dtype=np.float32)
    for core in range(8):
        b, p = core // 4, core % 4
        y[b, p * cfg.T:(p + 1) * cfg.T] = res.results[core]["y"]
    return y


# version 09 (final)
# speedup vs baseline: 1.2595x; 1.0233x over previous
from contextlib import ExitStack

import numpy as np
import concourse.bass as bass
import concourse.mybir as mybir

F32 = mybir.dt.float32
BF16 = mybir.dt.bfloat16
AF = mybir.ActivationFunctionType
ALU = mybir.AluOpType
AX = mybir.AxisListType

COMPUTE = ("tensor", "vector", "scalar", "gpsimd")
DMAQ = ("sync_q", "gp_q", "sc_q")
NSLOT = 8
LN_EPS = 1e-5


class Sched:

    def __init__(self, nc, ctx):
        self.nc = nc
        names = list(COMPUTE)
        for q in DMAQ:
            names += [f"{q}{i}" for i in range(NSLOT)]
        self.sems = {n: ctx.enter_context(nc.semaphore("sem_" + n)) for n in names}
        self.count = {n: 0 for n in names}
        self.qidx = {q: 0 for q in DMAQ}
        self.qlast = {q: [None] * NSLOT for q in DMAQ}
        self.ops = []

    def _tok(self, stream, amount):
        self.count[stream] += amount
        return (stream, self.count[stream])

    def op(self, eng, fn, deps=()):
        tok = self._tok(eng, 1)
        self.ops.append((eng, eng, fn, tuple(d for d in deps if d), 1))
        return tok

    def dma(self, queue, out, in_, deps=(), **kw):
        q = {"sync": "sync_q", "gp": "gp_q", "sc": "sc_q"}[queue]
        body = {"sync": "sync", "gp": "gpsimd", "sc": "scalar"}[queue]
        slot = self.qidx[q] % NSLOT
        self.qidx[q] += 1
        stream = f"{q}{slot}"
        deps = list(d for d in deps if d)
        prev = self.qlast[q][slot]
        if prev is not None:
            deps.append(prev)
        tok = self._tok(stream, 16)
        self.qlast[q][slot] = tok

        def fn(e):
            return e.dma_start(out=out, in_=in_, **kw)

        self.ops.append((body, stream, fn, tuple(deps), 16))
        return tok

    def frontier(self):
        return [(s, v) for s, v in self.count.items() if v > 0]

    def qfrontier(self, q):
        return [t for t in self.qlast[q] if t is not None]

    def build(self, blk):
        bodies = {}
        for body, stream, fn, deps, inc in self.ops:
            bodies.setdefault(body, []).append((stream, fn, deps, inc))
        if "sync" not in bodies:
            bodies["sync"] = []
        final_counts = dict(self.count)

        def make(body_name, items):
            def emit(eng):
                hw = {}
                for stream, fn, deps, inc in items:
                    for (s, v) in deps:
                        if hw.get(s, 0) < v:
                            eng.wait_ge(self.sems[s], v)
                            hw[s] = v
                    fn(eng).then_inc(self.sems[stream], inc)
                if body_name == "sync":
                    for q in self.count:
                        if final_counts[q] > hw.get(q, 0):
                            eng.wait_ge(self.sems[q], final_counts[q])
            return emit

        for body_name, items in bodies.items():
            getattr(blk, body_name)(make(body_name, items))


class CFG:
    def __init__(self, n_exp=16, inter=1792, t_own=2048, t_prior=6144):
        self.D = 2048
        self.NKD = 16
        self.E = n_exp
        self.I = inter
        self.NKI = inter // 128
        self.T = t_own
        self.TP = t_prior
        self.NT = t_own // 128
        self.NTP = t_prior // 128
        self.TH = t_own // 2
        self.NTB = self.TH // 512
        assert t_own % 1024 == 0 and self.TH % 512 == 0 and inter % 128 == 0


def build_nc(cfg: CFG, debug=False):
    nc = bass.Bass("TRN2", target_bir_lowering=False)
    D, E, I, T, TP = cfg.D, cfg.E, cfg.I, cfg.T, cfg.TP
    NKD, NKI, NT, NTP, TH, NTB = cfg.NKD, cfg.NKI, cfg.NT, cfg.NTP, cfg.TH, cfg.NTB

    xin = nc.dram_tensor("xin", [TP + T, D], F32, kind="ExternalInput").ap()
    logA = nc.dram_tensor("logA", [128], F32, kind="ExternalInput").ap()
    Bm = nc.dram_tensor("Bm", [128, D], F32, kind="ExternalInput").ap()
    Cm = nc.dram_tensor("Cm", [D, 128], F32, kind="ExternalInput").ap()
    Dv = nc.dram_tensor("Dv", [D], F32, kind="ExternalInput").ap()
    lnw = nc.dram_tensor("lnw", [D], F32, kind="ExternalInput").ap()
    lnb = nc.dram_tensor("lnb", [D], F32, kind="ExternalInput").ap()
    rw = nc.dram_tensor("rw", [E, D], F32, kind="ExternalInput").ap()
    rb = nc.dram_tensor("rb", [E], F32, kind="ExternalInput").ap()
    gpt = nc.dram_tensor("gpt", [E, NKI, 128, NKD, 128], F32, kind="ExternalInput").ap()
    upt = nc.dram_tensor("upt", [E, NKI, 128, NKD, 128], F32, kind="ExternalInput").ap()
    dpt = nc.dram_tensor("dpt", [E, NKI, 128, D], F32, kind="ExternalInput").ap()
    ident_in = nc.dram_tensor("ident", [128, 128], F32, kind="ExternalInput").ap()
    sel_in = nc.dram_tensor("sel", [E, E * 128], F32, kind="ExternalInput").ap()
    y = nc.dram_tensor("y", [T, D], F32, kind="ExternalOutput").ap()
    wfmd = nc.dram_tensor("wfmd", [E, T], BF16, kind="Internal").ap()
    dbg = {}
    if debug:
        dbg["x1_t0"] = nc.dram_tensor("dbg_x1_t0", [128, D], F32, kind="ExternalOutput").ap()
        dbg["hs_t0"] = nc.dram_tensor("dbg_hs_t0", [128, 128], F32, kind="ExternalOutput").ap()
        dbg["z_t0"] = nc.dram_tensor("dbg_z_t0", [128, D], F32, kind="ExternalOutput").ap()
        dbg["x3_t0"] = nc.dram_tensor("dbg_x3_t0", [128, D], F32, kind="ExternalOutput").ap()
        dbg["lg_t0"] = nc.dram_tensor("dbg_lg_t0", [128, E], F32, kind="ExternalOutput").ap()
        dbg["wtm_t0"] = nc.dram_tensor("dbg_wtm_t0", [128, E], F32, kind="ExternalOutput").ap()
        dbg["acc_th0"] = nc.dram_tensor("dbg_acc_th0", [128, NKD * (T // 2)], F32, kind="ExternalOutput").ap()
        dbg["act_e0"] = nc.dram_tensor("dbg_act_e0", [128, NKI * (T // 2)], F32, kind="ExternalOutput").ap()
        dbg["wbc_e0"] = nc.dram_tensor("dbg_wbc_e0", [128, T // 2], F32, kind="ExternalOutput").ap()
        dbg["wfm"] = nc.dram_tensor("dbg_wfm", [E, T], F32, kind="ExternalOutput").ap()
        dbg["selb"] = nc.dram_tensor("dbg_selb", [E, 128], F32, kind="ExternalOutput").ap()
        dbg["pbc"] = nc.dram_tensor("dbg_pbc", [128, 512], F32, kind="ExternalOutput").ap()

    with ExitStack() as ctx:
        en = ctx.enter_context
        sch = Sched(nc, ctx)

        x3T = en(nc.sbuf_tensor("x3T", [128, NKD * T], BF16)).ap()
        gub = en(nc.sbuf_tensor("gub", [128, 6 * NKD * 128], BF16)).ap()
        idf = en(nc.sbuf_tensor("idf", [128, 128], F32)).ap()
        idb = en(nc.sbuf_tensor("idb", [128, 128], BF16)).ap()
        rlhs = en(nc.sbuf_tensor("rlhs", [128, NKD * E], BF16)).ap()
        rbias = en(nc.sbuf_tensor("rbias", [E, 1], F32)).ap()
        pt = [en(nc.psum_tensor("pt0", [128, 512], BF16)).ap(),
              en(nc.psum_tensor("pt1", [128, 512], BF16)).ap()]
        pt_war = {0: [], 1: []}

        CHW = NKD * 128

        def pe_transpose(src_ap, dst_ap, k, deps, fp32=False):
            P, F = src_ap.shape
            bank = k % 2
            idap = (idf if fp32 else idb)[0:P, 0:P]
            pb = pt[bank].bitcast(F32) if fp32 else pt[bank]
            pdst = pb[0:F, 0:P]
            t_t = sch.op("tensor", lambda e: e.transpose(out=pdst, in_=src_ap, identity=idap),
                         list(deps) + pt_war[bank])
            t_e = sch.op("vector", lambda e: e.tensor_copy(out=dst_ap, in_=pdst), [t_t])
            pt_war[bank] = [t_e]
            return t_e

        d_id = sch.dma("sync", idf, ident_in)
        t_idb = sch.op("vector", lambda e: e.tensor_copy(out=idb, in_=idf), [d_id])

        def gu_load(eidx, ic, buf, war):
            tg = sch.dma("gp", gub[:, buf * CHW:(buf + 1) * CHW],
                         gpt[eidx, ic].rearrange("p k i -> p (k i)"), deps=war)
            tu = sch.dma("gp", gub[:, (3 + buf) * CHW:(3 + buf + 1) * CHW],
                         upt[eidx, ic].rearrange("p k i -> p (k i)"), deps=war)
            return (tg, tu)

        gu_seq = [(th, e_i, ic) for th in range(2) for e_i in range(E)
                  for ic in range(NKI)]

        with ExitStack() as actx:
            aen = actx.enter_context
            Blhs = aen(nc.sbuf_tensor("Blhs", [128, NKD * 128], BF16)).ap()
            CT = aen(nc.sbuf_tensor("CT", [128, D], BF16)).ap()
            dbc = aen(nc.sbuf_tensor("dbc", [128, D], BF16)).ap()
            wbcst = aen(nc.sbuf_tensor("wbcst", [128, D], BF16)).ap()
            bbcst = aen(nc.sbuf_tensor("bbcst", [128, D], F32)).ap()
            abc = aen(nc.sbuf_tensor("abc", [128, 128], F32)).ap()
            adv = aen(nc.sbuf_tensor("adv", [128, 4], F32)).ap()
            epsv = aen(nc.sbuf_tensor("epsv", [128, 1], F32)).ap()
            pua = aen(nc.psum_tensor("pua", [128, 512], F32)).ap()
            py = [aen(nc.psum_tensor("py0", [128, 512], F32)).ap(),
                  aen(nc.psum_tensor("py1", [128, 512], F32)).ap()]
            ppy = py[0]

            with ExitStack() as sctx:
                sen = sctx.enter_context
                rowb = sen(nc.sbuf_tensor("rowb", [1, D], F32)).ap()
                ones1 = sen(nc.sbuf_tensor("ones1", [1, 128], F32)).ap()
                stg = sen(nc.sbuf_tensor("stg", [128, 2048], F32)).ap()
                stgb = sen(nc.sbuf_tensor("stgb", [128, 2048], BF16)).ap()

                t_eps = sch.op("vector", lambda e: e.memset(epsv, LN_EPS))
                d_logA = sch.dma("sync", adv[:, 0:1], logA.rearrange("(p one) -> p one", one=1))
                t_rate = sch.op("scalar", lambda e: e.activation(out=adv[:, 1:2], in_=adv[:, 0:1], func=AF.Exp), [d_logA])
                t_nr = sch.op("vector", lambda e: e.tensor_scalar_mul(adv[:, 2:3], adv[:, 1:2], -1.0), [t_rate])
                t_ad = sch.op("scalar", lambda e: e.activation(out=adv[:, 3:4], in_=adv[:, 2:3], func=AF.Exp), [t_nr])
                t_ab0 = sch.op("vector", lambda e: e.memset(abc, 1.0))
                t_ab = sch.op("vector", lambda e: e.tensor_scalar_mul(abc, abc, adv[:, 3:4]), [t_ad, t_ab0])

                t_ones = sch.op("vector", lambda e: e.memset(ones1, 1.0))
                ppy_war = []
                rowb_war = []

                def row_bcast(dst_bf, src_row, add1):
                    nonlocal ppy_war, rowb_war
                    d_r = sch.dma("sync", rowb, src_row.rearrange("(one d) -> one d", one=1),
                                  deps=rowb_war)
                    tlast = None
                    last_mm = None
                    for c in range(D // 512):
                        sl = slice(c * 512, (c + 1) * 512)
                        t_mm = sch.op("tensor", lambda e, sl=sl: e.matmul(
                            out=ppy[:, 0:512], lhsT=ones1, rhs=rowb[0:1, sl],
                            start=True, stop=True), [d_r, t_ones] + ppy_war)
                        last_mm = t_mm
                        if add1:
                            t_ev = sch.op("vector", lambda e, sl=sl: e.tensor_scalar_add(
                                dst_bf[:, sl], ppy[:, 0:512], 1.0), [t_mm])
                        else:
                            t_ev = sch.op("vector", lambda e, sl=sl: e.tensor_copy(
                                out=dst_bf[:, sl], in_=ppy[:, 0:512]), [t_mm])
                        ppy_war = [t_ev]
                        tlast = t_ev
                    rowb_war = [last_mm]
                    return tlast

                t_dbc = row_bcast(dbc, Dv, add1=True)
                t_wb = row_bcast(wbcst, lnw, add1=False)
                t_bb = row_bcast(bbcst, lnb, add1=False)

                d_B = sch.dma("sync", stg, Bm)
                t_Bc = sch.op("vector", lambda e: e.tensor_copy(out=stgb, in_=stg), [d_B, t_idb])
                bl_toks = [pe_transpose(stgb[:, k * 128:(k + 1) * 128], Blhs[:, k * 128:(k + 1) * 128],
                                        k, [t_Bc]) for k in range(NKD)]
                t_Blhs = bl_toks[-1]
                d_C = sch.dma("sync", stg.rearrange("p (k n) -> p k n", k=NKD),
                              Cm.rearrange("(k p) n -> p k n", p=128), deps=[t_Bc])
                t_Cc = sch.op("vector", lambda e: e.tensor_copy(out=stgb, in_=stg), [d_C, t_Blhs])
                ct_toks = [pe_transpose(stgb[:, k * 128:(k + 1) * 128], CT[:, k * 128:(k + 1) * 128],
                                        k, [t_Cc]) for k in range(NKD)]
                t_CT = ct_toks[-1]
                d_rw = sch.dma("sync", stg[0:E, :], rw, deps=[t_Cc])
                t_rwc = sch.op("vector", lambda e: e.tensor_copy(out=stgb[0:E, :], in_=stg[0:E, :]),
                               [d_rw, t_CT])
                rl_toks = [pe_transpose(stgb[0:E, k * 128:(k + 1) * 128], rlhs[:, k * E:(k + 1) * E],
                                        k, [t_rwc]) for k in range(NKD)]
                t_rlhs = rl_toks[-1]
                d_rb = sch.dma("sync", rbias, rb.rearrange("(p one) -> p one", one=1))
                setup_scratch_end = [t_rlhs]

            NXB = 4
            xbuf = aen(nc.sbuf_tensor("xbuf", [128, NXB * D], F32)).ap()
            x1b = aen(nc.sbuf_tensor("x1b", [128, 2 * D], BF16)).ap()
            x1T = aen(nc.sbuf_tensor("x1T", [128, 2 * NKD * 128], BF16)).ap()
            r1b = aen(nc.sbuf_tensor("r1b", [128, 2 * D], F32)).ap()
            zb = aen(nc.sbuf_tensor("zb", [128, 2 * D], F32)).ap()
            hsb = aen(nc.sbuf_tensor("hsb", [128, 2 * 128], F32)).ap()
            hbfb = aen(nc.sbuf_tensor("hbfb", [128, 2 * 128], BF16)).ap()
            bnb = aen(nc.sbuf_tensor("bnb", [128, 6 * 24], F32)).ap()
            stat = aen(nc.sbuf_tensor("stat", [128, 6 * 4], F32)).ap()
            x2b = aen(nc.sbuf_tensor("x2b", [128, 2 * D], BF16)).ap()
            lgf = aen(nc.sbuf_tensor("lgf", [E, 128], F32)).ap()
            lgt = aen(nc.sbuf_tensor("lgt", [128, E], F32)).ap()
            m8 = aen(nc.sbuf_tensor("m8", [128, 8], F32)).ap()
            tp4 = aen(nc.sbuf_tensor("tp4", [128, 2 * E], F32)).ap()
            wtm = aen(nc.sbuf_tensor("wtm", [128, E], F32)).ap()
            wrow = aen(nc.sbuf_tensor("wrow", [E, 128], BF16)).ap()
            x3v = x3T.rearrange("p (k t) -> p k t", k=NKD)

            arena2_war = setup_scratch_end

            gu_toks = {}
            gu_done = {}
            for s in range(3):
                th_s, e_s, ic_s = gu_seq[s]
                gu_toks[s] = gu_load(e_s, ic_s, s % 3, [])

            NGT = NTP + NT
            x_rd = {i: list(arena2_war) for i in range(NXB)}
            x1_rd = {0: list(arena2_war), 1: list(arena2_war)}
            x1T_rd = {0: list(arena2_war), 1: list(arena2_war)}
            r1_rd = {0: list(arena2_war), 1: list(arena2_war)}
            zb_rd = {0: list(arena2_war), 1: list(arena2_war)}
            st_war = {i: [] for i in range(6)}
            st_rd = {i: [] for i in range(6)}
            x2_rd = {0: list(arena2_war), 1: list(arena2_war)}
            hs_rd = {0: [], 1: []}
            hbf_rd = {0: [], 1: []}
            pua_war = {0: [], 1: []}
            py_war = {0: list(ppy_war), 1: []}
            lg_war = []
            router_war = []
            carry = {"tok": None, "valid": False}
            d_x_tok = {}

            def x_load(gt):
                sl = gt % NXB
                xs = xbuf[:, sl * D:(sl + 1) * D]
                d0 = sch.dma("sync", xs[:, 0:D // 2],
                             xin[gt * 128:(gt + 1) * 128, 0:D // 2], deps=x_rd[sl])
                d1 = sch.dma("gp", xs[:, D // 2:D],
                             xin[gt * 128:(gt + 1) * 128, D // 2:D], deps=x_rd[sl])
                x_rd[sl] = []
                d_x_tok[gt] = [d0, d1]

            for gt in range(min(NXB, NGT)):
                x_load(gt)

            def ln_stats(src_ap, par, deps=(), chunk_deps=None):
                bn = bnb[:, par * 24:(par + 1) * 24]
                st = stat[:, par * 4:(par + 1) * 4]
                tks = []
                for c in range(4):
                    dd = list(deps) + (chunk_deps[c] if chunk_deps else [])
                    tks.append(sch.op("vector", lambda e, c=c, bn=bn, src_ap=src_ap: e.bn_stats(
                        out=bn[:, 6 * c:6 * (c + 1)],
                        in_=src_ap[:, 512 * c:512 * (c + 1)]), dd))
                t_ag = sch.op("vector", lambda e, bn=bn, st=st: e.bn_aggr(
                    out=st[:, 0:2], in_=bn), tks + st_war[par] + st_rd[par])
                t_ln = sch.op("scalar", lambda e, st=st: e.activation(
                    out=st[:, 2:3], in_=st[:, 1:2], func=AF.Ln, bias=epsv), [t_ag, t_eps])
                t_rs = sch.op("scalar", lambda e, st=st: e.activation(
                    out=st[:, 2:3], in_=st[:, 2:3], func=AF.Exp, scale=-0.5), [t_ln])
                t_nb = sch.op("vector", lambda e, st=st: e.tensor_scalar(
                    out=st[:, 3:4], in0=st[:, 0:1], scalar1=st[:, 2:3], scalar2=-1.0,
                    op0=ALU.mult, op1=ALU.mult), [t_rs])
                st_war[par] = [t_ln]
                st_rd[par] = []
                return t_nb, st

            def tr16(src2048, dst_evac, deps):
                evs = []
                for b in range(4):
                    bank = b % 2
                    tks = []
                    for j in range(4):
                        k = 4 * b + j
                        dep = list(deps) + (pt_war[bank] if j == 0 else [])
                        tks.append(sch.op("tensor", lambda e, k=k, bank=bank, j=j: e.transpose(
                            out=pt[bank][0:128, j * 128:(j + 1) * 128],
                            in_=src2048[:, k * 128:(k + 1) * 128], identity=idb), dep))
                    t_ev = dst_evac(b, pt[bank][0:128, 0:512], [tks[-1]])
                    pt_war[bank] = [t_ev]
                    evs.append(t_ev)
                return evs

            s1 = {}

            def stage1(gt):
                par = gt % 2
                sl = gt % NXB
                xs = xbuf[:, sl * D:(sl + 1) * D]
                d_x = d_x_tok[gt]
                x1s = x1b[:, par * D:(par + 1) * D]
                x1Ts = x1T[:, par * NKD * 128:(par + 1) * NKD * 128]
                t_nb, st = ln_stats(xs, par * 3, deps=d_x)
                t_x1 = sch.op("scalar", lambda e, xs=xs, x1s=x1s, st=st: e.activation(
                    out=x1s, in_=xs, func=AF.Identity, bias=st[:, 3:4],
                    scale=st[:, 2:3]), [t_nb] + x1_rd[par])
                x1_rd[par] = []
                st_rd[par * 3].append(t_x1)
                x_rd[sl].append(t_x1)
                s1[gt] = (xs, sl, x1s, x1Ts, t_x1)

            s2 = {}

            def stage2a(gt):
                own = gt >= NTP
                tt = gt - NTP
                par = gt % 2
                xs, sl, x1s, x1Ts, t_x1 = s1.pop(gt)

                def x1t_evac(b, pt_ap, dep, par=par, x1Ts=x1Ts):
                    eng = "scalar" if b % 2 == 0 else "vector"
                    dd = dep + (x1T_rd[par] if b < 2 else [])
                    if eng == "scalar":
                        return sch.op(eng, lambda e, b=b, pt_ap=pt_ap: e.activation(
                            out=x1Ts[:, b * 512:(b + 1) * 512], in_=pt_ap,
                            func=AF.Identity), dd)
                    return sch.op(eng, lambda e, b=b, pt_ap=pt_ap: e.tensor_copy(
                        out=x1Ts[:, b * 512:(b + 1) * 512], in_=pt_ap), dd)

                tr_evs = tr16(x1s, x1t_evac, [t_x1])
                x1T_rd[par] = []
                x1_rd[par].append(tr_evs[-1])
                x1_rd[par].append(tr_evs[-2])

                uap = pua[:, par * 128:(par + 1) * 128]

                def u_mm(e, x1Ts=x1Ts, uap=uap):
                    for k in range(NKD):
                        mm = e.matmul(out=uap, lhsT=Blhs[:, k * 128:(k + 1) * 128],
                                      rhs=x1Ts[:, k * 128:(k + 1) * 128],
                                      start=(k == 0), stop=(k == NKD - 1))
                    return mm
                t_umm = sch.op("tensor", u_mm, tr_evs + [t_Blhs] + pua_war[par])
                x1T_rd[par].append(t_umm)
                hs = hsb[:, par * 128:(par + 1) * 128]
                init = hsb[:, (1 - par) * 128 + 127:(1 - par) * 128 + 128] if carry["valid"] else 0.0
                t_scan = sch.op("vector", lambda e, hs=hs, init=init, uap=uap: e.tensor_tensor_scan(
                    out=hs, data0=abc, data1=uap, initial=init,
                    op0=ALU.mult, op1=ALU.add),
                    [t_umm, t_ab] + hs_rd[par] + ([carry["tok"]] if carry["valid"] else []))
                hs_rd[par] = []
                pua_war[par] = [t_scan]
                carry.update(tok=t_scan, valid=True)
                if not own:
                    if gt + NXB < NGT:
                        x_load(gt + NXB)
                    return

                hbs = hbfb[:, par * 128:(par + 1) * 128]
                t_hc = sch.op("vector", lambda e, hs=hs, hbs=hbs: e.tensor_copy(
                    out=hbs, in_=hs), [t_scan] + hbf_rd[par])
                hbf_rd[par] = []
                hs_rd[par].append(t_hc)
                zs = zb[:, par * D:(par + 1) * D]
                t_m1 = sch.op("vector", lambda e, zs=zs, x1s=x1s: e.tensor_tensor(
                    out=zs, in0=x1s, in1=dbc, op=ALU.mult), [t_x1, t_dbc] + zb_rd[par])
                zb_rd[par] = []
                x1_rd[par].append(t_m1)

                zc_toks = []
                for c in range(4):
                    pyb = py[c % 2]
                    t_yc = sch.op("tensor", lambda e, pyb=pyb, c=c, hbs=hbs: e.matmul(
                        out=pyb, lhsT=hbs, rhs=CT[:, c * 512:(c + 1) * 512],
                        start=True, stop=True), [t_hc, t_CT] + py_war[c % 2])
                    t_zc = sch.op("vector", lambda e, pyb=pyb, c=c, zs=zs: e.tensor_add(
                        out=zs[:, c * 512:(c + 1) * 512], in0=zs[:, c * 512:(c + 1) * 512],
                        in1=pyb), [t_yc, t_m1])
                    py_war[c % 2] = [t_zc]
                    zc_toks.append(t_zc)
                hbf_rd[par].append(t_yc)
                t_nb2, st2 = ln_stats(zs, par * 3 + 1, chunk_deps=[[t] for t in zc_toks])
                x2s = x2b[:, par * D:(par + 1) * D]
                t_nrm = sch.op("scalar", lambda e, zs=zs, x2s=x2s, st2=st2: e.activation(
                    out=x2s, in_=zs, func=AF.Identity, bias=st2[:, 3:4], scale=st2[:, 2:3]),
                    [t_nb2] + x2_rd[par])
                x2_rd[par] = []
                st_rd[par * 3 + 1].append(t_nrm)
                zb_rd[par].append(t_nrm)
                s2[gt] = (xs, sl, x2s, t_nrm)

            def stage2b(gt):
                nonlocal lg_war, router_war
                tt = gt - NTP
                par = gt % 2
                xs, sl, x2s, t_nrm = s2.pop(gt)
                r1 = r1b[:, par * D:(par + 1) * D]
                t_t2 = sch.op("gpsimd", lambda e, r1=r1, x2s=x2s: e.tensor_tensor(
                    out=r1, in0=x2s, in1=wbcst, op=ALU.mult), [t_nrm, t_wb] + r1_rd[par])
                r1_rd[par] = []
                x2_rd[par].append(t_t2)
                t_t3 = sch.op("gpsimd", lambda e, r1=r1: e.tensor_add(
                    out=r1, in0=r1, in1=bbcst), [t_t2, t_bb])
                t_r1 = sch.op("gpsimd", lambda e, r1=r1, xs=xs: e.tensor_add(
                    out=r1, in0=r1, in1=xs), [t_t3])
                x_rd[sl].append(t_r1)
                if gt + NXB < NGT:
                    x_load(gt + NXB)
                d_r1 = sch.dma("sync", y[tt * 128:(tt + 1) * 128, :], r1, deps=[t_r1])
                r1_rd[par].append(d_r1)
                t_nb3, st3 = ln_stats(r1, par * 3 + 2, deps=[t_r1])
                t_x3 = sch.op("scalar", lambda e, r1=r1, x2s=x2s, st3=st3: e.activation(
                    out=x2s, in_=r1, func=AF.Identity, bias=st3[:, 3:4],
                    scale=st3[:, 2:3]), [t_nb3] + x2_rd[par])
                x2_rd[par] = []
                st_rd[par * 3 + 2].append(t_x3)
                r1_rd[par].append(t_x3)

                def x3_evac(b, pt_ap, dep, tt=tt):
                    eng = "scalar" if b % 2 == 0 else "vector"
                    dst = x3v[:, b * 4:(b + 1) * 4, tt * 128:(tt + 1) * 128]
                    srcv = pt_ap.rearrange("p (j t) -> p j t", j=4)
                    if eng == "scalar":
                        return sch.op(eng, lambda e, dst=dst, srcv=srcv: e.activation(
                            out=dst, in_=srcv, func=AF.Identity), dep)
                    return sch.op(eng, lambda e, dst=dst, srcv=srcv: e.tensor_copy(
                        out=dst, in_=srcv), dep)

                tr3_evs = tr16(x2s, x3_evac, [t_x3])
                x2_rd[par] += [tr3_evs[-1], tr3_evs[-2]]

                def lg_mm(e, tt=tt):
                    for k in range(NKD):
                        mm = e.matmul(out=pua[0:E, 384:384 + 128], lhsT=rlhs[:, k * E:(k + 1) * E],
                                      rhs=x3T[:, k * T + tt * 128:k * T + tt * 128 + 128],
                                      start=(k == 0), stop=(k == NKD - 1))
                    return mm
                t_lg = sch.op("tensor", lg_mm, tr3_evs + [t_rlhs] + lg_war + router_war)
                t_lge = sch.op("vector", lambda e: e.tensor_scalar(
                    out=lgf, in0=pua[0:E, 384:384 + 128], scalar1=rbias, scalar2=None,
                    op0=ALU.add), [t_lg, d_rb])
                lg_war = [t_lge]
                t_lgte = pe_transpose(lgf, lgt, 0, [t_lge] + router_war, fp32=True)
                t_m8 = sch.op("vector", lambda e: e.max(out=m8, in_=lgt), [t_lgte])
                t_nmx = sch.op("gpsimd", lambda e: e.tensor_scalar_mul(m8[:, 7:8], m8[:, 0:1], -1.0),
                               [t_m8])
                t_ex = sch.op("scalar", lambda e: e.activation(
                    out=tp4[:, 0:E], in_=lgt, func=AF.Exp, bias=m8[:, 7:8]), [t_nmx])
                t_mk = sch.op("gpsimd", lambda e: e.tensor_scalar(
                    out=tp4[:, E:2 * E], in0=lgt, scalar1=m8[:, 3:4], scalar2=None,
                    op0=ALU.is_ge), [t_m8, t_ex])
                t_we = sch.op("gpsimd", lambda e: e.tensor_mul(
                    out=tp4[:, 0:E], in0=tp4[:, 0:E], in1=tp4[:, E:2 * E]), [t_ex, t_mk])
                t_sm = sch.op("vector", lambda e: e.reduce_sum(out=m8[:, 6:7], in_=tp4[:, 0:E],
                                                               axis=AX.X), [t_we])
                t_rc = sch.op("vector", lambda e: e.reciprocal(out=m8[:, 6:7], in_=m8[:, 6:7]),
                              [t_sm])
                t_wt = sch.op("gpsimd", lambda e: e.tensor_scalar_mul(wtm, tp4[:, 0:E], m8[:, 6:7]),
                              [t_rc])
                t_wT = pe_transpose(wtm, wrow, 1, [t_wt, t_lgte] + router_war[2:3],
                                    fp32=True)
                d_wfm = sch.dma("sync", wfmd[:, tt * 128:(tt + 1) * 128], wrow, deps=[t_wT])
                router_war = [t_ex, t_wT, d_wfm]

            for gt in range(NGT):
                stage1(gt)
                if gt >= 1:
                    stage2a(gt - 1)
                if gt >= 2 and gt - 2 >= NTP:
                    stage2b(gt - 2)
            stage2a(NGT - 1)
            stage2b(NGT - 2)
            stage2b(NGT - 1)

            aend = sch.frontier()

        NKH = NKI // 2
        with ExitStack() as fctx:
            fen = fctx.enter_context
            dwn = fen(nc.sbuf_tensor("dwn", [128, NKI * D], BF16)).ap()
            ostg = fen(nc.sbuf_tensor("ostg", [128, 2 * (D // 4)], F32)).ap()
            wbcf = fen(nc.sbuf_tensor("wbcf", [128, 2 * TH], BF16)).ap()
            sgb = fen(nc.sbuf_tensor("sgb", [128, 2 * 512], BF16)).ap()
            sub = fen(nc.sbuf_tensor("sub", [128, 2 * 512], BF16)).ap()
            actb = fen(nc.sbuf_tensor("actb", [128, NKH * TH], BF16)).ap()
            accb0 = fen(nc.sbuf_tensor("accb0", [128, (NKD // 2) * TH], BF16)).ap()
            accb1 = fen(nc.sbuf_tensor("accb1", [128, (NKD // 2) * TH], BF16)).ap()

            def acc_ap(dt, lo, hi):
                b = accb0 if dt < NKD // 2 else accb1
                return b[:, (dt % (NKD // 2)) * TH + lo:(dt % (NKD // 2)) * TH + hi]
            pg = [fen(nc.psum_tensor(f"pg{i}", [128, 512], F32)).ap() for i in range(NTB)]
            pu2 = [fen(nc.psum_tensor(f"pu{i}", [128, 512], F32)).ap() for i in range(NTB)]
            pd = [fen(nc.psum_tensor(f"pd{i}", [128, 512], F32)).ap() for i in range(2)]

            def dwn_load_ki(eidx, ki, war):
                return sch.dma("gp", dwn[:, ki * D:(ki + 1) * D], dpt[eidx, ki], deps=war)

            pd_war = {0: aend, 1: aend}
            pg_war = {i: aend for i in range(NTB)}
            pu_war = {i: aend for i in range(NTB)}
            sg_war = {i: list(aend) for i in range(NTB)}
            su_war = {i: list(aend) for i in range(NTB)}
            act_war = list(aend)
            acc_war = list(aend)
            ostg_war = {0: [], 1: []}
            d_dwn = [dwn_load_ki(0, ki, aend) for ki in range(NKI)]
            dwn_war = []
            dwn_pend = None
            last_act = None
            last_down = None
            seq = 0
            NSEQ = len(gu_seq)
            wbc_rd = {0: list(aend), 1: list(aend)}
            d_wbc_tok = {(0, 0): sch.dma("sync", wbcf[:, 0:TH],
                                         wfmd[0:1, 0:TH].to_broadcast([128, TH]),
                                         deps=list(aend))}

            for th in range(2):
                th0 = th * TH
                acc_tok = {}
                for e_i in range(E):
                    ne, nth = e_i + 1, th
                    if ne == E:
                        ne, nth = 0, th + 1
                    if nth < 2:
                        buf = ne % 2
                        d_wbc_tok[(nth, ne)] = sch.dma(
                            "sync", wbcf[:, buf * TH:(buf + 1) * TH],
                            wfmd[ne:ne + 1, nth * TH:nth * TH + TH].to_broadcast([128, TH]),
                            deps=wbc_rd[buf])
                        wbc_rd[buf] = []
                    d_wbc = d_wbc_tok[(th, e_i)]
                    wslot = (e_i % 2) * TH
                    for ih in range(2):
                        ic0 = ih * NKH
                        for kk in range(NKH):
                            ic = ic0 + kk
                            if dwn_pend is not None and e_i == dwn_pend:
                                d_dwn.append(dwn_load_ki(e_i, ic, dwn_war))
                            cur = gu_toks[seq]
                            curbuf = seq % 3
                            g_ap = gub[:, curbuf * CHW:(curbuf + 1) * CHW]
                            u_ap = gub[:, (3 + curbuf) * CHW:(3 + curbuf + 1) * CHW]

                            def bmm(e, w_ap, pbank, tb, th0=th0):
                                for k in range(NKD):
                                    mm = e.matmul(
                                        out=pbank, lhsT=w_ap[:, k * 128:(k + 1) * 128],
                                        rhs=x3T[:, k * T + th0 + tb * 512:k * T + th0 + tb * 512 + 512],
                                        start=(k == 0), stop=(k == NKD - 1))
                                return mm

                            t_bank = []
                            for bi in range(4):
                                tb = bi % 2
                                w_ap = g_ap if bi < 2 else u_ap
                                pbank = pg[tb] if bi < 2 else pu2[tb]
                                war = pg_war[tb] if bi < 2 else pu_war[tb]
                                dtk = cur[0] if bi < 2 else cur[1]
                                t_bank.append(sch.op(
                                    "tensor",
                                    lambda e, w_ap=w_ap, pbank=pbank, tb=tb, bmm=bmm: bmm(e, w_ap, pbank, tb),
                                    [dtk] + war))
                            gu_done[seq] = t_bank[3]
                            if seq + 3 < NSEQ:
                                th_n, e_n, ic_n = gu_seq[seq + 3]
                                gu_toks[seq + 3] = gu_load(e_n, ic_n, (seq + 3) % 3,
                                                           [gu_done[seq]])
                            for tb in range(NTB):
                                pgx, pux = pg[tb], pu2[tb]
                                sgs = sgb[:, tb * 512:(tb + 1) * 512]
                                sus = sub[:, tb * 512:(tb + 1) * 512]
                                t_sg = sch.op("scalar", lambda e, pgx=pgx, sgs=sgs: e.activation(
                                    out=sgs, in_=pgx, func=AF.Silu), [t_bank[tb]] + sg_war[tb])
                                t_su = sch.op("vector", lambda e, pux=pux, tb=tb, wslot=wslot, sus=sus: e.tensor_tensor(
                                    out=sus, in0=pux,
                                    in1=wbcf[:, wslot + tb * 512:wslot + (tb + 1) * 512],
                                    op=ALU.mult), [t_bank[2 + tb], d_wbc] + su_war[tb])
                                wbc_rd[e_i % 2].append(t_su)
                                dst = actb[:, kk * TH + tb * 512:kk * TH + (tb + 1) * 512]
                                t_act = sch.op("vector", lambda e, dst=dst, sgs=sgs, sus=sus: e.tensor_mul(
                                    out=dst, in0=sgs, in1=sus), [t_sg, t_su] + act_war)
                                act_war = []
                                sg_war[tb], su_war[tb] = [t_act], [t_act]
                                pg_war[tb] = [t_sg]
                                pu_war[tb] = [t_su]
                                last_act = t_act
                            seq += 1
                        for tb in range(NTB):
                            for dt in range(NKD):
                                pdx = pd[dt % 2]

                                def dmm(e, tb=tb, dt=dt, pdx=pdx, ic0=ic0):
                                    for kk in range(NKH):
                                        ki = ic0 + kk
                                        mm = e.matmul(
                                            out=pdx,
                                            lhsT=dwn[:, ki * D + dt * 128:ki * D + dt * 128 + 128],
                                            rhs=actb[:, kk * TH + tb * 512:kk * TH + (tb + 1) * 512],
                                            start=(kk == 0), stop=(kk == NKH - 1))
                                    return mm

                                t_d = sch.op("tensor", dmm, [last_act] + d_dwn + pd_war[dt % 2])
                                last_down = t_d
                                asl = acc_ap(dt, tb * 512, (tb + 1) * 512)
                                if e_i == 0 and ih == 0:
                                    t_a = sch.op("vector", lambda e, asl=asl, pdx=pdx: e.tensor_copy(
                                        out=asl, in_=pdx), [t_d] + acc_war)
                                else:
                                    t_a = sch.op("vector", lambda e, asl=asl, pdx=pdx: e.tensor_tensor(
                                        out=asl, in0=pdx, in1=asl, op=ALU.add),
                                        [t_d, acc_tok.get((dt, tb))] + acc_war)
                                acc_tok[(dt, tb)] = t_a
                                pd_war[dt % 2] = [t_a]
                        acc_war = []
                        act_war = [last_down]
                    dwn_war = [last_down]
                    dwn_pend = (e_i + 1) if e_i + 1 < E else (0 if th == 0 else None)
                    d_dwn = []
                wo_tr_last = None
                for tt in range(TH // 128):
                    trow = th0 + tt * 128
                    for dh in range(4):
                        bank = dh % 2
                        half = (tt * 4 + dh) % 2
                        etoks = []
                        for j in range(NKD // 4):
                            dt = dh * (NKD // 4) + j
                            src = acc_ap(dt, tt * 128, tt * 128 + 128)
                            dep = [acc_tok[(dt, (tt * 128) // 512)]]
                            if j == 0:
                                dep = dep + pt_war[bank]
                            t_t = sch.op("tensor", lambda e, src=src, bank=bank, j=j: e.transpose(
                                out=pt[bank][0:128, j * 128:(j + 1) * 128], in_=src,
                                identity=idb), dep)
                            etoks.append(t_t)
                            wo_tr_last = t_t
                        osl = ostg[:, half * 512:(half + 1) * 512]
                        t_e = sch.op("vector", lambda e, bank=bank, osl=osl: e.tensor_copy(
                            out=osl, in_=pt[bank][0:128, 0:512]),
                            etoks[-1:] + ostg_war[half])
                        pt_war[bank] = [t_e]
                        d_o = sch.dma("gp", y[trow:trow + 128, dh * (D // 4):(dh + 1) * (D // 4)],
                                      osl, deps=[t_e] + sch.qfrontier("sync_q"),
                                      accum_op=ALU.add)
                        ostg_war[half] = [d_o]
                acc_war = [wo_tr_last]

        blk = en(nc.Block())
        sch.build(blk)
    return nc


from concourse.bass_utils import run_bass_kernel_spmd

_D = 2048


def _pretile_gu(w, NKI, NKD=16):
    E = w.shape[0]
    return np.ascontiguousarray(
        w.reshape(E, NKD, 128, NKI, 128).transpose(0, 3, 2, 1, 4))


def _make_sel(E):
    s = np.zeros((E, E * 128), dtype=np.float32)
    for e in range(E):
        s[e, e * 128:(e + 1) * 128] = 1.0
    return s


def _make_xin(x_batch, p, t_own, t_prior):
    own = x_batch[p * t_own:(p + 1) * t_own]
    prior = x_batch[:p * t_own]
    pad = np.zeros((t_prior - prior.shape[0], _D), dtype=np.float32)
    return np.ascontiguousarray(np.concatenate([pad, prior, own], axis=0))


def kernel(L0_attn_x, L0_attn_log_A_diag, L0_attn_B_mat, L0_attn_C_mat,
           L0_attn_D_vec, L0_attn_ln_w, L0_attn_ln_b, L0_ffn_router_w,
           L0_ffn_expert_bias, L0_ffn_gate_proj, L0_ffn_up_proj,
           L0_ffn_down_proj):
    cfg = CFG(16, 1792, 2048, 6144)
    x = np.ascontiguousarray(np.asarray(L0_attn_x, dtype=np.float32))
    shared = {
        "logA": np.ascontiguousarray(np.asarray(L0_attn_log_A_diag, np.float32)),
        "Bm": np.ascontiguousarray(np.asarray(L0_attn_B_mat, np.float32)),
        "Cm": np.ascontiguousarray(np.asarray(L0_attn_C_mat, np.float32)),
        "Dv": np.ascontiguousarray(np.asarray(L0_attn_D_vec, np.float32)),
        "lnw": np.ascontiguousarray(np.asarray(L0_attn_ln_w, np.float32)),
        "lnb": np.ascontiguousarray(np.asarray(L0_attn_ln_b, np.float32)),
        "rw": np.ascontiguousarray(np.asarray(L0_ffn_router_w, np.float32)),
        "rb": np.ascontiguousarray(np.asarray(L0_ffn_expert_bias, np.float32)),
        "gpt": _pretile_gu(np.asarray(L0_ffn_gate_proj, np.float32), cfg.NKI),
        "upt": _pretile_gu(np.asarray(L0_ffn_up_proj, np.float32), cfg.NKI),
        "dpt": np.ascontiguousarray(
            np.asarray(L0_ffn_down_proj, np.float32).reshape(cfg.E, cfg.NKI, 128, _D)),
        "ident": np.eye(128, dtype=np.float32),
        "sel": _make_sel(cfg.E),
    }
    nc = build_nc(cfg)
    in_maps = []
    for core in range(8):
        b, p = core // 4, core % 4
        m = dict(shared)
        m["xin"] = _make_xin(x[b], p, cfg.T, cfg.TP)
        in_maps.append(m)
    res = run_bass_kernel_spmd(nc, in_maps, core_ids=list(range(8)))
    y = np.empty((2, 8192, _D), dtype=np.float32)
    for core in range(8):
        b, p = core // 4, core % 4
        y[b, p * cfg.T:(p + 1) * cfg.T] = res.results[core]["y"]
    return y


# version 10
# speedup vs baseline: 1.3464x; 1.0691x over previous
from contextlib import ExitStack

import numpy as np
import concourse.bass as bass
import concourse.mybir as mybir

F32 = mybir.dt.float32
BF16 = mybir.dt.bfloat16
AF = mybir.ActivationFunctionType
ALU = mybir.AluOpType
AX = mybir.AxisListType

COMPUTE = ("tensor", "vector", "scalar", "gpsimd")
DMAQ = ("sync_q", "gp_q", "sc_q")
NSLOT = 8
LN_EPS = 1e-5


class Sched:

    def __init__(self, nc, ctx):
        self.nc = nc
        names = list(COMPUTE)
        for q in DMAQ:
            names += [f"{q}{i}" for i in range(NSLOT)]
        self.sems = {n: ctx.enter_context(nc.semaphore("sem_" + n)) for n in names}
        self.count = {n: 0 for n in names}
        self.qidx = {q: 0 for q in DMAQ}
        self.qlast = {q: [None] * NSLOT for q in DMAQ}
        self.ops = []

    def _tok(self, stream, amount):
        self.count[stream] += amount
        return (stream, self.count[stream])

    def op(self, eng, fn, deps=()):
        tok = self._tok(eng, 1)
        self.ops.append((eng, eng, fn, tuple(d for d in deps if d), 1))
        return tok

    def dma(self, queue, out, in_, deps=(), **kw):
        q = {"sync": "sync_q", "gp": "gp_q", "sc": "sc_q"}[queue]
        body = {"sync": "sync", "gp": "gpsimd", "sc": "scalar"}[queue]
        slot = self.qidx[q] % NSLOT
        self.qidx[q] += 1
        stream = f"{q}{slot}"
        deps = list(d for d in deps if d)
        prev = self.qlast[q][slot]
        if prev is not None:
            deps.append(prev)
        tok = self._tok(stream, 16)
        self.qlast[q][slot] = tok

        def fn(e):
            return e.dma_start(out=out, in_=in_, **kw)

        self.ops.append((body, stream, fn, tuple(deps), 16))
        return tok

    def frontier(self):
        return [(s, v) for s, v in self.count.items() if v > 0]

    def qfrontier(self, q):
        return [t for t in self.qlast[q] if t is not None]

    def build(self, blk):
        bodies = {}
        for body, stream, fn, deps, inc in self.ops:
            bodies.setdefault(body, []).append((stream, fn, deps, inc))
        if "sync" not in bodies:
            bodies["sync"] = []
        final_counts = dict(self.count)

        def make(body_name, items):
            def emit(eng):
                hw = {}
                for stream, fn, deps, inc in items:
                    for (s, v) in deps:
                        if hw.get(s, 0) < v:
                            eng.wait_ge(self.sems[s], v)
                            hw[s] = v
                    fn(eng).then_inc(self.sems[stream], inc)
                if body_name == "sync":
                    for q in self.count:
                        if final_counts[q] > hw.get(q, 0):
                            eng.wait_ge(self.sems[q], final_counts[q])
            return emit

        for body_name, items in bodies.items():
            getattr(blk, body_name)(make(body_name, items))


class CFG:
    def __init__(self, n_exp=16, inter=1792, t_own=2048, t_prior=6144):
        self.D = 2048
        self.NKD = 16
        self.E = n_exp
        self.I = inter
        self.NKI = inter // 128
        self.T = t_own
        self.TP = t_prior
        self.NT = t_own // 128
        self.NTP = t_prior // 128
        self.TH = t_own // 2
        self.NTB = self.TH // 512
        assert t_own % 1024 == 0 and self.TH % 512 == 0 and inter % 128 == 0


def build_nc(cfg: CFG, debug=False):
    nc = bass.Bass("TRN2", target_bir_lowering=False)
    D, E, I, T, TP = cfg.D, cfg.E, cfg.I, cfg.T, cfg.TP
    NKD, NKI, NT, NTP, TH, NTB = cfg.NKD, cfg.NKI, cfg.NT, cfg.NTP, cfg.TH, cfg.NTB

    xin = nc.dram_tensor("xin", [TP + T, D], F32, kind="ExternalInput").ap()
    logA = nc.dram_tensor("logA", [128], F32, kind="ExternalInput").ap()
    Bm = nc.dram_tensor("Bm", [128, D], F32, kind="ExternalInput").ap()
    Cm = nc.dram_tensor("Cm", [D, 128], F32, kind="ExternalInput").ap()
    Dv = nc.dram_tensor("Dv", [D], F32, kind="ExternalInput").ap()
    lnw = nc.dram_tensor("lnw", [D], F32, kind="ExternalInput").ap()
    lnb = nc.dram_tensor("lnb", [D], F32, kind="ExternalInput").ap()
    rw = nc.dram_tensor("rw", [E, D], F32, kind="ExternalInput").ap()
    rb = nc.dram_tensor("rb", [E], F32, kind="ExternalInput").ap()
    gpt = nc.dram_tensor("gpt", [E, NKI, 128, NKD, 128], F32, kind="ExternalInput").ap()
    upt = nc.dram_tensor("upt", [E, NKI, 128, NKD, 128], F32, kind="ExternalInput").ap()
    dpt = nc.dram_tensor("dpt", [E, NKI, 128, D], F32, kind="ExternalInput").ap()
    ident_in = nc.dram_tensor("ident", [128, 128], F32, kind="ExternalInput").ap()
    sel_in = nc.dram_tensor("sel", [E, E * 128], F32, kind="ExternalInput").ap()
    y = nc.dram_tensor("y", [T, D], F32, kind="ExternalOutput").ap()
    wfmd = nc.dram_tensor("wfmd", [E, T], BF16, kind="Internal").ap()
    dbg = {}
    if debug:
        dbg["x1_t0"] = nc.dram_tensor("dbg_x1_t0", [128, D], F32, kind="ExternalOutput").ap()
        dbg["hs_t0"] = nc.dram_tensor("dbg_hs_t0", [128, 128], F32, kind="ExternalOutput").ap()
        dbg["z_t0"] = nc.dram_tensor("dbg_z_t0", [128, D], F32, kind="ExternalOutput").ap()
        dbg["x3_t0"] = nc.dram_tensor("dbg_x3_t0", [128, D], F32, kind="ExternalOutput").ap()
        dbg["lg_t0"] = nc.dram_tensor("dbg_lg_t0", [128, E], F32, kind="ExternalOutput").ap()
        dbg["wtm_t0"] = nc.dram_tensor("dbg_wtm_t0", [128, E], F32, kind="ExternalOutput").ap()
        dbg["acc_th0"] = nc.dram_tensor("dbg_acc_th0", [128, NKD * (T // 2)], F32, kind="ExternalOutput").ap()
        dbg["act_e0"] = nc.dram_tensor("dbg_act_e0", [128, NKI * (T // 2)], F32, kind="ExternalOutput").ap()
        dbg["wbc_e0"] = nc.dram_tensor("dbg_wbc_e0", [128, T // 2], F32, kind="ExternalOutput").ap()
        dbg["wfm"] = nc.dram_tensor("dbg_wfm", [E, T], F32, kind="ExternalOutput").ap()
        dbg["selb"] = nc.dram_tensor("dbg_selb", [E, 128], F32, kind="ExternalOutput").ap()
        dbg["pbc"] = nc.dram_tensor("dbg_pbc", [128, 512], F32, kind="ExternalOutput").ap()

    with ExitStack() as ctx:
        en = ctx.enter_context
        sch = Sched(nc, ctx)

        x3T = en(nc.sbuf_tensor("x3T", [128, NKD * T], BF16)).ap()
        gub = en(nc.sbuf_tensor("gub", [128, 6 * NKD * 128], BF16)).ap()
        idf = en(nc.sbuf_tensor("idf", [128, 128], F32)).ap()
        idb = en(nc.sbuf_tensor("idb", [128, 128], BF16)).ap()
        rlhs = en(nc.sbuf_tensor("rlhs", [128, NKD * E], BF16)).ap()
        rbias = en(nc.sbuf_tensor("rbias", [E, 1], F32)).ap()
        pt = [en(nc.psum_tensor("pt0", [128, 512], BF16)).ap(),
              en(nc.psum_tensor("pt1", [128, 512], BF16)).ap()]
        pt_war = {0: [], 1: []}

        CHW = NKD * 128

        def pe_transpose(src_ap, dst_ap, k, deps, fp32=False):
            P, F = src_ap.shape
            bank = k % 2
            idap = (idf if fp32 else idb)[0:P, 0:P]
            pb = pt[bank].bitcast(F32) if fp32 else pt[bank]
            pdst = pb[0:F, 0:P]
            t_t = sch.op("tensor", lambda e: e.transpose(out=pdst, in_=src_ap, identity=idap),
                         list(deps) + pt_war[bank])
            t_e = sch.op("vector", lambda e: e.tensor_copy(out=dst_ap, in_=pdst), [t_t])
            pt_war[bank] = [t_e]
            return t_e

        d_id = sch.dma("sync", idf, ident_in)
        t_idb = sch.op("vector", lambda e: e.tensor_copy(out=idb, in_=idf), [d_id])

        def gu_load(eidx, ic, buf, war):
            tg = sch.dma("gp", gub[:, buf * CHW:(buf + 1) * CHW],
                         gpt[eidx, ic].rearrange("p k i -> p (k i)"), deps=war)
            tu = sch.dma("gp", gub[:, (3 + buf) * CHW:(3 + buf + 1) * CHW],
                         upt[eidx, ic].rearrange("p k i -> p (k i)"), deps=war)
            return (tg, tu)

        gu_seq = [(th, e_i, ic) for th in range(2) for e_i in range(E)
                  for ic in range(NKI)]

        with ExitStack() as actx:
            aen = actx.enter_context
            Blhs = aen(nc.sbuf_tensor("Blhs", [128, NKD * 128], BF16)).ap()
            CT = aen(nc.sbuf_tensor("CT", [128, D], BF16)).ap()
            dbc = aen(nc.sbuf_tensor("dbc", [128, D], BF16)).ap()
            wbcst = aen(nc.sbuf_tensor("wbcst", [128, D], BF16)).ap()
            bbcst = aen(nc.sbuf_tensor("bbcst", [128, D], F32)).ap()
            abc = aen(nc.sbuf_tensor("abc", [128, 128], F32)).ap()
            adv = aen(nc.sbuf_tensor("adv", [128, 4], F32)).ap()
            epsv = aen(nc.sbuf_tensor("epsv", [128, 1], F32)).ap()
            pua = aen(nc.psum_tensor("pua", [128, 512], F32)).ap()
            py = [aen(nc.psum_tensor("py0", [128, 512], F32)).ap(),
                  aen(nc.psum_tensor("py1", [128, 512], F32)).ap()]
            ppy = py[0]

            with ExitStack() as sctx:
                sen = sctx.enter_context
                rowb = sen(nc.sbuf_tensor("rowb", [1, D], F32)).ap()
                ones1 = sen(nc.sbuf_tensor("ones1", [1, 128], F32)).ap()
                stg = sen(nc.sbuf_tensor("stg", [128, 2048], F32)).ap()
                stgb = sen(nc.sbuf_tensor("stgb", [128, 2048], BF16)).ap()

                t_eps = sch.op("vector", lambda e: e.memset(epsv, LN_EPS))
                d_logA = sch.dma("sync", adv[:, 0:1], logA.rearrange("(p one) -> p one", one=1))
                t_rate = sch.op("scalar", lambda e: e.activation(out=adv[:, 1:2], in_=adv[:, 0:1], func=AF.Exp), [d_logA])
                t_nr = sch.op("vector", lambda e: e.tensor_scalar_mul(adv[:, 2:3], adv[:, 1:2], -1.0), [t_rate])
                t_ad = sch.op("scalar", lambda e: e.activation(out=adv[:, 3:4], in_=adv[:, 2:3], func=AF.Exp), [t_nr])
                t_ab0 = sch.op("vector", lambda e: e.memset(abc, 1.0))
                t_ab = sch.op("vector", lambda e: e.tensor_scalar_mul(abc, abc, adv[:, 3:4]), [t_ad, t_ab0])

                t_ones = sch.op("vector", lambda e: e.memset(ones1, 1.0))
                ppy_war = []
                rowb_war = []

                def row_bcast(dst_bf, src_row, add1):
                    nonlocal ppy_war, rowb_war
                    d_r = sch.dma("sync", rowb, src_row.rearrange("(one d) -> one d", one=1),
                                  deps=rowb_war)
                    tlast = None
                    last_mm = None
                    for c in range(D // 512):
                        sl = slice(c * 512, (c + 1) * 512)
                        t_mm = sch.op("tensor", lambda e, sl=sl: e.matmul(
                            out=ppy[:, 0:512], lhsT=ones1, rhs=rowb[0:1, sl],
                            start=True, stop=True), [d_r, t_ones] + ppy_war)
                        last_mm = t_mm
                        if add1:
                            t_ev = sch.op("vector", lambda e, sl=sl: e.tensor_scalar_add(
                                dst_bf[:, sl], ppy[:, 0:512], 1.0), [t_mm])
                        else:
                            t_ev = sch.op("vector", lambda e, sl=sl: e.tensor_copy(
                                out=dst_bf[:, sl], in_=ppy[:, 0:512]), [t_mm])
                        ppy_war = [t_ev]
                        tlast = t_ev
                    rowb_war = [last_mm]
                    return tlast

                t_dbc = row_bcast(dbc, Dv, add1=True)
                t_wb = row_bcast(wbcst, lnw, add1=False)
                t_bb = row_bcast(bbcst, lnb, add1=False)

                d_B = sch.dma("sync", stg, Bm)
                t_Bc = sch.op("vector", lambda e: e.tensor_copy(out=stgb, in_=stg), [d_B, t_idb])
                bl_toks = [pe_transpose(stgb[:, k * 128:(k + 1) * 128], Blhs[:, k * 128:(k + 1) * 128],
                                        k, [t_Bc]) for k in range(NKD)]
                t_Blhs = bl_toks[-1]
                d_C = sch.dma("sync", stg.rearrange("p (k n) -> p k n", k=NKD),
                              Cm.rearrange("(k p) n -> p k n", p=128), deps=[t_Bc])
                t_Cc = sch.op("vector", lambda e: e.tensor_copy(out=stgb, in_=stg), [d_C, t_Blhs])
                ct_toks = [pe_transpose(stgb[:, k * 128:(k + 1) * 128], CT[:, k * 128:(k + 1) * 128],
                                        k, [t_Cc]) for k in range(NKD)]
                t_CT = ct_toks[-1]
                d_rw = sch.dma("sync", stg[0:E, :], rw, deps=[t_Cc])
                t_rwc = sch.op("vector", lambda e: e.tensor_copy(out=stgb[0:E, :], in_=stg[0:E, :]),
                               [d_rw, t_CT])
                rl_toks = [pe_transpose(stgb[0:E, k * 128:(k + 1) * 128], rlhs[:, k * E:(k + 1) * E],
                                        k, [t_rwc]) for k in range(NKD)]
                t_rlhs = rl_toks[-1]
                d_rb = sch.dma("sync", rbias, rb.rearrange("(p one) -> p one", one=1))
                setup_scratch_end = [t_rlhs]

            NXB = 4
            xbuf = aen(nc.sbuf_tensor("xbuf", [128, NXB * D], F32)).ap()
            x1b = aen(nc.sbuf_tensor("x1b", [128, 3 * D], BF16)).ap()
            x1T = aen(nc.sbuf_tensor("x1T", [128, 2 * NKD * 128], BF16)).ap()
            r1b = aen(nc.sbuf_tensor("r1b", [128, 2 * D], F32)).ap()
            zb = aen(nc.sbuf_tensor("zb", [128, D], F32)).ap()
            hsb = aen(nc.sbuf_tensor("hsb", [128, 2 * 128], F32)).ap()
            hbfb = aen(nc.sbuf_tensor("hbfb", [128, 2 * 128], BF16)).ap()
            bnb = aen(nc.sbuf_tensor("bnb", [128, 7 * 24], F32)).ap()
            stat = aen(nc.sbuf_tensor("stat", [128, 7 * 4], F32)).ap()
            x2b = aen(nc.sbuf_tensor("x2b", [128, 2 * D], BF16)).ap()
            lgf = aen(nc.sbuf_tensor("lgf", [E, 128], F32)).ap()
            lgt = aen(nc.sbuf_tensor("lgt", [128, E], F32)).ap()
            m8 = aen(nc.sbuf_tensor("m8", [128, 8], F32)).ap()
            tp4 = aen(nc.sbuf_tensor("tp4", [128, 2 * E], F32)).ap()
            wtm = aen(nc.sbuf_tensor("wtm", [128, E], F32)).ap()
            wrow = aen(nc.sbuf_tensor("wrow", [E, 128], BF16)).ap()
            x3v = x3T.rearrange("p (k t) -> p k t", k=NKD)

            arena2_war = setup_scratch_end

            gu_toks = {}
            gu_done = {}
            for s in range(3):
                th_s, e_s, ic_s = gu_seq[s]
                gu_toks[s] = gu_load(e_s, ic_s, s % 3, [])

            NGT = NTP + NT
            x_rd = {i: list(arena2_war) for i in range(NXB)}
            x1_rd = {i: list(arena2_war) for i in range(3)}
            x1T_rd = {0: list(arena2_war), 1: list(arena2_war)}
            r1_rd = {0: list(arena2_war), 1: list(arena2_war)}
            zb_rd = list(arena2_war)
            st_war = {i: [] for i in range(7)}
            st_rd = {i: [] for i in range(7)}
            x2_rd = {0: list(arena2_war), 1: list(arena2_war)}
            hs_rd = {0: [], 1: []}
            hbf_rd = {0: [], 1: []}
            pua_war = {0: [], 1: []}
            py_war = {0: list(ppy_war), 1: []}
            lg_war = []
            router_war = []
            carry = {"tok": None, "valid": False}
            d_x_tok = {}

            def x_load(gt):
                sl = gt % NXB
                xs = xbuf[:, sl * D:(sl + 1) * D]
                d0 = sch.dma("sync", xs[:, 0:D // 2],
                             xin[gt * 128:(gt + 1) * 128, 0:D // 2], deps=x_rd[sl])
                d1 = sch.dma("gp", xs[:, D // 2:D],
                             xin[gt * 128:(gt + 1) * 128, D // 2:D], deps=x_rd[sl])
                x_rd[sl] = []
                d_x_tok[gt] = [d0, d1]

            for gt in range(min(NXB, NGT)):
                x_load(gt)

            def ln_stats(src_ap, par, deps=(), chunk_deps=None):
                bn = bnb[:, par * 24:(par + 1) * 24]
                st = stat[:, par * 4:(par + 1) * 4]
                tks = []
                for c in range(4):
                    dd = list(deps) + (chunk_deps[c] if chunk_deps else [])
                    tks.append(sch.op("vector", lambda e, c=c, bn=bn, src_ap=src_ap: e.bn_stats(
                        out=bn[:, 6 * c:6 * (c + 1)],
                        in_=src_ap[:, 512 * c:512 * (c + 1)]), dd))
                t_ag = sch.op("vector", lambda e, bn=bn, st=st: e.bn_aggr(
                    out=st[:, 0:2], in_=bn), tks + st_war[par] + st_rd[par])
                t_ln = sch.op("scalar", lambda e, st=st: e.activation(
                    out=st[:, 2:3], in_=st[:, 1:2], func=AF.Ln, bias=epsv), [t_ag, t_eps])
                t_rs = sch.op("scalar", lambda e, st=st: e.activation(
                    out=st[:, 2:3], in_=st[:, 2:3], func=AF.Exp, scale=-0.5), [t_ln])
                t_nb = sch.op("vector", lambda e, st=st: e.tensor_scalar(
                    out=st[:, 3:4], in0=st[:, 0:1], scalar1=st[:, 2:3], scalar2=-1.0,
                    op0=ALU.mult, op1=ALU.mult), [t_rs])
                st_war[par] = [t_ln]
                st_rd[par] = []
                return t_nb, st

            def tr16(src2048, dst_evac, deps):
                evs = []
                for b in range(4):
                    bank = b % 2
                    tks = []
                    for j in range(4):
                        k = 4 * b + j
                        dep = list(deps) + (pt_war[bank] if j == 0 else [])
                        tks.append(sch.op("tensor", lambda e, k=k, bank=bank, j=j: e.transpose(
                            out=pt[bank][0:128, j * 128:(j + 1) * 128],
                            in_=src2048[:, k * 128:(k + 1) * 128], identity=idb), dep))
                    t_ev = dst_evac(b, pt[bank][0:128, 0:512], [tks[-1]])
                    pt_war[bank] = [t_ev]
                    evs.append(t_ev)
                return evs

            s1 = {}

            def stage1(gt):
                par = gt % 2
                sl = gt % NXB
                xs = xbuf[:, sl * D:(sl + 1) * D]
                d_x = d_x_tok[gt]
                sl3 = gt % 3
                x1s = x1b[:, sl3 * D:(sl3 + 1) * D]
                x1Ts = x1T[:, par * NKD * 128:(par + 1) * NKD * 128]
                t_nb, st = ln_stats(xs, sl3, deps=d_x)
                t_x1 = sch.op("scalar", lambda e, xs=xs, x1s=x1s, st=st: e.activation(
                    out=x1s, in_=xs, func=AF.Identity, bias=st[:, 3:4],
                    scale=st[:, 2:3]), [t_nb] + x1_rd[sl3])
                x1_rd[sl3] = []
                st_rd[sl3].append(t_x1)
                x_rd[sl].append(t_x1)
                s1[gt] = (xs, sl, sl3, x1s, x1Ts, t_x1)

            s2 = {}

            def stage2a(gt):
                nonlocal zb_rd
                own = gt >= NTP
                tt = gt - NTP
                par = gt % 2
                xs, sl, sl3, x1s, x1Ts, t_x1 = s1.pop(gt)

                def x1t_evac(b, pt_ap, dep, par=par, x1Ts=x1Ts):
                    eng = "scalar" if b % 2 == 0 else "vector"
                    dd = dep + (x1T_rd[par] if b < 2 else [])
                    if eng == "scalar":
                        return sch.op(eng, lambda e, b=b, pt_ap=pt_ap: e.activation(
                            out=x1Ts[:, b * 512:(b + 1) * 512], in_=pt_ap,
                            func=AF.Identity), dd)
                    return sch.op(eng, lambda e, b=b, pt_ap=pt_ap: e.tensor_copy(
                        out=x1Ts[:, b * 512:(b + 1) * 512], in_=pt_ap), dd)

                tr_evs = tr16(x1s, x1t_evac, [t_x1])
                x1T_rd[par] = []
                x1_rd[sl3].append(tr_evs[-1])
                x1_rd[sl3].append(tr_evs[-2])

                uap = pua[:, par * 128:(par + 1) * 128]

                def u_mm(e, x1Ts=x1Ts, uap=uap):
                    for k in range(NKD):
                        mm = e.matmul(out=uap, lhsT=Blhs[:, k * 128:(k + 1) * 128],
                                      rhs=x1Ts[:, k * 128:(k + 1) * 128],
                                      start=(k == 0), stop=(k == NKD - 1))
                    return mm
                t_umm = sch.op("tensor", u_mm, tr_evs + [t_Blhs] + pua_war[par])
                x1T_rd[par].append(t_umm)
                hs = hsb[:, par * 128:(par + 1) * 128]
                init = hsb[:, (1 - par) * 128 + 127:(1 - par) * 128 + 128] if carry["valid"] else 0.0
                t_scan = sch.op("vector", lambda e, hs=hs, init=init, uap=uap: e.tensor_tensor_scan(
                    out=hs, data0=abc, data1=uap, initial=init,
                    op0=ALU.mult, op1=ALU.add),
                    [t_umm, t_ab] + hs_rd[par] + ([carry["tok"]] if carry["valid"] else []))
                hs_rd[par] = []
                pua_war[par] = [t_scan]
                carry.update(tok=t_scan, valid=True)
                if not own:
                    if gt + NXB < NGT:
                        x_load(gt + NXB)
                    return

                hbs = hbfb[:, par * 128:(par + 1) * 128]
                t_hc = sch.op("vector", lambda e, hs=hs, hbs=hbs: e.tensor_copy(
                    out=hbs, in_=hs), [t_scan] + hbf_rd[par])
                hbf_rd[par] = []
                hs_rd[par].append(t_hc)
                zs = zb
                t_m1 = sch.op("vector", lambda e, zs=zs, x1s=x1s: e.tensor_tensor(
                    out=zs, in0=x1s, in1=dbc, op=ALU.mult), [t_x1, t_dbc] + zb_rd)
                zb_rd = []
                x1_rd[sl3].append(t_m1)

                zc_toks = []
                for c in range(4):
                    pyb = py[c % 2]
                    t_yc = sch.op("tensor", lambda e, pyb=pyb, c=c, hbs=hbs: e.matmul(
                        out=pyb, lhsT=hbs, rhs=CT[:, c * 512:(c + 1) * 512],
                        start=True, stop=True), [t_hc, t_CT] + py_war[c % 2])
                    t_zc = sch.op("vector", lambda e, pyb=pyb, c=c, zs=zs: e.tensor_add(
                        out=zs[:, c * 512:(c + 1) * 512], in0=zs[:, c * 512:(c + 1) * 512],
                        in1=pyb), [t_yc, t_m1])
                    py_war[c % 2] = [t_zc]
                    zc_toks.append(t_zc)
                hbf_rd[par].append(t_yc)
                t_nb2, st2 = ln_stats(zs, 3 + par, chunk_deps=[[t] for t in zc_toks])
                x2s = x2b[:, par * D:(par + 1) * D]
                t_nrm = sch.op("scalar", lambda e, zs=zs, x2s=x2s, st2=st2: e.activation(
                    out=x2s, in_=zs, func=AF.Identity, bias=st2[:, 3:4], scale=st2[:, 2:3]),
                    [t_nb2] + x2_rd[par])
                x2_rd[par] = []
                st_rd[3 + par].append(t_nrm)
                zb_rd.append(t_nrm)
                s2[gt] = (xs, sl, x2s, t_nrm)

            def stage2b(gt):
                nonlocal lg_war, router_war
                tt = gt - NTP
                par = gt % 2
                xs, sl, x2s, t_nrm = s2.pop(gt)
                r1 = r1b[:, par * D:(par + 1) * D]
                t_t2 = sch.op("gpsimd", lambda e, r1=r1, x2s=x2s: e.tensor_tensor(
                    out=r1, in0=x2s, in1=wbcst, op=ALU.mult), [t_nrm, t_wb] + r1_rd[par])
                r1_rd[par] = []
                x2_rd[par].append(t_t2)
                t_t3 = sch.op("gpsimd", lambda e, r1=r1: e.tensor_add(
                    out=r1, in0=r1, in1=bbcst), [t_t2, t_bb])
                t_r1 = sch.op("gpsimd", lambda e, r1=r1, xs=xs: e.tensor_add(
                    out=r1, in0=r1, in1=xs), [t_t3])
                x_rd[sl].append(t_r1)
                if gt + NXB < NGT:
                    x_load(gt + NXB)
                d_r1 = sch.dma("sync", y[tt * 128:(tt + 1) * 128, :], r1, deps=[t_r1])
                r1_rd[par].append(d_r1)
                t_nb3, st3 = ln_stats(r1, 5 + par, deps=[t_r1])
                t_x3 = sch.op("scalar", lambda e, r1=r1, x2s=x2s, st3=st3: e.activation(
                    out=x2s, in_=r1, func=AF.Identity, bias=st3[:, 3:4],
                    scale=st3[:, 2:3]), [t_nb3] + x2_rd[par])
                x2_rd[par] = []
                st_rd[5 + par].append(t_x3)
                r1_rd[par].append(t_x3)

                def x3_evac(b, pt_ap, dep, tt=tt):
                    eng = "scalar" if b % 2 == 0 else "vector"
                    dst = x3v[:, b * 4:(b + 1) * 4, tt * 128:(tt + 1) * 128]
                    srcv = pt_ap.rearrange("p (j t) -> p j t", j=4)
                    if eng == "scalar":
                        return sch.op(eng, lambda e, dst=dst, srcv=srcv: e.activation(
                            out=dst, in_=srcv, func=AF.Identity), dep)
                    return sch.op(eng, lambda e, dst=dst, srcv=srcv: e.tensor_copy(
                        out=dst, in_=srcv), dep)

                tr3_evs = tr16(x2s, x3_evac, [t_x3])
                x2_rd[par] += [tr3_evs[-1], tr3_evs[-2]]

                def lg_mm(e, tt=tt):
                    for k in range(NKD):
                        mm = e.matmul(out=pua[0:E, 384:384 + 128], lhsT=rlhs[:, k * E:(k + 1) * E],
                                      rhs=x3T[:, k * T + tt * 128:k * T + tt * 128 + 128],
                                      start=(k == 0), stop=(k == NKD - 1))
                    return mm
                t_lg = sch.op("tensor", lg_mm, tr3_evs + [t_rlhs] + lg_war + router_war)
                t_lge = sch.op("vector", lambda e: e.tensor_scalar(
                    out=lgf, in0=pua[0:E, 384:384 + 128], scalar1=rbias, scalar2=None,
                    op0=ALU.add), [t_lg, d_rb])
                lg_war = [t_lge]
                t_lgte = pe_transpose(lgf, lgt, 0, [t_lge] + router_war, fp32=True)
                t_m8 = sch.op("vector", lambda e: e.max(out=m8, in_=lgt), [t_lgte])
                t_nmx = sch.op("gpsimd", lambda e: e.tensor_scalar_mul(m8[:, 7:8], m8[:, 0:1], -1.0),
                               [t_m8])
                t_ex = sch.op("scalar", lambda e: e.activation(
                    out=tp4[:, 0:E], in_=lgt, func=AF.Exp, bias=m8[:, 7:8]), [t_nmx])
                t_mk = sch.op("gpsimd", lambda e: e.tensor_scalar(
                    out=tp4[:, E:2 * E], in0=lgt, scalar1=m8[:, 3:4], scalar2=None,
                    op0=ALU.is_ge), [t_m8, t_ex])
                t_we = sch.op("gpsimd", lambda e: e.tensor_mul(
                    out=tp4[:, 0:E], in0=tp4[:, 0:E], in1=tp4[:, E:2 * E]), [t_ex, t_mk])
                t_sm = sch.op("vector", lambda e: e.reduce_sum(out=m8[:, 6:7], in_=tp4[:, 0:E],
                                                               axis=AX.X), [t_we])
                t_rc = sch.op("vector", lambda e: e.reciprocal(out=m8[:, 6:7], in_=m8[:, 6:7]),
                              [t_sm])
                t_wt = sch.op("gpsimd", lambda e: e.tensor_scalar_mul(wtm, tp4[:, 0:E], m8[:, 6:7]),
                              [t_rc])
                t_wT = pe_transpose(wtm, wrow, 1, [t_wt, t_lgte] + router_war[2:3],
                                    fp32=True)
                d_wfm = sch.dma("sync", wfmd[:, tt * 128:(tt + 1) * 128], wrow, deps=[t_wT])
                router_war = [t_ex, t_wT, d_wfm]

            stage1(0)
            if NGT > 1:
                stage1(1)
            for gt in range(NGT):
                if gt + 2 < NGT:
                    stage1(gt + 2)
                stage2a(gt)
                if gt >= 1 and gt - 1 >= NTP:
                    stage2b(gt - 1)
            stage2b(NGT - 1)

            aend = sch.frontier()

        NKH = NKI // 2
        with ExitStack() as fctx:
            fen = fctx.enter_context
            dwn = fen(nc.sbuf_tensor("dwn", [128, NKI * D], BF16)).ap()
            ostg = fen(nc.sbuf_tensor("ostg", [128, 2 * (D // 4)], F32)).ap()
            wbcf = fen(nc.sbuf_tensor("wbcf", [128, 2 * TH], BF16)).ap()
            sgb = fen(nc.sbuf_tensor("sgb", [128, 2 * 512], BF16)).ap()
            sub = fen(nc.sbuf_tensor("sub", [128, 2 * 512], BF16)).ap()
            actb = fen(nc.sbuf_tensor("actb", [128, NKH * TH], BF16)).ap()
            accb0 = fen(nc.sbuf_tensor("accb0", [128, (NKD // 2) * TH], BF16)).ap()
            accb1 = fen(nc.sbuf_tensor("accb1", [128, (NKD // 2) * TH], BF16)).ap()

            def acc_ap(dt, lo, hi):
                b = accb0 if dt < NKD // 2 else accb1
                return b[:, (dt % (NKD // 2)) * TH + lo:(dt % (NKD // 2)) * TH + hi]
            pg = [fen(nc.psum_tensor(f"pg{i}", [128, 512], F32)).ap() for i in range(NTB)]
            pu2 = [fen(nc.psum_tensor(f"pu{i}", [128, 512], F32)).ap() for i in range(NTB)]
            pd = [fen(nc.psum_tensor(f"pd{i}", [128, 512], F32)).ap() for i in range(2)]

            def dwn_load_ki(eidx, ki, war):
                return sch.dma("gp", dwn[:, ki * D:(ki + 1) * D], dpt[eidx, ki], deps=war)

            pd_war = {0: aend, 1: aend}
            pg_war = {i: aend for i in range(NTB)}
            pu_war = {i: aend for i in range(NTB)}
            sg_war = {i: list(aend) for i in range(NTB)}
            su_war = {i: list(aend) for i in range(NTB)}
            act_war = list(aend)
            acc_war = list(aend)
            ostg_war = {0: [], 1: []}
            d_dwn = [dwn_load_ki(0, ki, aend) for ki in range(NKI)]
            dwn_war = []
            dwn_pend = None
            last_act = None
            last_down = None
            seq = 0
            NSEQ = len(gu_seq)
            wbc_rd = {0: list(aend), 1: list(aend)}
            d_wbc_tok = {(0, 0): sch.dma("sync", wbcf[:, 0:TH],
                                         wfmd[0:1, 0:TH].to_broadcast([128, TH]),
                                         deps=list(aend))}

            for th in range(2):
                th0 = th * TH
                acc_tok = {}
                for e_i in range(E):
                    ne, nth = e_i + 1, th
                    if ne == E:
                        ne, nth = 0, th + 1
                    if nth < 2:
                        buf = ne % 2
                        d_wbc_tok[(nth, ne)] = sch.dma(
                            "sync", wbcf[:, buf * TH:(buf + 1) * TH],
                            wfmd[ne:ne + 1, nth * TH:nth * TH + TH].to_broadcast([128, TH]),
                            deps=wbc_rd[buf])
                        wbc_rd[buf] = []
                    d_wbc = d_wbc_tok[(th, e_i)]
                    wslot = (e_i % 2) * TH
                    for ih in range(2):
                        ic0 = ih * NKH
                        for kk in range(NKH):
                            ic = ic0 + kk
                            if dwn_pend is not None and e_i == dwn_pend:
                                d_dwn.append(dwn_load_ki(e_i, ic, dwn_war))
                            cur = gu_toks[seq]
                            curbuf = seq % 3
                            g_ap = gub[:, curbuf * CHW:(curbuf + 1) * CHW]
                            u_ap = gub[:, (3 + curbuf) * CHW:(3 + curbuf + 1) * CHW]

                            def bmm(e, w_ap, pbank, tb, th0=th0):
                                for k in range(NKD):
                                    mm = e.matmul(
                                        out=pbank, lhsT=w_ap[:, k * 128:(k + 1) * 128],
                                        rhs=x3T[:, k * T + th0 + tb * 512:k * T + th0 + tb * 512 + 512],
                                        start=(k == 0), stop=(k == NKD - 1))
                                return mm

                            t_bank = []
                            for bi in range(4):
                                tb = bi % 2
                                w_ap = g_ap if bi < 2 else u_ap
                                pbank = pg[tb] if bi < 2 else pu2[tb]
                                war = pg_war[tb] if bi < 2 else pu_war[tb]
                                dtk = cur[0] if bi < 2 else cur[1]
                                t_bank.append(sch.op(
                                    "tensor",
                                    lambda e, w_ap=w_ap, pbank=pbank, tb=tb, bmm=bmm: bmm(e, w_ap, pbank, tb),
                                    [dtk] + war))
                            gu_done[seq] = t_bank[3]
                            if seq + 3 < NSEQ:
                                th_n, e_n, ic_n = gu_seq[seq + 3]
                                gu_toks[seq + 3] = gu_load(e_n, ic_n, (seq + 3) % 3,
                                                           [gu_done[seq]])
                            for tb in range(NTB):
                                pgx, pux = pg[tb], pu2[tb]
                                sgs = sgb[:, tb * 512:(tb + 1) * 512]
                                sus = sub[:, tb * 512:(tb + 1) * 512]
                                t_sg = sch.op("scalar", lambda e, pgx=pgx, sgs=sgs: e.activation(
                                    out=sgs, in_=pgx, func=AF.Silu), [t_bank[tb]] + sg_war[tb])
                                t_su = sch.op("vector", lambda e, pux=pux, tb=tb, wslot=wslot, sus=sus: e.tensor_tensor(
                                    out=sus, in0=pux,
                                    in1=wbcf[:, wslot + tb * 512:wslot + (tb + 1) * 512],
                                    op=ALU.mult), [t_bank[2 + tb], d_wbc] + su_war[tb])
                                wbc_rd[e_i % 2].append(t_su)
                                dst = actb[:, kk * TH + tb * 512:kk * TH + (tb + 1) * 512]
                                t_act = sch.op("vector", lambda e, dst=dst, sgs=sgs, sus=sus: e.tensor_mul(
                                    out=dst, in0=sgs, in1=sus), [t_sg, t_su] + act_war)
                                act_war = []
                                sg_war[tb], su_war[tb] = [t_act], [t_act]
                                pg_war[tb] = [t_sg]
                                pu_war[tb] = [t_su]
                                last_act = t_act
                            seq += 1
                        for tb in range(NTB):
                            for dt in range(NKD):
                                pdx = pd[dt % 2]

                                def dmm(e, tb=tb, dt=dt, pdx=pdx, ic0=ic0):
                                    for kk in range(NKH):
                                        ki = ic0 + kk
                                        mm = e.matmul(
                                            out=pdx,
                                            lhsT=dwn[:, ki * D + dt * 128:ki * D + dt * 128 + 128],
                                            rhs=actb[:, kk * TH + tb * 512:kk * TH + (tb + 1) * 512],
                                            start=(kk == 0), stop=(kk == NKH - 1))
                                    return mm

                                t_d = sch.op("tensor", dmm, [last_act] + d_dwn + pd_war[dt % 2])
                                last_down = t_d
                                asl = acc_ap(dt, tb * 512, (tb + 1) * 512)
                                if e_i == 0 and ih == 0:
                                    t_a = sch.op("vector", lambda e, asl=asl, pdx=pdx: e.tensor_copy(
                                        out=asl, in_=pdx), [t_d] + acc_war)
                                else:
                                    t_a = sch.op("vector", lambda e, asl=asl, pdx=pdx: e.tensor_tensor(
                                        out=asl, in0=pdx, in1=asl, op=ALU.add),
                                        [t_d, acc_tok.get((dt, tb))] + acc_war)
                                acc_tok[(dt, tb)] = t_a
                                pd_war[dt % 2] = [t_a]
                        acc_war = []
                        act_war = [last_down]
                    dwn_war = [last_down]
                    dwn_pend = (e_i + 1) if e_i + 1 < E else (0 if th == 0 else None)
                    d_dwn = []
                wo_tr_last = None
                for tt in range(TH // 128):
                    trow = th0 + tt * 128
                    for dh in range(4):
                        bank = dh % 2
                        half = (tt * 4 + dh) % 2
                        etoks = []
                        for j in range(NKD // 4):
                            dt = dh * (NKD // 4) + j
                            src = acc_ap(dt, tt * 128, tt * 128 + 128)
                            dep = [acc_tok[(dt, (tt * 128) // 512)]]
                            if j == 0:
                                dep = dep + pt_war[bank]
                            t_t = sch.op("tensor", lambda e, src=src, bank=bank, j=j: e.transpose(
                                out=pt[bank][0:128, j * 128:(j + 1) * 128], in_=src,
                                identity=idb), dep)
                            etoks.append(t_t)
                            wo_tr_last = t_t
                        osl = ostg[:, half * 512:(half + 1) * 512]
                        t_e = sch.op("vector", lambda e, bank=bank, osl=osl: e.tensor_copy(
                            out=osl, in_=pt[bank][0:128, 0:512]),
                            etoks[-1:] + ostg_war[half])
                        pt_war[bank] = [t_e]
                        d_o = sch.dma("gp", y[trow:trow + 128, dh * (D // 4):(dh + 1) * (D // 4)],
                                      osl, deps=[t_e] + sch.qfrontier("sync_q"),
                                      accum_op=ALU.add)
                        ostg_war[half] = [d_o]
                acc_war = [wo_tr_last]

        blk = en(nc.Block())
        sch.build(blk)
    return nc


from concourse.bass_utils import run_bass_kernel_spmd

_D = 2048


def _pretile_gu(w, NKI, NKD=16):
    E = w.shape[0]
    return np.ascontiguousarray(
        w.reshape(E, NKD, 128, NKI, 128).transpose(0, 3, 2, 1, 4))


def _make_sel(E):
    s = np.zeros((E, E * 128), dtype=np.float32)
    for e in range(E):
        s[e, e * 128:(e + 1) * 128] = 1.0
    return s


def _make_xin(x_batch, p, t_own, t_prior):
    own = x_batch[p * t_own:(p + 1) * t_own]
    prior = x_batch[:p * t_own]
    pad = np.zeros((t_prior - prior.shape[0], _D), dtype=np.float32)
    return np.ascontiguousarray(np.concatenate([pad, prior, own], axis=0))


def kernel(L0_attn_x, L0_attn_log_A_diag, L0_attn_B_mat, L0_attn_C_mat,
           L0_attn_D_vec, L0_attn_ln_w, L0_attn_ln_b, L0_ffn_router_w,
           L0_ffn_expert_bias, L0_ffn_gate_proj, L0_ffn_up_proj,
           L0_ffn_down_proj):
    cfg = CFG(16, 1792, 2048, 6144)
    x = np.ascontiguousarray(np.asarray(L0_attn_x, dtype=np.float32))
    shared = {
        "logA": np.ascontiguousarray(np.asarray(L0_attn_log_A_diag, np.float32)),
        "Bm": np.ascontiguousarray(np.asarray(L0_attn_B_mat, np.float32)),
        "Cm": np.ascontiguousarray(np.asarray(L0_attn_C_mat, np.float32)),
        "Dv": np.ascontiguousarray(np.asarray(L0_attn_D_vec, np.float32)),
        "lnw": np.ascontiguousarray(np.asarray(L0_attn_ln_w, np.float32)),
        "lnb": np.ascontiguousarray(np.asarray(L0_attn_ln_b, np.float32)),
        "rw": np.ascontiguousarray(np.asarray(L0_ffn_router_w, np.float32)),
        "rb": np.ascontiguousarray(np.asarray(L0_ffn_expert_bias, np.float32)),
        "gpt": _pretile_gu(np.asarray(L0_ffn_gate_proj, np.float32), cfg.NKI),
        "upt": _pretile_gu(np.asarray(L0_ffn_up_proj, np.float32), cfg.NKI),
        "dpt": np.ascontiguousarray(
            np.asarray(L0_ffn_down_proj, np.float32).reshape(cfg.E, cfg.NKI, 128, _D)),
        "ident": np.eye(128, dtype=np.float32),
        "sel": _make_sel(cfg.E),
    }
    nc = build_nc(cfg)
    in_maps = []
    for core in range(8):
        b, p = core // 4, core % 4
        m = dict(shared)
        m["xin"] = _make_xin(x[b], p, cfg.T, cfg.TP)
        in_maps.append(m)
    res = run_bass_kernel_spmd(nc, in_maps, core_ids=list(range(8)))
    y = np.empty((2, 8192, _D), dtype=np.float32)
    for core in range(8):
        b, p = core // 4, core % 4
        y[b, p * cfg.T:(p + 1) * cfg.T] = res.results[core]["y"]
    return y
